# version 14
# baseline (speedup 1.0000x reference)
import numpy as np
import ml_dtypes
import concourse.bass as bass
import concourse.mybir as mybir
from concourse.bass_utils import run_bass_kernel_spmd

F32 = mybir.dt.float32
BF16 = mybir.dt.bfloat16
AF = mybir.ActivationFunctionType
OP = mybir.AluOpType
AX = mybir.AxisListType

D = 1024
T = 2048
H = 16
DK = 64
FF = 4096
NJ = D // 128
NF = FF // 128
NT = T // 128
GB = 512
NG = T // GB
CH = 128
NC_ = T // CH
CPG = GB // CH

ENGS = ("T", "V", "S", "G", "Y")


class Prog:
    NLANE = 6

    def __init__(self, nc):
        self.nc = nc
        self.prog = {e: [] for e in ENGS}
        self.count = {e: 0 for e in ENGS}
        self.hwm = {}
        self._pend = {e: [] for e in ENGS}
        self.sems = {}
        self.dsems = {}

    def make_sems(self, stack):
        for e in ENGS:
            self.sems[e] = stack.enter_context(self.nc.semaphore(f"ms{e}"))
        for e in ("Y", "G", "S", "V"):
            self.dsems[e] = [stack.enter_context(self.nc.semaphore(f"dm{e}{i}"))
                             for i in range(self.NLANE)]
        self.lane_cnt = {e: [0] * self.NLANE for e in ("Y", "G", "S", "V")}
        self.lane_idx = {e: 0 for e in ("Y", "G", "S", "V")}

    def wait(self, eng, tok):
        if tok is None:
            return
        if isinstance(tok, list):
            for t in tok:
                self.wait(eng, t)
            return
        prod, kind, val = tok
        if kind == "c":
            if prod == eng or val <= 0:
                return
            key = (eng, prod, "c")
            if self.hwm.get(key, 0) >= val:
                return
            self.hwm[key] = val
            self._pend[eng].append((prod, "c", val))
        else:
            lane, v = val
            if v <= 0:
                return
            key = (eng, prod, "d", lane)
            if self.hwm.get(key, 0) >= v:
                return
            self.hwm[key] = v
            self._pend[eng].append((prod, ("d", lane), v))

    def emit(self, eng, fn, mark=False, dma=False):
        tok = None
        if dma:
            lane = self.lane_idx[eng]
            self.lane_idx[eng] = (lane + 1) % self.NLANE
            prev = self.lane_cnt[eng][lane]
            if prev > 0:
                self.wait(eng, (eng, "d", (lane, prev)))
            self.lane_cnt[eng][lane] = prev + 16
            waits, self._pend[eng] = self._pend[eng], []
            incs = [(eng, ("d", lane), 16)]
            tok = (eng, "d", (lane, prev + 16))
            self.prog[eng].append((waits, fn, incs))
            return tok
        waits, self._pend[eng] = self._pend[eng], []
        incs = []
        if mark:
            self.count[eng] += 1
            incs.append((eng, "c", 1))
            tok = (eng, "c", self.count[eng])
        self.prog[eng].append((waits, fn, incs))
        return tok

    def tok(self, eng):
        return (eng, "c", self.count[eng])

    def qdone(self, q):
        return [(q, "d", (ln, c)) for ln, c in enumerate(self.lane_cnt[q])
                if c > 0]

    def barrier(self):
        snap_c = dict(self.count)
        for e in ENGS:
            for o in ENGS:
                if o != e and snap_c[o] > 0:
                    self.wait(e, (o, "c", snap_c[o]))
                if o in ("Y", "G", "S", "V"):
                    self.wait(e, self.qdone(o))

    def run(self, eng_name, engine):
        for waits, fn, incs in self.prog[eng_name]:
            for (prod, kind, val) in waits:
                if kind == "c":
                    engine.wait_ge(self.sems[prod], val)
                else:
                    engine.wait_ge(self.dsems[prod][kind[1]], val)
            instr = fn(engine)
            for (prod, kind, amt) in incs:
                if kind == "c":
                    instr.then_inc(self.sems[prod], amt)
                else:
                    instr.then_inc(self.dsems[prod][kind[1]], amt)


class Buf:
    def __init__(self, name=""):
        self.name = name
        self.w = None
        self.r = {}

    def wrote(self, tok):
        if tok is not None:
            self.w = tok
            self.r = {}

    def read(self, tok):
        if tok is not None:
            self.r[tok[0]] = tok

    def war(self):
        toks = list(self.r.values())
        if self.w is not None:
            toks.append(self.w)
        return toks


def ts(i, n=128):
    return slice(i * n, (i + 1) * n)


def build(debug=False, nsc=99, wo_on=True, sgate=(1 << 20) - 1):
    EV_S = bool(sgate & 2048)
    EV_V = bool(sgate & 4096)
    import contextlib
    nc = bass.Bass("TRN2", target_bir_lowering=False,
                   detect_race_conditions=False)
    stack = contextlib.ExitStack()
    P = Prog(nc)
    P.make_sems(stack)

    x_d = nc.dram_tensor("x", [D, T], F32, kind="ExternalInput")
    wq_d = nc.dram_tensor("wq", [D, D], F32, kind="ExternalInput")
    wk_d = nc.dram_tensor("wk", [D, D], F32, kind="ExternalInput")
    wv_d = nc.dram_tensor("wv", [D, D], F32, kind="ExternalInput")
    wo_d = nc.dram_tensor("wo", [D, D], F32, kind="ExternalInput")
    w1_d = nc.dram_tensor("w1", [FF, D], F32, kind="ExternalInput")
    b1_d = nc.dram_tensor("b1", [FF], F32, kind="ExternalInput")
    w2_d = nc.dram_tensor("w2", [D, FF], F32, kind="ExternalInput")
    b2_d = nc.dram_tensor("b2", [D], F32, kind="ExternalInput")
    triu_d = nc.dram_tensor("triu1", [128, 128], BF16, kind="ExternalInput")
    striu_d = nc.dram_tensor("striu1", [128, 128], BF16, kind="ExternalInput")
    bdm_d = nc.dram_tensor("bdm1", [128, 128], BF16, kind="ExternalInput")
    pm_d = nc.dram_tensor("pm1", [128, 256], BF16, kind="ExternalInput")
    ident_d = nc.dram_tensor("ident", [128, 128], F32, kind="ExternalInput")
    out_d = nc.dram_tensor("out", [D, T], F32, kind="ExternalOutput")
    s1_d = nc.dram_tensor("s1scr", [FF, D], BF16, kind="Internal")
    s2_d = nc.dram_tensor("s2scr", [D, FF], BF16, kind="Internal")
    w2t_d = nc.dram_tensor("w2tscr", [FF, D], BF16, kind="Internal")
    w1t_d = nc.dram_tensor("w1tscr", [128, NJ * FF], BF16, kind="Internal")
    dbg_d = {}
    if debug:
        dbg_d["dbg1"] = nc.dram_tensor("dbg1", [128, NJ * T], F32,
                                       kind="ExternalOutput")
        dbg_d["dbg2"] = nc.dram_tensor("dbg2", [128, NJ * T], F32,
                                       kind="ExternalOutput")
        for nm, wid, dt in (("txn", NJ * GB, BF16), ("tQT", 8 * GB, BF16),
                            ("tKT", 8 * GB, BF16), ("tqn", D, BF16),
                            ("tkn", D, BF16), ("tvn", D, BF16),
                            ("tUP", GB, F32),
                            ("tOTg", 4096, BF16), ("tS1", 1024, F32),
                            ("tC1", 1024, F32), ("tG1", 1024, F32),
                            ("tS2", 1024, F32), ("tC2", 1024, F32),
                            ("tG2", 1024, F32),
                            ("tw1T", NJ * FF, BF16),
                            ("tghT", NF * GB, BF16),
                            ("tF0", GB, F32), ("tF7", GB, F32),
                            ("tW20", 2048, BF16), ("tW27", 2048, BF16),
                            ("tA2h", NJ, F32), ("tA2t", 32, F32)):
            dbg_d[nm] = nc.dram_tensor(nm, [128, wid], dt,
                                       kind="ExternalOutput")

    xres = nc.alloc_sbuf_tensor("xres", [128, NJ * T], F32)
    xn = nc.alloc_sbuf_tensor("xn", [128, NJ * GB], BF16)
    triu = nc.alloc_sbuf_tensor("triu_s", [128, 128], BF16)
    striu = nc.alloc_sbuf_tensor("striu_s", [128, 128], BF16)
    bdm = nc.alloc_sbuf_tensor("bdm_s", [128, 128], BF16)
    pm = nc.alloc_sbuf_tensor("pm_s", [128, 256], BF16)
    ident = nc.alloc_sbuf_tensor("idents", [128, 128], F32)
    identb = nc.alloc_sbuf_tensor("identb", [128, 128], BF16)
    ones_col = nc.alloc_sbuf_tensor("ones_col", [128, 1], F32)
    ones_row = nc.alloc_sbuf_tensor("ones_row", [1, 128], F32)
    alpha1 = nc.alloc_sbuf_tensor("alpha1", [128, NF], F32)
    b1s = nc.alloc_sbuf_tensor("b1s", [128, NF], F32)
    a2h = nc.alloc_sbuf_tensor("a2h", [128, NJ], F32)
    b2h = nc.alloc_sbuf_tensor("b2h", [128, NJ], F32)
    a2t = nc.alloc_sbuf_tensor("a2t", [128, 32], F32)
    inv_c = nc.alloc_sbuf_tensor("inv_c", [128, 1], F32)
    xsq = nc.alloc_sbuf_tensor("xsq", [128, GB], F32)
    lnt = nc.alloc_sbuf_tensor("lnt", [128, GB], F32)
    updt = nc.alloc_sbuf_tensor("updt", [128, GB], F32)
    m_row = lnt[0:1, :]
    v_row = updt[0:1, :]
    r_row = xsq[0:1, :]
    cst = nc.alloc_sbuf_tensor("cst", [128, 4], F32)

    SCR0 = (nc._sbuf_addr_for_side("left") + 31) // 32 * 32
    SCRTOP = nc._sbuf_addr_for_side("right")
    AVAIL = SCRTOP - SCR0

    def at(name, shape, dtype, off):
        return nc.alloc_sbuf_tensor_at(name, shape, dtype, offset=SCR0 + off)

    w1T_t = at("w1T", [128, NJ * FF], BF16, 0)
    ghT_t = at("ghT", [128, NF * GB], BF16, 65536)
    w2s_t = at("w2s", [128, 2 * 2048], BF16, 98304)
    s1stage = [at(f"s1stage{i}", [128, 1024], F32, 98304 + i * 4096)
               for i in range(2)]
    w1stage = [at(f"w1stage{i}", [128, 1024], F32, 106496 + i * 4096)
               for i in range(2)]
    w2stage = [at(f"w2stage{i}", [128, 1024], F32, 114688 + i * 4096)
               for i in range(2)]
    s2stage = [at(f"s2stage{i}", [128, 1024], BF16, 122880 + i * 2048)
               for i in range(2)]
    bstage = at("bstage", [32, 128], F32, 126976)
    o = 0
    wqs_t = at("wqs", [128, NJ * D], BF16, o); o += 16384
    wks_t = at("wks", [128, NJ * D], BF16, o); o += 16384
    wvs_t = at("wvs", [128, NJ * D], BF16, o); o += 16384
    wos_t = at("wos", [128, NJ * D], BF16, o); o += 16384
    QTg_t = at("QTg", [128, 8 * GB], BF16, o); o += 8192
    KTg_t = at("KTg", [128, 8 * GB], BF16, o); o += 8192
    OTg_t = at("OTg", [128, 8 * 512], BF16, o); o += 8192
    QTz_t = at("QTz", [128, 8 * 128], BF16, o); o += 2048
    KTz_t = at("KTz", [128, 8 * 128], BF16, o); o += 2048
    qn_t = at("qn", [128, D], BF16, o); o += 2048
    kn_t = at("kn", [128, D], BF16, o); o += 2048
    vn_t = at("vn", [128, D], BF16, o); o += 2048
    Sp_t = at("Sp", [128, 8 * 128], F32, o); o += 4096
    Cp_t = at("Cp", [128, 8 * 128], F32, o); o += 4096
    Gp_t = at("Gp", [128, 8 * 128], F32, o); o += 4096
    S0bd_t = at("S0bd", [128, 8 * 128], BF16, o); o += 2048
    C0bd_t = at("C0bd", [128, 8 * 128], BF16, o); o += 2048
    Atb_t = at("Atb", [128, 8 * 128], BF16, o); o += 2048
    Atr_t = at("Atr", [128, 8 * 128], BF16, o); o += 2048
    Aqs_t = at("Aqs", [128, 8 * 128], BF16, o); o += 2048
    MtS_t = at("MtS", [128, 8 * 128], BF16, o); o += 2048
    Ybf_t = at("Ybf", [128, 4 * 128], BF16, o); o += 1024
    W0b_t = at("W0b", [128, 4 * 128], BF16, o); o += 1024
    KKbd_t = at("KKbd", [128, 4 * 128], BF16, o); o += 1024
    Zb_t = at("Zb", [128, 8 * 64], BF16, o); o += 1024
    assert o <= AVAIL, f"attn scratch {o} > {AVAIL}"

    psA = stack.enter_context(nc.psum_tensor("psA", [128, 1536], F32))
    psB = stack.enter_context(nc.psum_tensor("psB", [128, 1536], F32))
    pL = [stack.enter_context(nc.psum_tensor(f"pL{i}", [128, 512], F32))
          for i in range(2)]

    def bank(i):
        if i < 3:
            return psA[:, i * 512:(i + 1) * 512]
        if i < 6:
            return psB[:, (i - 3) * 512:(i - 3) * 512 + 512]
        return pL[i - 6][:]

    pB = [Buf(f"bank{i}") for i in range(8)]

    B = {}
    for name in ("xres", "xn", "xsq", "lnt", "m_row", "v_row", "r_row",
                 "updt", "alpha1", "b1s", "a2h", "b2h", "a2t", "consts",
                 "masks", "w1T", "ghT", "S", "C", "G", "S0bd", "C0bd",
                 "Atb", "Atr", "Aqs", "MtS", "Ybf", "W0b", "KKbd", "Zb",
                 "QTz", "KTz", "Aqb",
                 "QTg", "KTg", "OTg", "qn", "kn", "vn",
                 "wqs", "wks", "wvs", "wos", "w2t"):
        B[name] = Buf(name)
    w2s = [Buf("w2s0"), Buf("w2s1")]
    xst = [Buf("xst0"), Buf("xst1")]
    ost = [Buf("ost0"), Buf("ost1")]
    w1st = [Buf("w1st0"), Buf("w1st1")]
    s1st = [Buf("s1st0"), Buf("s1st1")]
    w2st = [Buf("w2st0"), Buf("w2st1")]
    s2st = [Buf("s2st0"), Buf("s2st1")]
    bst = Buf("bst")

    def _do(eng, fn, reads, writes, mark, dma=False):
        for r in reads:
            if isinstance(r, Buf):
                P.wait(eng, r.w)
            elif r is not None:
                P.wait(eng, r)
        for w in writes:
            for t in w.war():
                P.wait(eng, t)
        tok = P.emit(eng, fn, mark=mark, dma=dma)
        eff = tok if tok is not None else P.tok(eng)
        for r in reads:
            if isinstance(r, Buf):
                r.read(eff)
        for w in writes:
            w.wrote(tok)
        return tok

    def T_(fn, reads=(), writes=(), mark=True):
        return _do("T", fn, reads, writes, mark)

    def V_(fn, reads=(), writes=(), mark=True):
        return _do("V", fn, reads, writes, mark)

    def S_(fn, reads=(), writes=(), mark=True):
        return _do("S", fn, reads, writes, mark)

    def G_(fn, reads=(), writes=(), mark=True):
        return _do("G", fn, reads, writes, mark)

    def SE_(fn, reads=(), writes=(), mark=True):
        if not EV_S:
            return None
        return _do("S", fn, reads, writes, mark)

    def VE_(fn, reads=(), writes=(), mark=True):
        if not EV_V:
            return None
        return _do("V", fn, reads, writes, mark)

    def DMA(q, fn, reads=(), writes=()):
        return _do(q, fn, reads, writes, mark=False, dma=True)

    xres_v = xres.rearrange("p (j t) -> p j t", t=T)
    w2t_v = w2t_d.ap().rearrange("(c p) d -> p c d", p=128)

    def mrep(mask_t, n):
        return bass.AP(mask_t, 0, [[128, 128], [0, n], [1, 128]])

    def mrep2(mask_t, n1, n2):
        return bass.AP(mask_t, 0, [[128, 128], [0, n1], [0, n2], [1, 128]])

    V_(lambda e: e.memset(ones_col[:], 1.0), writes=[B["consts"]])
    V_(lambda e: e.memset(ones_row[:], 1.0), writes=[B["consts"]])
    V_(lambda e: e.memset(inv_c[:], 1.0 / 1024.0), writes=[B["consts"]])
    V_(lambda e: e.memset(cst[:, 0:1], 1e-5), writes=[B["consts"]])
    V_(lambda e: e.memset(cst[:, 1:2], 0.5), writes=[B["consts"]])
    V_(lambda e: e.memset(cst[:, 2:3], 0.5 / 4096.0), writes=[B["consts"]])
    V_(lambda e: e.memset(cst[:, 3:4], 0.0), writes=[B["consts"]])
    DMA("Y", lambda e: e.dma_start(triu[:], triu_d.ap()))
    DMA("Y", lambda e: e.dma_start(striu[:], striu_d.ap()))
    DMA("Y", lambda e: e.dma_start(bdm[:], bdm_d.ap()))
    DMA("Y", lambda e: e.dma_start(pm[:], pm_d.ap()))
    tid = DMA("Y", lambda e: e.dma_start(ident[:], ident_d.ap()))
    V_(lambda e: e.tensor_copy(identb[:], ident[:]),
       reads=[tid], writes=[B["masks"]])
    B["masks"].wrote(tid)

    xld_toks = []
    xg_toks = [[] for _ in range(NG)]
    for g_ in range(NG):
        for j in range(NJ):
            q = "Y" if g_ == 0 else "S"
            t_ = DMA(q, lambda e, j=j, g_=g_: e.dma_start(
                xres[:, j * T + g_ * GB: j * T + (g_ + 1) * GB],
                x_d.ap()[j * 128:(j + 1) * 128, g_ * GB:(g_ + 1) * GB]),
                writes=[B["xres"]] if (g_ == 0 and j == 0) else [])
            xld_toks.append(t_)
            xg_toks[g_].append(t_)
    B["xres"].wrote(xld_toks)

    ld = DMA("Y", lambda e: e.dma_start(
        bstage[:], b1_d.ap().rearrange("(a b) -> a b", b=128)), writes=[bst])
    T_(lambda e: e.transpose(bank(4)[:, 0:32], bstage[:], ident[0:32, 0:32]),
       reads=[bst, ld, tid], writes=[pB[4]])
    ct = S_(lambda e: e.copy(b1s[:], bank(4)[:, 0:32]),
            reads=[pB[4]], writes=[B["b1s"]])
    pB[4].read(ct)
    bst.read(ct)
    ld = DMA("Y", lambda e: e.dma_start(
        bstage[0:8, :], b2_d.ap().rearrange("(a b) -> a b", b=128)),
        writes=[bst])
    T_(lambda e: e.transpose(bank(5)[:, 0:8], bstage[0:8, :], ident[0:8, 0:8]),
       reads=[bst, ld], writes=[pB[5]])
    ct = S_(lambda e: e.activation(b2h[:], bank(5)[:, 0:8], AF.Copy,
                                   scale=cst[:, 1:2]),
            reads=[pB[5], B["consts"]], writes=[B["b2h"]])
    pB[5].read(ct)
    bst.read(ct)


    def stage_w2_chunk(cc2):
        j, q4 = cc2 // 4, cc2 % 4
        b = cc2 % 2
        w2g, s2g = w2stage[b], s2stage[b]
        wb, sb_ = w2st[b], s2st[b]
        ld = DMA("S", lambda e, j=j, q4=q4, w2g=w2g: e.dma_start(
            w2g[:], w2_d.ap()[ts(j), q4 * 1024:(q4 + 1) * 1024]),
            writes=[wb])
        rt = V_(lambda e, j=j, q4=q4, w2g=w2g: e.reduce_sum(
            a2t[:, j * 4 + q4:j * 4 + q4 + 1], w2g[:], axis=AX.X,
            apply_absolute_value=True),
            reads=[wb, ld], writes=[B["a2t"]])
        st = S_(lambda e, w2g=w2g, s2g=s2g: e.activation(
            s2g[:], w2g[:], AF.Sign),
            reads=[wb, ld], writes=[sb_])
        dt = DMA("S", lambda e, j=j, q4=q4, s2g=s2g: e.dma_start(
            s2_d.ap()[ts(j), q4 * 1024:(q4 + 1) * 1024], s2g[:]),
            reads=[sb_])
        sb_.read(dt)
        wb.read(rt)
        wb.read(st)

    def emit_a2h_tail():
        for j in range(NJ):
            V_(lambda e, j=j: e.tensor_tensor(
                a2t[:, j * 4:j * 4 + 1], a2t[:, j * 4:j * 4 + 1],
                a2t[:, j * 4 + 1:j * 4 + 2], OP.add),
               reads=[B["a2t"]], writes=[B["a2t"]])
        for j in range(NJ):
            V_(lambda e, j=j: e.tensor_tensor(
                a2t[:, j * 4 + 2:j * 4 + 3], a2t[:, j * 4 + 2:j * 4 + 3],
                a2t[:, j * 4 + 3:j * 4 + 4], OP.add),
               reads=[B["a2t"]], writes=[B["a2t"]])
        for j in range(NJ):
            V_(lambda e, j=j: e.tensor_tensor(
                a2h[:, j:j + 1], a2t[:, j * 4:j * 4 + 1],
                a2t[:, j * 4 + 2:j * 4 + 3], OP.add),
               reads=[B["a2t"]], writes=[])
        B["a2h"].wrote(P.tok("V"))
        S_(lambda e: e.activation(a2h[:], a2h[:], AF.Identity,
                                  scale=cst[:, 2:3], bias=cst[:, 3:4]),
           reads=[B["a2h"], B["consts"]], writes=[B["a2h"]])
        if debug and "tA2h" in dbg_d:
            dta = DMA("G", lambda e: e.dma_start(dbg_d["tA2h"].ap(), a2h[:]),
                      reads=[B["a2h"]])
            B["a2h"].read(dta)
            dtb = DMA("G", lambda e: e.dma_start(dbg_d["tA2t"].ap(), a2t[:]),
                      reads=[B["a2t"]])
            B["a2t"].read(dtb)

    def ln_stats(g):
        col0 = g * GB
        p6, p7 = bank(6), bank(7)
        for j in range(NJ):
            xs = xres[:, j * T + col0: j * T + col0 + GB]
            S_(lambda e, xs=xs: e.square(xsq[:], xs),
               reads=[xg_toks[g][j]], writes=[B["xsq"]])
            T_(lambda e, xs=xs, j=j: e.matmul(
                p6[0:1, :], ones_col[:], xs,
                start=(j == 0), stop=(j == NJ - 1)),
               reads=[B["consts"], xg_toks[g][j]],
               writes=[pB[6]] if j == 0 else [], mark=(j == NJ - 1))
            tq = T_(lambda e, j=j: e.matmul(
                p7[0:1, :], ones_col[:], xsq[:],
                start=(j == 0), stop=(j == NJ - 1)),
                reads=[B["xsq"], B["consts"]],
                writes=[pB[7]] if j == 0 else [], mark=True)
            B["xsq"].read(tq)
        B["xres"].read(P.tok("S"))
        B["xres"].read(P.tok("T"))
        pB[6].wrote(P.tok("T"))
        pB[7].wrote(P.tok("T"))
        mt = V_(lambda e: e.tensor_scalar_mul(
            m_row, p6[0:1, :], inv_c[0:1, 0:1]),
            reads=[pB[6], B["consts"]], writes=[B["m_row"]])
        pB[6].read(mt)
        vt = V_(lambda e: e.tensor_scalar_mul(
            v_row, p7[0:1, :], inv_c[0:1, 0:1]),
            reads=[pB[7], B["consts"]], writes=[B["v_row"]])
        pB[7].read(vt)
        V_(lambda e: e.scalar_tensor_tensor(
            r_row, m_row, -1.0, m_row, OP.mult, OP.mult),
           reads=[B["m_row"]], writes=[B["r_row"]])
        V_(lambda e: e.tensor_tensor(v_row, v_row, r_row, OP.add),
           reads=[B["r_row"], B["v_row"]], writes=[B["v_row"]])
        S_(lambda e: e.activation(v_row, v_row, AF.Sqrt,
                                  bias=cst[0:1, 0:1]),
           reads=[B["v_row"], B["consts"]], writes=[B["v_row"]])
        rc = V_(lambda e: e.reciprocal(r_row, v_row),
                reads=[B["v_row"]], writes=[B["r_row"]])
        B["updt"].read(rc)

    xnj_toks = [None] * NJ

    def ln_apply(g, xn_war_tok=None):
        col0 = g * GB
        p6, p7 = bank(6), bank(7)
        tm = T_(lambda e: e.matmul(p6[:], ones_row[:], m_row,
                                   start=True, stop=True),
                reads=[B["m_row"], B["consts"]], writes=[pB[6]])
        B["lnt"].read(tm)
        tr = T_(lambda e: e.matmul(p7[:], ones_row[:], r_row,
                                   start=True, stop=True),
                reads=[B["r_row"], B["consts"]], writes=[pB[7]])
        B["xsq"].read(tr)
        for j in range(NJ):
            xs = xres[:, j * T + col0: j * T + col0 + GB]
            V_(lambda e, xs=xs: e.tensor_tensor(lnt[:], xs, p6[:],
                                                OP.subtract),
               reads=[xg_toks[g][j], pB[6]], writes=[B["lnt"]])
            ex = [xn_war_tok] if (j == 0 and xn_war_tok is not None) else []
            vt = V_(lambda e, j=j: e.tensor_tensor(
                xn[:, j * GB:(j + 1) * GB], lnt[:], p7[:], OP.mult),
                reads=[B["lnt"], pB[7]] + ex,
                writes=[B["xn"]] if j == 0 else [])
            xnj_toks[j] = vt
            pB[6].read(vt)
            pB[7].read(vt)
        B["xn"].wrote(P.tok("V"))
        B["xres"].read(P.tok("V"))

    def layernorm_group(g, xn_war_tok=None):
        ln_stats(g)
        ln_apply(g, xn_war_tok=xn_war_tok)

    def ffn1_chunk(c, w1gate=None):
        bi = c % 2
        pt = bank(bi)
        pb = pB[bi]
        for j in range(NJ):
            lhs = w1T_t[:, j * FF + c * 128: j * FF + c * 128 + 128]
            rhs = xn[:, j * GB:(j + 1) * GB]
            T_(lambda e, pt=pt, lhs=lhs, rhs=rhs, j=j: e.matmul(
                pt, lhs, rhs, start=(j == 0), stop=(j == NJ - 1)),
               reads=[xnj_toks[j]]
               + ([w1gate if w1gate is not None else B["w1T"]]
                  if j == 0 else []),
               writes=([pb] if j == 0 else []), mark=(j == NJ - 1))
        pb.wrote(P.tok("T"))
        st = S_(lambda e, pt=pt, c=c: e.activation(
            ghT_t[:, c * GB:(c + 1) * GB], pt, AF.Gelu,
            scale=alpha1[:, c:c + 1], bias=b1s[:, c:c + 1]),
            reads=[pb, B["alpha1"], B["b1s"]],
            writes=[B["ghT"]] if c == 0 else [])
        pb.read(st)

    def ffn_sublayer(first_call, skip_g0_head=False):
        if not first_call:
            rl = []
            for hh in range(2):
                rl.append(DMA(("Y", "G")[hh], lambda e, hh=hh: e.dma_start(
                    w1T_t[:, hh * 4 * FF:(hh + 1) * 4 * FF],
                    w1t_d.ap()[:, hh * 4 * FF:(hh + 1) * 4 * FF]),
                    reads=[w1t_stok] if hh == 0 else [],
                    writes=[B["w1T"]] if hh == 0 else []))
            B["w1T"].wrote(rl)
        if debug and first_call and "tw1T" in dbg_d:
            dtw = DMA("G", lambda e: e.dma_start(dbg_d["tw1T"].ap(), w1T_t[:]),
                      reads=[B["w1T"]])
            B["w1T"].read(dtw)

        if not skip_g0_head:
            layernorm_group(0, xn_war_tok=P.tok("T"))
        for g in range(NG):
            if not (skip_g0_head and g == 0):
                for c in range(NF):
                    ffn1_chunk(c)
                B["ghT"].wrote(P.tok("S"))
            if debug and first_call and g == 0 and "tghT" in dbg_d:
                dtg = DMA("G", lambda e: e.dma_start(dbg_d["tghT"].ap(),
                                                     ghT_t[:]),
                          reads=[B["ghT"]])
                B["ghT"].read(dtg)

            if g + 1 < NG:
                ln_stats(g + 1)

            for w in range(2):
                for c in range(NF):
                    slot = (c // 4) % 2
                    ci = c % 4
                    sb = w2s[slot]
                    if ci == 0:
                        if first_call and g == 0:
                            q = "Y" if slot == 0 else "S"
                            tl_toks = []
                            for cc in range(4):
                                src = s2_d.ap()[w * 512:(w + 1) * 512,
                                                (c + cc) * 128:(c + cc + 1) * 128]
                                d1 = w2s_t[:, slot * 2048 + cc * 512:
                                           slot * 2048 + cc * 512 + 512]
                                tl_toks.append(DMA(
                                    q, lambda e, d1=d1, src=src:
                                    e.dma_start_transpose(d1, src),
                                    reads=([s2_done, B["a2h"], B["w1T"]]
                                           if cc == 0 else []),
                                    writes=[sb]))
                            sb.wrote(tl_toks)
                            if debug and ((w == 0 and c == 0) or
                                          (w == 1 and c == 28)) and \
                                    "tW20" in dbg_d:
                                nmw = "tW20" if w == 0 else "tW27"
                                dtw2 = DMA("S", lambda e, nmw=nmw, slot=slot:
                                           e.dma_start(
                                               dbg_d[nmw].ap(),
                                               w2s_t[:, slot * 2048:
                                                     slot * 2048 + 2048]),
                                           reads=[sb])
                                sb.read(dtw2)
                            wdst = w2t_v[:, c:c + 4, w * 512:(w + 1) * 512]
                            wsrc = w2s_t.rearrange(
                                "p (s c dd) -> p s c dd", s=2, dd=512)[
                                :, slot, :, :]
                            dt = DMA("G", lambda e, wdst=wdst, wsrc=wsrc:
                                     e.dma_start(wdst, wsrc), reads=[sb])
                            sb.read(dt)
                        else:
                            q = "Y" if slot == 0 else "S"
                            src = w2t_v[:, c:c + 4, w * 512:(w + 1) * 512]
                            dst = w2s_t.rearrange(
                                "p (s c dd) -> p s c dd", s=2, dd=512)[
                                :, slot, :, :]
                            DMA(q, lambda e, dst=dst, src=src:
                                e.dma_start(dst, src),
                                reads=[B["w2t"]], writes=[sb])
                    for ji in range(4):
                        j = 4 * w + ji
                        bi = 2 + ji
                        pt = bank(bi)
                        pb = pB[bi]
                        lhs = w2s_t[:, slot * 2048 + ci * 512 + ji * 128:
                                    slot * 2048 + ci * 512 + ji * 128 + 128]
                        rhs = ghT_t[:, c * GB:(c + 1) * GB]
                        T_(lambda e, pt=pt, lhs=lhs, rhs=rhs, c=c: e.matmul(
                            pt, lhs, rhs, start=(c == 0), stop=(c == NF - 1)),
                           reads=([sb, B["ghT"]] if ji == 0 else [sb]),
                           writes=([pb] if c == 0 else []),
                           mark=(ji == 3))
                ytok = P.tok("T")
                B["ghT"].read(ytok)
                for ji in range(4):
                    pB[2 + ji].wrote(ytok)
                for ji in range(4):
                    j = 4 * w + ji
                    pt = bank(2 + ji)
                    pb = pB[2 + ji]
                    st = S_(lambda e, pt=pt, j=j: e.activation(
                        updt[:], pt, AF.Identity,
                        scale=a2h[:, j:j + 1], bias=b2h[:, j:j + 1]),
                        reads=[pb, B["a2h"], B["b2h"]], writes=[B["updt"]])
                    pb.read(st)
                    if debug and first_call and g == 0 and \
                            ((w == 0 and ji == 0) or (w == 1 and ji == 3)) \
                            and "tF0" in dbg_d:
                        nmf = "tF0" if w == 0 else "tF7"
                        dtf = DMA("G", lambda e, nmf=nmf: e.dma_start(
                            dbg_d[nmf].ap(), updt[:]), reads=[B["updt"]])
                        B["updt"].read(dtf)
                    xs = xres[:, j * T + g * GB: j * T + g * GB + GB]
                    vt = V_(lambda e, xs=xs: e.tensor_tensor(
                        xs, xs, updt[:], OP.add),
                        reads=[B["updt"]], writes=[B["xres"]])
                    B["updt"].read(vt)
                if w == 0 and g + 1 < NG:
                    ln_apply(g + 1, xn_war_tok=P.tok("T"))
            if first_call and g == 0:
                B["w2t"].wrote(P.qdone("G"))

    layernorm_group(0, xn_war_tok=P.tok("T"))
    w1ctoks = []
    for c in range(NF + 1):
        if c < NF:
            b = c % 2
            w1g, s1g = w1stage[b], s1stage[b]
            wb, sb_ = w1st[b], s1st[b]
            ld = DMA("Y", lambda e, c=c, w1g=w1g: e.dma_start(
                w1g[:], w1_d.ap()[ts(c), :]), writes=[wb])
            rt = V_(lambda e, c=c, w1g=w1g: e.reduce_sum(
                alpha1[:, c:c + 1], w1g[:], axis=AX.X,
                apply_absolute_value=True),
                reads=[wb, ld], writes=[])
            S_(lambda e, c=c: e.activation(
                alpha1[:, c:c + 1], alpha1[:, c:c + 1], AF.Identity,
                scale=inv_c[:, 0:1], bias=cst[:, 3:4]),
               reads=[rt, B["consts"]], writes=[])
            st = S_(lambda e, w1g=w1g, s1g=s1g: e.activation(
                s1g[:], w1g[:], AF.Sign),
                reads=[wb, ld], writes=[sb_])
            wb.read(rt)
            wb.read(st)
            for j in range(NJ):
                half = j // 4
                pt2 = (psA if half == 0 else psB)[:, 1024 + (j % 4) * 128:
                                                  1024 + (j % 4) * 128 + 128]
                pbh = pB[2] if half == 0 else pB[5]
                T_(lambda e, pt2=pt2, s1g=s1g, j=j: e.transpose(
                    pt2, s1g[:, ts(j)], ident[:]),
                   reads=([sb_, tid] if j == 0 else []),
                   writes=([pbh] if j % 4 == 0 else []),
                   mark=(j % 4 == 3))
            ttok = P.tok("T")
            pB[2].wrote(ttok)
            pB[5].wrote(ttok)
            sb_.read(ttok)
            ctl = None
            for half, pst in ((0, psA), (1, psB)):
                w1dst = bass.AP(w1T_t, half * 4 * FF + c * 128,
                                [[NJ * FF, 128], [FF, 4], [1, 128]])
                psrc_ = bass.AP(pst.ap().tensor, pst.ap().offset + 1024,
                                [[1536, 128], [128, 4], [1, 128]])
                pbh = pB[2] if half == 0 else pB[5]
                ctl = S_(lambda e, w1dst=w1dst, psrc_=psrc_: e.copy(
                    w1dst, psrc_), reads=[pbh], writes=[B["w1T"]])
                pbh.read(ctl)
            w1ctoks.append(P.tok("S"))
            stage_w2_chunk(c)
        if c >= 1:
            ffn1_chunk(c - 1, w1gate=w1ctoks[c - 1])
    B["ghT"].wrote(P.tok("S"))
    B["alpha1"].wrote(P.tok("S"))
    s2_done = P.qdone("S")
    emit_a2h_tail()
    s1_done = P.qdone("Y")

    ffn_sublayer(first_call=True, skip_g0_head=True)
    if debug:
        dt = DMA("G", lambda e: e.dma_start(dbg_d["dbg1"].ap(), xres[:]),
                 reads=[B["xres"]])
        B["xres"].read(dt)

    P.barrier()

    w1t_stok = DMA("G", lambda e: e.dma_start(w1t_d.ap(), w1T_t[:]),
                   reads=[B["w1T"]])
    B["w1T"].read(w1t_stok)

    for nm, srcd, dstt in (("wqs", wq_d, wqs_t), ("wks", wk_d, wks_t),
                           ("wvs", wv_d, wvs_t), ("wos", wo_d, wos_t)):
        src_v = srcd.ap().rearrange("(j p) e -> p j e", p=128)
        DMA("G", lambda e, dstt=dstt, src_v=src_v: e.dma_start(dstt[:], src_v),
            writes=[B[nm]])

    for _pi, _pt in ((0, psA), (1, psB)):
        vt = V_(lambda e, _pt=_pt: e.memset(_pt[:], 0.0),
                writes=[pB[3 * _pi], pB[3 * _pi + 1], pB[3 * _pi + 2]])
        for _b in range(3):
            pB[3 * _pi + _b].wrote(vt)
    V_(lambda e: e.memset(Sp_t[:], 0.0), writes=[B["S"]])
    V_(lambda e: e.memset(Cp_t[:], 0.0), writes=[B["C"]])
    V_(lambda e: e.memset(Gp_t[:], 0.0), writes=[B["G"]])

    def proj_group_T(g, Wbuf, Wt, dstbuf, dst_t):
        for jE in range(8):
            bi = jE % 4
            pt = bank(bi)
            pb = pB[bi]
            for j in range(NJ):
                lhs = Wt[:, j * D + jE * 128: j * D + jE * 128 + 128]
                rhs = xn[:, j * GB:(j + 1) * GB]
                T_(lambda e, pt=pt, lhs=lhs, rhs=rhs, j=j: e.matmul(
                    pt, lhs, rhs, start=(j == 0), stop=(j == NJ - 1)),
                   reads=[xnj_toks[j]] + ([Wbuf] if j == 0 else []),
                   writes=([pb] if j == 0 else []), mark=(j == NJ - 1))
            pb.wrote(P.tok("T"))
            if jE % 2 == 0:
                st = S_(lambda e, pt=pt, jE=jE: e.copy(
                    dst_t[:, jE * GB:(jE + 1) * GB], pt),
                    reads=[pb], writes=[dstbuf] if jE == 0 else [])
            else:
                st = V_(lambda e, pt=pt, jE=jE: e.tensor_copy(
                    dst_t[:, jE * GB:(jE + 1) * GB], pt),
                    reads=[pb], writes=[dstbuf] if jE == 0 else [])
            pb.read(st)
        dstbuf.wrote(P.tok("S"))
        dstbuf.wrote(P.tok("V"))

    def proj_chunk_nat(c, Wbuf, Wt, dstbuf, dst_t):
        tc = c % CPG
        for h2 in range(2):
            bi = h2
            pt = bank(bi)
            pb = pB[bi]
            for j in range(NJ):
                lhs = xn[:, j * GB + tc * 128: j * GB + tc * 128 + 128]
                rhs = Wt[:, j * D + h2 * 512: j * D + h2 * 512 + 512]
                T_(lambda e, pt=pt, lhs=lhs, rhs=rhs, j=j: e.matmul(
                    pt, lhs, rhs, start=(j == 0), stop=(j == NJ - 1)),
                   reads=[xnj_toks[j]] + ([Wbuf] if j == 0 else []),
                   writes=([pb] if j == 0 else []), mark=(j == NJ - 1))
            pb.wrote(P.tok("T"))
            st = S_(lambda e, pt=pt, h2=h2: e.copy(
                dst_t[:, h2 * 512:(h2 + 1) * 512], pt),
                reads=[pb], writes=[dstbuf] if h2 == 0 else [])
            pb.read(st)
        dstbuf.wrote(P.tok("S"))

    def evac_grid8(dst_t, set3, cw, engine_op, reads_extra=(), out_bf=None,
                   csub=128):
        base = psA if set3 == "A" else psB
        ops = []
        src1 = bass.AP(base.ap().tensor, base.ap().offset,
                       [[1536, 128], [512, 3], [128, 2], [1, csub]])
        dst1 = bass.AP(dst_t, 0, [[8 * cw, 128], [cw, 3], [3 * cw, 2], [1, cw]])
        src2 = bass.AP(base.ap().tensor, base.ap().offset + 256,
                       [[1536, 128], [512, 2], [1, csub]])
        dst2 = bass.AP(dst_t, 6 * cw, [[8 * cw, 128], [cw, 2], [1, cw]])
        return (dst1, src1), (dst2, src2)

    rot = {"A": 0, "B": 0}

    def sect(setn, ncells, cellw=128):
        r = rot[setn]
        rot[setn] = (r + ncells) % 3
        base = 0 if setn == "A" else 3
        cells = []
        for l in range(ncells):
            bl = (r + l) % 3
            bi = base + bl
            col = (l // 3) * cellw
            cells.append((bi, col, l < 3, l >= ncells - 3))
        evacs = []
        for bl in range(3):
            l0 = (bl - r) % 3
            if l0 >= ncells:
                continue
            nb = (ncells - l0 + 2) // 3
            evacs.append((base + bl, l0, nb))
        return cells, evacs

    def psrc(bi, colw, nb, subw):
        tens = psA if bi < 3 else psB
        b = bi % 3
        return bass.AP(tens.ap().tensor, tens.ap().offset + b * 512,
                       [[1536, 128], [colw, nb], [1, subw]])

    def ddst(dst_t, stride_row, l0, nb, cw, subw):
        return bass.AP(dst_t, l0 * cw,
                       [[stride_row, 128], [3 * cw, nb], [1, subw]])

    def pcell(bi, col, subw):
        tens = psA if bi < 3 else psB
        b = bi % 3
        return bass.AP(tens.ap().tensor, tens.ap().offset + b * 512 + col,
                       [[1536, 128], [1, subw]])

    def pbank2(bi, width):
        tens = psA if bi < 3 else psB
        b = bi % 3
        return bass.AP(tens.ap().tensor, tens.ap().offset + b * 512,
                       [[1536, 128], [1, width]])

    def slot3(dst_t, row_stride, l0, nb, cw=128):
        return bass.AP(dst_t, l0 * cw, [[row_stride, 128], [3 * cw, nb],
                                        [1, cw]])

    def proj_t2n(c, srcT_t, srcbuf, dstbuf, dst_t, setn):
        tc = c % CPG
        cells, evacs = sect(setn, 8)
        for l in range(8):
            bi, col, st_, sp_ = cells[l]
            cell = bank(bi)[:, col:col + 128]
            srcs = srcT_t[:, l * GB + tc * 128: l * GB + tc * 128 + 128]
            T_(lambda e, cell=cell, srcs=srcs: e.matmul(
                cell, srcs, identb[:], start=True, stop=True,
                skip_group_check=True),
               reads=([srcbuf, B["masks"]] if l == 0 else []),
               writes=[pB[bi]] if st_ else [], mark=sp_)
        ttok = P.tok("T")
        for bi, l0, nb in evacs:
            pB[bi].wrote(ttok)
        for _k, (bi, l0, nb) in enumerate(evacs):
            sap = psrc(bi, 128, nb, 128)
            dap = ddst(dst_t, 1024, l0, nb, 128, 128)
            SE_(lambda e, dap=dap, sap=sap: e.copy(dap, sap),
               reads=[pB[bi]], writes=[dstbuf] if _k == 0 else [])
        dstbuf.wrote(P.tok("S"))
        for bi, l0, nb in evacs:
            pB[bi].read(P.tok("S"))

    def scan_chunk(c):
        tc = c % CPG
        first = (c == 0)
        if not first:
            vt = VE_(lambda e: e.tensor_tensor(
                S0bd_t.rearrange("p (r c) -> p r c", c=128),
                Sp_t.rearrange("p (r c) -> p r c", c=128),
                mrep(bdm, 8), OP.mult),
                reads=[B["S"], B["masks"]], writes=[B["S0bd"]])
            B["S"].read(vt)
            vt = VE_(lambda e: e.tensor_tensor(
                C0bd_t.rearrange("p (r c) -> p r c", c=128),
                Cp_t.rearrange("p (r c) -> p r c", c=128),
                mrep(bdm, 8), OP.mult),
                reads=[B["C"], B["masks"]], writes=[B["C0bd"]])
            B["C"].read(vt)
        for w in range(2):
            if sgate & 1024:
                src3q = bass.AP(QTg_t, (4 * w) * GB + tc * 128,
                                [[8 * GB, 128], [GB, 4], [1, 128]])
                src3k = bass.AP(KTg_t, (4 * w) * GB + tc * 128,
                                [[8 * GB, 128], [GB, 4], [1, 128]])
                for qq in range(2):
                    pm3 = bass.AP(pm, qq * 128, [[256, 128], [0, 4], [1, 128]])
                    qdst = bass.AP(QTz_t, qq * 128, [[1024, 128], [256, 4], [1, 128]])
                    kdst = bass.AP(KTz_t, qq * 128, [[1024, 128], [256, 4], [1, 128]])
                    V_(lambda e, qdst=qdst, src3q=src3q, pm3=pm3: e.tensor_tensor(
                        qdst, src3q, pm3, OP.mult),
                       reads=[B["QTg"], B["masks"]], writes=[B["QTz"]])
                    V_(lambda e, kdst=kdst, src3k=src3k, pm3=pm3: e.tensor_tensor(
                        kdst, src3k, pm3, OP.mult),
                       reads=[B["KTg"], B["masks"]], writes=[B["KTz"]])

            cells, evacs = sect("B", 4)
            for u in (range(4) if sgate & 128 else []):
                p = 4 * w + u
                bi, col, st_, sp_ = cells[u]
                cell = bank(bi)[:, col:col + 128]
                kns = kn_t[:, p * 128:(p + 1) * 128]
                T_(lambda e, cell=cell, kns=kns, st_=st_, sp_=sp_: e.matmul(
                    cell, kns, kns, start=st_, stop=sp_,
                    skip_group_check=True),
                   reads=([B["kn"]] if u == 0 else []),
                   writes=[pB[bi]] if st_ else [], mark=sp_)
            for bi, l0, nb in evacs:
                pB[bi].wrote(P.tok("T"))
            for _k, (bi, l0, nb) in enumerate(
                    evacs if (sgate & 128 and sgate & 131072) else []):
                VE_(lambda e, bi=bi, l0=l0, nb=nb: e.tensor_tensor(
                    slot3(KKbd_t, 512, l0, nb), pbank2(bi, nb * 128),
                    mrep(bdm, nb), OP.mult),
                   reads=[pB[bi]] + ([B["masks"]] if _k == 0 else []),
                   writes=[B["KKbd"]] if _k == 0 else [])
                sdst = bass.AP(Sp_t, (4 * w + l0) * 128,
                               [[1024, 128], [3 * 128, nb], [1, 128]])
                VE_(lambda e, bi=bi, nb=nb, sdst=sdst: e.tensor_tensor(
                    sdst, sdst, pbank2(bi, nb * 128), OP.add),
                   reads=[pB[bi]], writes=[B["S"]] if _k == 0 else [])
            B["KKbd"].wrote(P.tok("V"))
            B["S"].wrote(P.tok("V"))
            for bi, l0, nb in evacs:
                pB[bi].read(P.tok("V"))

            cells, evacs = sect("A", 4)
            for u in (range(4) if sgate & 256 else []):
                p = 4 * w + u
                bi, col, st_, sp_ = cells[u]
                cell = bank(bi)[:, col:col + 128]
                qns = qn_t[:, p * 128:(p + 1) * 128]
                vns = vn_t[:, p * 128:(p + 1) * 128]
                T_(lambda e, cell=cell, qns=qns, vns=vns, st_=st_, sp_=sp_:
                   e.matmul(cell, qns, vns, start=st_, stop=sp_,
                            skip_group_check=True),
                   reads=([B["qn"]] if u == 0 else []),
                   writes=[pB[bi]] if st_ else [], mark=sp_)
            for bi, l0, nb in evacs:
                pB[bi].wrote(P.tok("T"))
            for _k, (bi, l0, nb) in enumerate(
                    evacs if (sgate & 256 and sgate & 262144) else []):
                cdst = bass.AP(Cp_t, (4 * w + l0) * 128,
                               [[1024, 128], [3 * 128, nb], [1, 128]])
                VE_(lambda e, bi=bi, nb=nb, cdst=cdst: e.tensor_tensor(
                    cdst, cdst, pbank2(bi, nb * 128), OP.add),
                   reads=[pB[bi]], writes=[B["C"]] if _k == 0 else [])
            B["C"].wrote(P.tok("V"))
            for bi, l0, nb in evacs:
                pB[bi].read(P.tok("V"))

            cells, evacs = sect("A", 8)
            for l in (range(8) if sgate & 1 else []):
                h = 8 * w + l
                p = h // 2
                bi, col, st_, sp_ = cells[l]
                cell = bank(bi)[:, col:col + 128]
                QTc = QTg_t[:, p * GB + tc * 128: p * GB + tc * 128 + 128]
                KTzs = KTz_t[:, ts(l)]
                T_(lambda e, cell=cell, KTzs=KTzs, QTc=QTc, st_=st_, sp_=sp_:
                   e.matmul(cell, KTzs, QTc, start=st_, stop=sp_,
                            skip_group_check=True),
                   reads=([B["QTg"], B["KTz"]] if l == 0 else []),
                   writes=[pB[bi]] if st_ else [], mark=sp_)
            at_tok = P.tok("T")
            for bi, l0, nb in evacs:
                pB[bi].wrote(at_tok)
            for _k, (bi, l0, nb) in enumerate(evacs):
                sap = psrc(bi, 128, nb, 128)
                dap = ddst(Atb_t, 1024, l0, nb, 128, 128)
                SE_(lambda e, dap=dap, sap=sap: e.copy(dap, sap),
                   reads=[pB[bi]], writes=[B["Atb"]] if _k == 0 else [])
            B["Atb"].wrote(P.tok("S"))
            for bi, l0, nb in evacs:
                pB[bi].read(P.tok("S"))
            for _k, (bi, l0, nb) in enumerate(
                    evacs if (sgate & 1 and sgate & 8192) else []):
                VE_(lambda e, bi=bi, l0=l0, nb=nb: e.tensor_tensor(
                    slot3(Atr_t, 1024, l0, nb), pbank2(bi, nb * 128),
                    mrep(triu, nb), OP.mult),
                   reads=[pB[bi]] + ([B["masks"], B["Atb"]] if _k == 0
                                     else []),
                   writes=[B["Atr"]] if _k == 0 else [])
            B["Atr"].wrote(P.tok("V"))
            for bi, l0, nb in evacs:
                pB[bi].read(P.tok("V"))

            cells, evacs = sect("B", 8)
            for l in (range(8) if sgate & 2 else []):
                h = 8 * w + l
                p = h // 2
                bi, col, st_, sp_ = cells[l]
                cell = bank(bi)[:, col:col + 128]
                KTc = KTg_t[:, p * GB + tc * 128: p * GB + tc * 128 + 128]
                QTzs = QTz_t[:, ts(l)]
                T_(lambda e, cell=cell, QTzs=QTzs, KTc=KTc, st_=st_, sp_=sp_:
                   e.matmul(cell, QTzs, KTc, start=st_, stop=sp_,
                            skip_group_check=True),
                   reads=([B["QTz"], B["KTg"]] if l == 0 else []),
                   writes=[pB[bi]] if st_ else [], mark=sp_)
            for bi, l0, nb in evacs:
                pB[bi].wrote(P.tok("T"))
            for _k, (bi, l0, nb) in enumerate(
                    evacs if (sgate & 2 and sgate & 16384) else []):
                VE_(lambda e, bi=bi, l0=l0, nb=nb: e.tensor_tensor(
                    slot3(Aqs_t, 1024, l0, nb), pbank2(bi, nb * 128),
                    mrep(striu, nb), OP.mult),
                   reads=[pB[bi]] + ([B["masks"]] if _k == 0 else []),
                   writes=[B["Aqs"]] if _k == 0 else [])
            B["Aqs"].wrote(P.tok("V"))
            for bi, l0, nb in evacs:
                pB[bi].read(P.tok("V"))

            if (not first) and (sgate & 4):
                cells, evacs = sect("A", 4)
                for u in range(4):
                    p = 4 * w + u
                    bi, col, st_, sp_ = cells[u]
                    cell = bank(bi)[:, col:col + 128]
                    S0s = S0bd_t[:, p * 128:(p + 1) * 128]
                    QTc = QTg_t[:, p * GB + tc * 128: p * GB + tc * 128 + 128]
                    T_(lambda e, cell=cell, S0s=S0s, QTc=QTc, st_=st_,
                       sp_=sp_: e.matmul(cell, S0s, QTc, start=st_, stop=sp_,
                                         skip_group_check=True),
                       reads=([B["S0bd"]] if u == 0 else []),
                       writes=[pB[bi]] if st_ else [], mark=sp_)
                for bi, l0, nb in evacs:
                    pB[bi].wrote(P.tok("T"))
                for _k, (bi, l0, nb) in enumerate(evacs):
                    sap = psrc(bi, 128, nb, 128)
                    dap = ddst(Ybf_t, 512, l0, nb, 128, 128)
                    SE_(lambda e, dap=dap, sap=sap: e.copy(dap, sap),
                       reads=[pB[bi]], writes=[B["Ybf"]] if _k == 0 else [])
                B["Ybf"].wrote(P.tok("S"))
                for bi, l0, nb in evacs:
                    pB[bi].read(P.tok("S"))
                cells, evacs = sect("B", 4)
                for u in range(4):
                    p = 4 * w + u
                    bi, col, st_, sp_ = cells[u]
                    cell = bank(bi)[:, col:col + 128]
                    S0s = S0bd_t[:, p * 128:(p + 1) * 128]
                    C0s = C0bd_t[:, p * 128:(p + 1) * 128]
                    T_(lambda e, cell=cell, S0s=S0s, C0s=C0s, st_=st_,
                       sp_=sp_: e.matmul(cell, S0s, C0s, start=st_, stop=sp_,
                                         skip_group_check=True),
                       reads=([B["C0bd"]] if u == 0 else []),
                       writes=[pB[bi]] if st_ else [], mark=sp_)
                for bi, l0, nb in evacs:
                    pB[bi].wrote(P.tok("T"))
                for _k, (bi, l0, nb) in enumerate(
                        evacs if sgate & 32768 else []):
                    gsrc = bass.AP(Gp_t, (4 * w + l0) * 128,
                                   [[1024, 128], [3 * 128, nb], [1, 128]])
                    VE_(lambda e, bi=bi, l0=l0, nb=nb, gsrc=gsrc:
                        e.tensor_tensor(
                            slot3(W0b_t, 512, l0, nb), pbank2(bi, nb * 128),
                            gsrc, OP.subtract),
                       reads=[pB[bi]] + ([B["G"]] if _k == 0 else []),
                       writes=[B["W0b"]] if _k == 0 else [])
                B["W0b"].wrote(P.tok("V"))
                B["G"].read(P.tok("V"))
                for bi, l0, nb in evacs:
                    pB[bi].read(P.tok("V"))

            cellsM, evacsM = sect("A", 8)
            for l in (range(8) if sgate & 8 else []):
                bi, col, st_, sp_ = cellsM[l]
                cell = bank(bi)[:, col:col + 128]
                Atrs = Atr_t[:, ts(l)]
                Atbs = Atb_t[:, ts(l)]
                T_(lambda e, cell=cell, Atrs=Atrs, Atbs=Atbs, st_=st_,
                   sp_=sp_, first=first: e.matmul(
                       cell, Atrs, Atbs, start=st_,
                       stop=(first and sp_), skip_group_check=True),
                   reads=([B["Atr"], B["Atb"]] if l == 0 else []),
                   writes=[pB[bi]] if st_ else [], mark=(first and l == 7))
            cellsZ, evacsZ = sect("B", 8)
            for l in (range(8) if sgate & 16 else []):
                h = 8 * w + l
                Aqss = Aqs_t[:, ts(l)]
                vns = vn_t[:, h * 64: h * 64 + 64]
                bi, col, st_, sp_ = cellsZ[l]
                cell = bank(bi)[:, col:col + 64]
                T_(lambda e, cell=cell, Aqss=Aqss, vns=vns, st_=st_, sp_=sp_:
                   e.matmul(cell, Aqss, vns, start=st_, stop=sp_,
                            skip_group_check=True),
                   reads=([B["Aqs"], B["vn"]] if l == 0 else []),
                   writes=[pB[bi]] if st_ else [], mark=sp_)
            for bi, l0, nb in evacsZ:
                pB[bi].wrote(P.tok("T"))
            for _k, (bi, l0, nb) in enumerate(evacsZ):
                sap = psrc(bi, 128, nb, 64)
                dap = ddst(Zb_t, 512, l0, nb, 64, 64)
                SE_(lambda e, dap=dap, sap=sap: e.copy(dap, sap),
                   reads=[pB[bi]], writes=[B["Zb"]] if _k == 0 else [])
            B["Zb"].wrote(P.tok("S"))
            for bi, l0, nb in evacsZ:
                pB[bi].read(P.tok("S"))
            if (not first) and (sgate & 32):
                for l in range(8):
                    u = l // 2
                    bi, col, st_, sp_ = cellsM[l]
                    cell = bank(bi)[:, col:col + 128]
                    QTzs = QTz_t[:, ts(l)]
                    Ys = Ybf_t[:, u * 128: u * 128 + 128]
                    T_(lambda e, cell=cell, QTzs=QTzs, Ys=Ys, sp_=sp_:
                       e.matmul(cell, QTzs, Ys, start=False, stop=sp_,
                                skip_group_check=True),
                       reads=([B["Ybf"], B["QTz"]] if l == 0 else []),
                       writes=[], mark=(l == 7))
            mt_tok = P.tok("T")
            for bi, l0, nb in evacsM:
                pB[bi].wrote(mt_tok)
            for _k, (bi, l0, nb) in enumerate(
                    evacsM if (sgate & 8 and sgate & 65536) else []):
                VE_(lambda e, bi=bi, l0=l0, nb=nb: e.tensor_tensor(
                    slot3(MtS_t, 1024, l0, nb), pbank2(bi, nb * 128),
                    mrep(triu, nb), OP.mult),
                   reads=[pB[bi]] + ([B["masks"]] if _k == 0 else []),
                   writes=[B["MtS"]] if _k == 0 else [])
            B["MtS"].wrote(P.tok("V"))
            for bi, l0, nb in evacsM:
                pB[bi].read(P.tok("V"))

            cellsO, evacsO = sect("A", 4)
            if (not first) and (sgate & 64):
                for u in range(4):
                    p = 4 * w + u
                    bi, col, st_, sp_ = cellsO[u]
                    cell = bank(bi)[:, col:col + 128]
                    W0s = W0b_t[:, u * 128:(u + 1) * 128]
                    QTc = QTg_t[:, p * GB + tc * 128: p * GB + tc * 128 + 128]
                    T_(lambda e, cell=cell, W0s=W0s, QTc=QTc, st_=st_:
                       e.matmul(cell, W0s, QTc, start=st_, stop=False,
                                skip_group_check=True),
                       reads=([B["W0b"]] if u == 0 else []),
                       writes=[pB[bi]] if st_ else [], mark=False)
            cells, evacs = sect("B", 4)
            if (not first) and (sgate & 512):
                for u in range(4):
                    p = 4 * w + u
                    bi, col, st_, sp_ = cells[u]
                    cell = bank(bi)[:, col:col + 128]
                    KKs = KKbd_t[:, u * 128:(u + 1) * 128]
                    C0s = C0bd_t[:, p * 128:(p + 1) * 128]
                    T_(lambda e, cell=cell, KKs=KKs, C0s=C0s, st_=st_:
                       e.matmul(cell, KKs, C0s, start=st_, stop=False,
                                skip_group_check=True),
                       reads=([B["KKbd"]] if u == 0 else []),
                       writes=[pB[bi]] if st_ else [], mark=False)
            for l in ([0, 2, 4, 6, 1, 3, 5, 7] if sgate & 64 else []):
                h = 8 * w + l
                q = l % 2
                u = l // 2
                psl = slice(64 * q, 64 * q + 64)
                bi, col, st_, sp_ = cellsO[u]
                cell = bank(bi)[psl, col:col + 128]
                vns = vn_t[:, h * 64: h * 64 + 64]
                MtSs = MtS_t[:, ts(l)]
                T_(lambda e, cell=cell, vns=vns, MtSs=MtSs, q=q, st_=st_,
                   sp_=sp_, first=first: e.matmul(
                       cell, vns, MtSs, start=(st_ if first else False),
                       stop=(sp_ and q == 1),
                       tile_position=(0, 64 * q), skip_group_check=True),
                   reads=([B["MtS"], B["vn"]] if l == 0 else []),
                   writes=[pB[bi]] if (first and st_ and q == 0) else [],
                   mark=(l == 7))
            ot_tok = P.tok("T")
            for bi, l0, nb in evacsO:
                pB[bi].wrote(ot_tok)
            for bi, l0, nb in evacsO:
                sap = psrc(bi, 128, nb, 128)
                dap = bass.AP(OTg_t, (4 * w + l0) * 512 + (tc % 4) * 128,
                              [[4096, 128], [3 * 512, nb], [1, 128]])
                SE_(lambda e, dap=dap, sap=sap: e.copy(dap, sap),
                   reads=[pB[bi]], writes=[B["OTg"]])
            for bi, l0, nb in evacsO:
                pB[bi].read(P.tok("S"))

            for u in (range(4) if sgate & 512 else []):
                p = 4 * w + u
                bi, col, st_, sp_ = cells[u]
                cell = bank(bi)[:, col:col + 128]
                kns = kn_t[:, p * 128:(p + 1) * 128]
                Zs = Zb_t[:, u * 128:(u + 1) * 128]
                T_(lambda e, cell=cell, kns=kns, Zs=Zs, st_=st_, sp_=sp_,
                   first=first: e.matmul(
                       cell, kns, Zs, start=(first and st_), stop=sp_,
                       skip_group_check=True),
                   reads=([B["Zb"], B["kn"]] if u == 0 else []),
                   writes=[pB[bi]] if (first and st_) else [],
                   mark=sp_)
            g_tok = P.tok("T")
            for nm in ("KKbd", "C0bd", "S0bd", "W0b", "Ybf", "MtS", "vn",
                       "Atr", "Atb", "Aqs", "Zb", "kn", "qn", "QTg", "KTg",
                       "QTz", "KTz"):
                B[nm].read(g_tok)
            for bi, l0, nb in evacs:
                pB[bi].wrote(g_tok)
            for _k, (bi, l0, nb) in enumerate(
                    evacs if (sgate & 512 and sgate & 524288) else []):
                VE_(lambda e, bi=bi, l0=l0, nb=nb: e.tensor_tensor(
                    slot3(updt, 512, l0, nb), pbank2(bi, nb * 128),
                    mrep(bdm, nb), OP.mult),
                   reads=[pB[bi]] + ([B["masks"]] if _k == 0 else []),
                   writes=[B["updt"]] if _k == 0 else [])
            B["updt"].wrote(P.tok("V"))
            for bi, l0, nb in evacs:
                pB[bi].read(P.tok("V"))
            vt = VE_(lambda e, w=w: e.tensor_tensor(
                Gp_t[:, 4 * w * 128: 4 * w * 128 + 512],
                Gp_t[:, 4 * w * 128: 4 * w * 128 + 512],
                updt[:], OP.add),
                reads=[B["updt"]], writes=[B["G"]])
            B["updt"].read(vt)

    def tap(nm, t_, buf):
        if debug and nm in dbg_d:
            dt = DMA("G", lambda e, nm=nm, t_=t_: e.dma_start(
                dbg_d[nm].ap(), t_[:]), reads=[buf])
            buf.read(dt)

    for g in range(NG):
        if g == 0:
            layernorm_group(0, xn_war_tok=P.tok("T"))
            tap("txn", xn, B["xn"])
        proj_group_T(g, B["wqs"], wqs_t, B["QTg"], QTg_t)
        proj_group_T(g, B["wks"], wks_t, B["KTg"], KTg_t)
        if g == 0:
            tap("tQT", QTg_t, B["QTg"])
            tap("tKT", KTg_t, B["KTg"])
        for ci in range(CPG):
            c = g * CPG + ci
            proj_t2n(c, QTg_t, B["QTg"], B["qn"], qn_t, "A")
            proj_t2n(c, KTg_t, B["KTg"], B["kn"], kn_t, "B")
            proj_chunk_nat(c, B["wvs"], wvs_t, B["vn"], vn_t)
            if ci == CPG - 1 and g + 1 < NG:
                ln_stats(g + 1)
            if c == 0:
                tap("tqn", qn_t, B["qn"])
                tap("tkn", kn_t, B["kn"])
                tap("tvn", vn_t, B["vn"])
            if c < nsc:
                scan_chunk(c)
            if ci == CPG - 1 and wo_on:
                for w2 in range(2):
                    for ji in range(4):
                        jd = 4 * w2 + ji
                        bi = ji
                        pt = bank(bi)
                        pb = pB[bi]
                        for jh in range(NJ):
                            lhs = wos_t[:, jh * D + jd * 128:
                                        jh * D + jd * 128 + 128]
                            rhs = OTg_t[:, jh * 512: jh * 512 + 512]
                            T_(lambda e, pt=pt, lhs=lhs, rhs=rhs, jh=jh:
                               e.matmul(pt, lhs, rhs,
                                        start=(jh == 0),
                                        stop=(jh == NJ - 1)),
                               reads=([B["wos"], B["OTg"]] if jh == 0
                                      else []),
                               writes=([pb] if jh == 0 else []),
                               mark=(jh == NJ - 1))
                        pb.wrote(P.tok("T"))
                        xs = xres[:, jd * T + g * GB:
                                  jd * T + g * GB + 512]
                        vt = V_(lambda e, xs=xs, pt=pt: e.tensor_tensor(
                            xs, xs, pt[:], OP.add),
                            reads=[pb], writes=[B["xres"]])
                        pb.read(vt)
                B["OTg"].read(P.tok("T"))
            if c == 0:
                tap("tS1", Sp_t, B["S"])
                tap("tC1", Cp_t, B["C"])
                tap("tG1", Gp_t, B["G"])
            if c == 1:
                tap("tS2", Sp_t, B["S"])
                tap("tC2", Cp_t, B["C"])
                tap("tG2", Gp_t, B["G"])
        if g + 1 < NG:
            ln_apply(g + 1, xn_war_tok=P.tok("T"))

    if debug:
        dt = DMA("G", lambda e: e.dma_start(dbg_d["dbg2"].ap(), xres[:]),
                 reads=[B["xres"]])
        B["xres"].read(dt)

    P.barrier()
    ffn_sublayer(first_call=False)

    for j in range(NJ):
        q = ("Y", "G")[j % 2]
        dt = DMA(q, lambda e, j=j: e.dma_start(
            out_d.ap()[j * 128:(j + 1) * 128, :], xres[:, j * T:(j + 1) * T]),
            reads=[B["xres"]])
        B["xres"].read(dt)

    P.wait("Y", P.qdone("Y"))
    P.wait("Y", P.qdone("G"))
    P.emit("Y", lambda e: e.nop(), mark=False)

    with nc.Block() as blk:
        blk.tensor(lambda e: P.run("T", e))
        blk.vector(lambda e: P.run("V", e))
        blk.scalar(lambda e: P.run("S", e))
        blk.gpsimd(lambda e: P.run("G", e))
        blk.sync(lambda e: P.run("Y", e))

    stack.close()
    return nc


_NC_CACHE = {}


def _get_nc(debug=False):
    key = bool(debug)
    if key not in _NC_CACHE:
        _NC_CACHE[key] = build(debug=debug)
    return _NC_CACHE[key]


def _host_constants():
    triu1 = np.triu(np.ones((128, 128), np.float32)).astype(ml_dtypes.bfloat16)
    striu1 = np.triu(np.ones((128, 128), np.float32), 1).astype(
        ml_dtypes.bfloat16)
    q = np.arange(128) // 64
    bdm1 = (q[:, None] == q[None, :]).astype(np.float32).astype(
        ml_dtypes.bfloat16)
    pm1 = np.zeros((128, 256), np.float32)
    pm1[0:64, 0:128] = 1.0
    pm1[64:128, 128:256] = 1.0
    pm1 = pm1.astype(ml_dtypes.bfloat16)
    ident = np.eye(128, dtype=np.float32)
    return triu1, striu1, bdm1, pm1, ident


def _in_maps(L0_attn_x, L0_attn_W_q, L0_attn_W_k, L0_attn_W_v, L0_attn_W_o,
             L0_ffn_w1, L0_ffn_b1, L0_ffn_w2, L0_ffn_b2):
    triu1, striu1, bdm1, pm1, ident = _host_constants()
    common = {
        "wq": np.ascontiguousarray(L0_attn_W_q, np.float32),
        "wk": np.ascontiguousarray(L0_attn_W_k, np.float32),
        "wv": np.ascontiguousarray(L0_attn_W_v, np.float32),
        "wo": np.ascontiguousarray(L0_attn_W_o, np.float32),
        "w1": np.ascontiguousarray(L0_ffn_w1, np.float32),
        "b1": np.ascontiguousarray(L0_ffn_b1, np.float32),
        "w2": np.ascontiguousarray(L0_ffn_w2, np.float32),
        "b2": np.ascontiguousarray(L0_ffn_b2, np.float32),
        "triu1": triu1, "striu1": striu1, "bdm1": bdm1, "pm1": pm1,
        "ident": ident,
    }
    x = np.ascontiguousarray(L0_attn_x, np.float32)
    return [dict(common, x=np.ascontiguousarray(x[i].T)) for i in range(8)]


def kernel(L0_attn_x, L0_attn_W_q, L0_attn_W_k, L0_attn_W_v, L0_attn_W_o,
           L0_ffn_w1, L0_ffn_b1, L0_ffn_w2, L0_ffn_b2):
    nc = _get_nc(debug=False)
    in_maps = _in_maps(L0_attn_x, L0_attn_W_q, L0_attn_W_k, L0_attn_W_v,
                       L0_attn_W_o, L0_ffn_w1, L0_ffn_b1, L0_ffn_w2,
                       L0_ffn_b2)
    res = run_bass_kernel_spmd(nc, in_maps, core_ids=list(range(8)))
    out = np.stack([np.asarray(r["out"]).T for r in res.results], axis=0)
    return np.ascontiguousarray(out, np.float32)


# version 15
# speedup vs baseline: 1.0131x; 1.0131x over previous
import numpy as np
import ml_dtypes
import concourse.bass as bass
import concourse.mybir as mybir
from concourse.bass_utils import run_bass_kernel_spmd

F32 = mybir.dt.float32
BF16 = mybir.dt.bfloat16
AF = mybir.ActivationFunctionType
OP = mybir.AluOpType
AX = mybir.AxisListType

D = 1024
T = 2048
H = 16
DK = 64
FF = 4096
NJ = D // 128
NF = FF // 128
NT = T // 128
GB = 512
NG = T // GB
CH = 128
NC_ = T // CH
CPG = GB // CH

ENGS = ("T", "V", "S", "G", "Y")


class Prog:
    NLANE = 6

    def __init__(self, nc):
        self.nc = nc
        self.prog = {e: [] for e in ENGS}
        self.count = {e: 0 for e in ENGS}
        self.hwm = {}
        self._pend = {e: [] for e in ENGS}
        self.sems = {}
        self.dsems = {}

    def make_sems(self, stack):
        for e in ENGS:
            self.sems[e] = stack.enter_context(self.nc.semaphore(f"ms{e}"))
        for e in ("Y", "G", "S", "V"):
            self.dsems[e] = [stack.enter_context(self.nc.semaphore(f"dm{e}{i}"))
                             for i in range(self.NLANE)]
        self.lane_cnt = {e: [0] * self.NLANE for e in ("Y", "G", "S", "V")}
        self.lane_idx = {e: 0 for e in ("Y", "G", "S", "V")}

    def wait(self, eng, tok):
        if tok is None:
            return
        if isinstance(tok, list):
            for t in tok:
                self.wait(eng, t)
            return
        prod, kind, val = tok
        if kind == "c":
            if prod == eng or val <= 0:
                return
            key = (eng, prod, "c")
            if self.hwm.get(key, 0) >= val:
                return
            self.hwm[key] = val
            self._pend[eng].append((prod, "c", val))
        else:
            lane, v = val
            if v <= 0:
                return
            key = (eng, prod, "d", lane)
            if self.hwm.get(key, 0) >= v:
                return
            self.hwm[key] = v
            self._pend[eng].append((prod, ("d", lane), v))

    def emit(self, eng, fn, mark=False, dma=False):
        tok = None
        if dma:
            lane = self.lane_idx[eng]
            self.lane_idx[eng] = (lane + 1) % self.NLANE
            prev = self.lane_cnt[eng][lane]
            if prev > 0:
                self.wait(eng, (eng, "d", (lane, prev)))
            self.lane_cnt[eng][lane] = prev + 16
            waits, self._pend[eng] = self._pend[eng], []
            incs = [(eng, ("d", lane), 16)]
            tok = (eng, "d", (lane, prev + 16))
            self.prog[eng].append((waits, fn, incs))
            return tok
        waits, self._pend[eng] = self._pend[eng], []
        incs = []
        if mark:
            self.count[eng] += 1
            incs.append((eng, "c", 1))
            tok = (eng, "c", self.count[eng])
        self.prog[eng].append((waits, fn, incs))
        return tok

    def tok(self, eng):
        return (eng, "c", self.count[eng])

    def qdone(self, q):
        return [(q, "d", (ln, c)) for ln, c in enumerate(self.lane_cnt[q])
                if c > 0]

    def barrier(self):
        snap_c = dict(self.count)
        for e in ENGS:
            for o in ENGS:
                if o != e and snap_c[o] > 0:
                    self.wait(e, (o, "c", snap_c[o]))
                if o in ("Y", "G", "S", "V"):
                    self.wait(e, self.qdone(o))

    def run(self, eng_name, engine):
        for waits, fn, incs in self.prog[eng_name]:
            for (prod, kind, val) in waits:
                if kind == "c":
                    engine.wait_ge(self.sems[prod], val)
                else:
                    engine.wait_ge(self.dsems[prod][kind[1]], val)
            instr = fn(engine)
            for (prod, kind, amt) in incs:
                if kind == "c":
                    instr.then_inc(self.sems[prod], amt)
                else:
                    instr.then_inc(self.dsems[prod][kind[1]], amt)


class Buf:
    def __init__(self, name=""):
        self.name = name
        self.w = None
        self.r = {}

    def wrote(self, tok):
        if tok is not None:
            self.w = tok
            self.r = {}

    def read(self, tok):
        if tok is not None:
            self.r[tok[0]] = tok

    def war(self):
        toks = list(self.r.values())
        if self.w is not None:
            toks.append(self.w)
        return toks


def ts(i, n=128):
    return slice(i * n, (i + 1) * n)


def build(debug=False, nsc=99, wo_on=True, sgate=(1 << 20) - 1):
    EV_S = bool(sgate & 2048)
    EV_V = bool(sgate & 4096)
    import contextlib
    nc = bass.Bass("TRN2", target_bir_lowering=False,
                   detect_race_conditions=False)
    stack = contextlib.ExitStack()
    P = Prog(nc)
    P.make_sems(stack)

    x_d = nc.dram_tensor("x", [D, T], F32, kind="ExternalInput")
    wq_d = nc.dram_tensor("wq", [D, D], F32, kind="ExternalInput")
    wk_d = nc.dram_tensor("wk", [D, D], F32, kind="ExternalInput")
    wv_d = nc.dram_tensor("wv", [D, D], F32, kind="ExternalInput")
    wo_d = nc.dram_tensor("wo", [D, D], F32, kind="ExternalInput")
    w1_d = nc.dram_tensor("w1", [FF, D], F32, kind="ExternalInput")
    b1_d = nc.dram_tensor("b1", [FF], F32, kind="ExternalInput")
    w2_d = nc.dram_tensor("w2", [D, FF], F32, kind="ExternalInput")
    b2_d = nc.dram_tensor("b2", [D], F32, kind="ExternalInput")
    triu_d = nc.dram_tensor("triu1", [128, 128], BF16, kind="ExternalInput")
    striu_d = nc.dram_tensor("striu1", [128, 128], BF16, kind="ExternalInput")
    bdm_d = nc.dram_tensor("bdm1", [128, 128], BF16, kind="ExternalInput")
    pm_d = nc.dram_tensor("pm1", [128, 256], BF16, kind="ExternalInput")
    ident_d = nc.dram_tensor("ident", [128, 128], F32, kind="ExternalInput")
    out_d = nc.dram_tensor("out", [D, T], F32, kind="ExternalOutput")
    s1_d = nc.dram_tensor("s1scr", [FF, D], BF16, kind="Internal")
    s2_d = nc.dram_tensor("s2scr", [D, FF], BF16, kind="Internal")
    w2t_d = nc.dram_tensor("w2tscr", [FF, D], BF16, kind="Internal")
    w1t_d = nc.dram_tensor("w1tscr", [128, NJ * FF], BF16, kind="Internal")
    dbg_d = {}
    if debug:
        dbg_d["dbg1"] = nc.dram_tensor("dbg1", [128, NJ * T], F32,
                                       kind="ExternalOutput")
        dbg_d["dbg2"] = nc.dram_tensor("dbg2", [128, NJ * T], F32,
                                       kind="ExternalOutput")
        for nm, wid, dt in (("txn", NJ * GB, BF16), ("tQT", 8 * GB, BF16),
                            ("tKT", 8 * GB, BF16), ("tqn", D, BF16),
                            ("tkn", D, BF16), ("tvn", D, BF16),
                            ("tUP", GB, F32),
                            ("tOTg", 4096, BF16), ("tS1", 1024, F32),
                            ("tC1", 1024, F32), ("tG1", 1024, F32),
                            ("tS2", 1024, F32), ("tC2", 1024, F32),
                            ("tG2", 1024, F32),
                            ("tw1T", NJ * FF, BF16),
                            ("tghT", NF * GB, BF16),
                            ("tF0", GB, F32), ("tF7", GB, F32),
                            ("tW20", 2048, BF16), ("tW27", 2048, BF16),
                            ("tA2h", NJ, F32), ("tA2t", 32, F32)):
            dbg_d[nm] = nc.dram_tensor(nm, [128, wid], dt,
                                       kind="ExternalOutput")

    xres = nc.alloc_sbuf_tensor("xres", [128, NJ * T], F32)
    xn = nc.alloc_sbuf_tensor("xn", [128, NJ * GB], BF16)
    triu = nc.alloc_sbuf_tensor("triu_s", [128, 128], BF16)
    striu = nc.alloc_sbuf_tensor("striu_s", [128, 128], BF16)
    bdm = nc.alloc_sbuf_tensor("bdm_s", [128, 128], BF16)
    pm = nc.alloc_sbuf_tensor("pm_s", [128, 256], BF16)
    ident = nc.alloc_sbuf_tensor("idents", [128, 128], F32)
    identb = nc.alloc_sbuf_tensor("identb", [128, 128], BF16)
    ones_col = nc.alloc_sbuf_tensor("ones_col", [128, 1], F32)
    ones_row = nc.alloc_sbuf_tensor("ones_row", [1, 128], F32)
    alpha1 = nc.alloc_sbuf_tensor("alpha1", [128, NF], F32)
    b1s = nc.alloc_sbuf_tensor("b1s", [128, NF], F32)
    a2h = nc.alloc_sbuf_tensor("a2h", [128, NJ], F32)
    b2h = nc.alloc_sbuf_tensor("b2h", [128, NJ], F32)
    a2t = nc.alloc_sbuf_tensor("a2t", [128, 32], F32)
    inv_c = nc.alloc_sbuf_tensor("inv_c", [128, 1], F32)
    xsq = nc.alloc_sbuf_tensor("xsq", [128, GB], F32)
    lnt = nc.alloc_sbuf_tensor("lnt", [128, GB], F32)
    updt = nc.alloc_sbuf_tensor("updt", [128, GB], F32)
    m_row = lnt[0:1, :]
    v_row = updt[0:1, :]
    r_row = xsq[0:1, :]
    cst = nc.alloc_sbuf_tensor("cst", [128, 4], F32)

    SCR0 = (nc._sbuf_addr_for_side("left") + 31) // 32 * 32
    SCRTOP = nc._sbuf_addr_for_side("right")
    AVAIL = SCRTOP - SCR0

    def at(name, shape, dtype, off):
        return nc.alloc_sbuf_tensor_at(name, shape, dtype, offset=SCR0 + off)

    w1T_t = at("w1T", [128, NJ * FF], BF16, 0)
    ghT_t = at("ghT", [128, NF * GB], BF16, 65536)
    w2s_t = at("w2s", [128, 2 * 2048], BF16, 98304)
    s1stage = [at(f"s1stage{i}", [128, 1024], F32, 98304 + i * 4096)
               for i in range(2)]
    w1stage = [at(f"w1stage{i}", [128, 1024], F32, 106496 + i * 4096)
               for i in range(2)]
    w2stage = [at(f"w2stage{i}", [128, 1024], F32, 114688 + i * 4096)
               for i in range(2)]
    s2stage = [at(f"s2stage{i}", [128, 1024], BF16, 122880 + i * 2048)
               for i in range(2)]
    bstage = at("bstage", [32, 128], F32, 126976)
    o = 0
    wqs_t = at("wqs", [128, NJ * D], BF16, o); o += 16384
    wks_t = at("wks", [128, NJ * D], BF16, o); o += 16384
    wvs_t = at("wvs", [128, NJ * D], BF16, o); o += 16384
    wos_t = at("wos", [128, NJ * D], BF16, o); o += 16384
    QTg_t = at("QTg", [128, 8 * GB], BF16, o); o += 8192
    KTg_t = at("KTg", [128, 8 * GB], BF16, o); o += 8192
    OTg_t = at("OTg", [128, 8 * 512], BF16, o); o += 8192
    QTz_t = at("QTz", [128, 8 * 128], BF16, o); o += 2048
    KTz_t = at("KTz", [128, 8 * 128], BF16, o); o += 2048
    qn_t = at("qn", [128, D], BF16, o); o += 2048
    kn_t = at("kn", [128, D], BF16, o); o += 2048
    vn_t = at("vn", [128, D], BF16, o); o += 2048
    Sp_t = at("Sp", [128, 8 * 128], F32, o); o += 4096
    Cp_t = at("Cp", [128, 8 * 128], F32, o); o += 4096
    Gp_t = at("Gp", [128, 8 * 128], F32, o); o += 4096
    S0bd_t = at("S0bd", [128, 8 * 128], BF16, o); o += 2048
    C0bd_t = at("C0bd", [128, 8 * 128], BF16, o); o += 2048
    Atb_t = at("Atb", [128, 8 * 128], BF16, o); o += 2048
    Atr_t = at("Atr", [128, 8 * 128], BF16, o); o += 2048
    Aqs_t = at("Aqs", [128, 8 * 128], BF16, o); o += 2048
    MtS_t = at("MtS", [128, 8 * 128], BF16, o); o += 2048
    Ybf_t = at("Ybf", [128, 4 * 128], BF16, o); o += 1024
    W0b_t = at("W0b", [128, 4 * 128], BF16, o); o += 1024
    KKbd_t = at("KKbd", [128, 4 * 128], BF16, o); o += 1024
    Zb_t = at("Zb", [128, 8 * 64], BF16, o); o += 1024
    assert o <= AVAIL, f"attn scratch {o} > {AVAIL}"

    psA = stack.enter_context(nc.psum_tensor("psA", [128, 1536], F32))
    psB = stack.enter_context(nc.psum_tensor("psB", [128, 1536], F32))
    pL = [stack.enter_context(nc.psum_tensor(f"pL{i}", [128, 512], F32))
          for i in range(2)]

    def bank(i):
        if i < 3:
            return psA[:, i * 512:(i + 1) * 512]
        if i < 6:
            return psB[:, (i - 3) * 512:(i - 3) * 512 + 512]
        return pL[i - 6][:]

    pB = [Buf(f"bank{i}") for i in range(8)]

    B = {}
    for name in ("xres", "xn", "xsq", "lnt", "m_row", "v_row", "r_row",
                 "updt", "alpha1", "b1s", "a2h", "b2h", "a2t", "consts",
                 "masks", "w1T", "ghT", "S", "C", "G", "S0bd", "C0bd",
                 "Atb", "Atr", "Aqs", "MtS", "Ybf", "W0b", "KKbd", "Zb",
                 "QTz", "KTz", "Aqb",
                 "QTg", "KTg", "OTg", "qn", "kn", "vn",
                 "wqs", "wks", "wvs", "wos", "w2t"):
        B[name] = Buf(name)
    w2s = [Buf("w2s0"), Buf("w2s1")]
    xst = [Buf("xst0"), Buf("xst1")]
    ost = [Buf("ost0"), Buf("ost1")]
    w1st = [Buf("w1st0"), Buf("w1st1")]
    s1st = [Buf("s1st0"), Buf("s1st1")]
    w2st = [Buf("w2st0"), Buf("w2st1")]
    s2st = [Buf("s2st0"), Buf("s2st1")]
    bst = Buf("bst")

    def _do(eng, fn, reads, writes, mark, dma=False):
        for r in reads:
            if isinstance(r, Buf):
                P.wait(eng, r.w)
            elif r is not None:
                P.wait(eng, r)
        for w in writes:
            for t in w.war():
                P.wait(eng, t)
        tok = P.emit(eng, fn, mark=mark, dma=dma)
        eff = tok if tok is not None else P.tok(eng)
        for r in reads:
            if isinstance(r, Buf):
                r.read(eff)
        for w in writes:
            w.wrote(tok)
        return tok

    def T_(fn, reads=(), writes=(), mark=True):
        return _do("T", fn, reads, writes, mark)

    def V_(fn, reads=(), writes=(), mark=True):
        return _do("V", fn, reads, writes, mark)

    def S_(fn, reads=(), writes=(), mark=True):
        return _do("S", fn, reads, writes, mark)

    def G_(fn, reads=(), writes=(), mark=True):
        return _do("G", fn, reads, writes, mark)

    def SE_(fn, reads=(), writes=(), mark=True):
        if not EV_S:
            return None
        return _do("S", fn, reads, writes, mark)

    def VE_(fn, reads=(), writes=(), mark=True):
        if not EV_V:
            return None
        return _do("V", fn, reads, writes, mark)

    def DMA(q, fn, reads=(), writes=()):
        return _do(q, fn, reads, writes, mark=False, dma=True)

    xres_v = xres.rearrange("p (j t) -> p j t", t=T)
    w2t_v = w2t_d.ap().rearrange("(c p) d -> p c d", p=128)

    def mrep(mask_t, n):
        return bass.AP(mask_t, 0, [[128, 128], [0, n], [1, 128]])

    def mrep2(mask_t, n1, n2):
        return bass.AP(mask_t, 0, [[128, 128], [0, n1], [0, n2], [1, 128]])

    V_(lambda e: e.memset(ones_col[:], 1.0), writes=[B["consts"]])
    V_(lambda e: e.memset(ones_row[:], 1.0), writes=[B["consts"]])
    V_(lambda e: e.memset(inv_c[:], 1.0 / 1024.0), writes=[B["consts"]])
    V_(lambda e: e.memset(cst[:, 0:1], 1e-5), writes=[B["consts"]])
    V_(lambda e: e.memset(cst[:, 1:2], 0.5), writes=[B["consts"]])
    V_(lambda e: e.memset(cst[:, 2:3], 0.5 / 4096.0), writes=[B["consts"]])
    V_(lambda e: e.memset(cst[:, 3:4], 0.0), writes=[B["consts"]])
    DMA("Y", lambda e: e.dma_start(triu[:], triu_d.ap()))
    DMA("Y", lambda e: e.dma_start(striu[:], striu_d.ap()))
    DMA("Y", lambda e: e.dma_start(bdm[:], bdm_d.ap()))
    DMA("Y", lambda e: e.dma_start(pm[:], pm_d.ap()))
    tid = DMA("Y", lambda e: e.dma_start(ident[:], ident_d.ap()))
    V_(lambda e: e.tensor_copy(identb[:], ident[:]),
       reads=[tid], writes=[B["masks"]])
    B["masks"].wrote(tid)

    xld_toks = []
    xg_toks = [[] for _ in range(NG)]
    for g_ in range(NG):
        for j in range(NJ):
            q = "Y" if g_ == 0 else "S"
            t_ = DMA(q, lambda e, j=j, g_=g_: e.dma_start(
                xres[:, j * T + g_ * GB: j * T + (g_ + 1) * GB],
                x_d.ap()[j * 128:(j + 1) * 128, g_ * GB:(g_ + 1) * GB]),
                writes=[B["xres"]] if (g_ == 0 and j == 0) else [])
            xld_toks.append(t_)
            xg_toks[g_].append(t_)
    B["xres"].wrote(xld_toks)

    ld = DMA("Y", lambda e: e.dma_start(
        bstage[:], b1_d.ap().rearrange("(a b) -> a b", b=128)), writes=[bst])
    T_(lambda e: e.transpose(bank(4)[:, 0:32], bstage[:], ident[0:32, 0:32]),
       reads=[bst, ld, tid], writes=[pB[4]])
    ct = S_(lambda e: e.copy(b1s[:], bank(4)[:, 0:32]),
            reads=[pB[4]], writes=[B["b1s"]])
    pB[4].read(ct)
    bst.read(ct)
    ld = DMA("Y", lambda e: e.dma_start(
        bstage[0:8, :], b2_d.ap().rearrange("(a b) -> a b", b=128)),
        writes=[bst])
    T_(lambda e: e.transpose(bank(5)[:, 0:8], bstage[0:8, :], ident[0:8, 0:8]),
       reads=[bst, ld], writes=[pB[5]])
    ct = S_(lambda e: e.activation(b2h[:], bank(5)[:, 0:8], AF.Copy,
                                   scale=cst[:, 1:2]),
            reads=[pB[5], B["consts"]], writes=[B["b2h"]])
    pB[5].read(ct)
    bst.read(ct)


    def stage_w2_chunk(cc2):
        j, q4 = cc2 // 4, cc2 % 4
        b = cc2 % 2
        w2g, s2g = w2stage[b], s2stage[b]
        wb, sb_ = w2st[b], s2st[b]
        ld = DMA("S", lambda e, j=j, q4=q4, w2g=w2g: e.dma_start(
            w2g[:], w2_d.ap()[ts(j), q4 * 1024:(q4 + 1) * 1024]),
            writes=[wb])
        rt = V_(lambda e, j=j, q4=q4, w2g=w2g: e.reduce_sum(
            a2t[:, j * 4 + q4:j * 4 + q4 + 1], w2g[:], axis=AX.X,
            apply_absolute_value=True),
            reads=[wb, ld], writes=[B["a2t"]])
        st = S_(lambda e, w2g=w2g, s2g=s2g: e.activation(
            s2g[:], w2g[:], AF.Sign),
            reads=[wb, ld], writes=[sb_])
        dt = DMA("S", lambda e, j=j, q4=q4, s2g=s2g: e.dma_start(
            s2_d.ap()[ts(j), q4 * 1024:(q4 + 1) * 1024], s2g[:]),
            reads=[sb_])
        sb_.read(dt)
        wb.read(rt)
        wb.read(st)

    def emit_a2h_tail():
        for j in range(NJ):
            V_(lambda e, j=j: e.tensor_tensor(
                a2t[:, j * 4:j * 4 + 1], a2t[:, j * 4:j * 4 + 1],
                a2t[:, j * 4 + 1:j * 4 + 2], OP.add),
               reads=[B["a2t"]], writes=[B["a2t"]])
        for j in range(NJ):
            V_(lambda e, j=j: e.tensor_tensor(
                a2t[:, j * 4 + 2:j * 4 + 3], a2t[:, j * 4 + 2:j * 4 + 3],
                a2t[:, j * 4 + 3:j * 4 + 4], OP.add),
               reads=[B["a2t"]], writes=[B["a2t"]])
        for j in range(NJ):
            V_(lambda e, j=j: e.tensor_tensor(
                a2h[:, j:j + 1], a2t[:, j * 4:j * 4 + 1],
                a2t[:, j * 4 + 2:j * 4 + 3], OP.add),
               reads=[B["a2t"]], writes=[])
        B["a2h"].wrote(P.tok("V"))
        S_(lambda e: e.activation(a2h[:], a2h[:], AF.Identity,
                                  scale=cst[:, 2:3], bias=cst[:, 3:4]),
           reads=[B["a2h"], B["consts"]], writes=[B["a2h"]])
        if debug and "tA2h" in dbg_d:
            dta = DMA("G", lambda e: e.dma_start(dbg_d["tA2h"].ap(), a2h[:]),
                      reads=[B["a2h"]])
            B["a2h"].read(dta)
            dtb = DMA("G", lambda e: e.dma_start(dbg_d["tA2t"].ap(), a2t[:]),
                      reads=[B["a2t"]])
            B["a2t"].read(dtb)

    def ln_stats(g):
        col0 = g * GB
        p6, p7 = bank(6), bank(7)
        for j in range(NJ):
            xs = xres[:, j * T + col0: j * T + col0 + GB]
            S_(lambda e, xs=xs: e.square(xsq[:], xs),
               reads=[xg_toks[g][j]], writes=[B["xsq"]])
            T_(lambda e, xs=xs, j=j: e.matmul(
                p6[0:1, :], ones_col[:], xs,
                start=(j == 0), stop=(j == NJ - 1)),
               reads=[B["consts"], xg_toks[g][j]],
               writes=[pB[6]] if j == 0 else [], mark=(j == NJ - 1))
            tq = T_(lambda e, j=j: e.matmul(
                p7[0:1, :], ones_col[:], xsq[:],
                start=(j == 0), stop=(j == NJ - 1)),
                reads=[B["xsq"], B["consts"]],
                writes=[pB[7]] if j == 0 else [], mark=True)
            B["xsq"].read(tq)
        B["xres"].read(P.tok("S"))
        B["xres"].read(P.tok("T"))
        pB[6].wrote(P.tok("T"))
        pB[7].wrote(P.tok("T"))
        mt = V_(lambda e: e.tensor_scalar_mul(
            m_row, p6[0:1, :], inv_c[0:1, 0:1]),
            reads=[pB[6], B["consts"]], writes=[B["m_row"]])
        pB[6].read(mt)
        vt = V_(lambda e: e.tensor_scalar_mul(
            v_row, p7[0:1, :], inv_c[0:1, 0:1]),
            reads=[pB[7], B["consts"]], writes=[B["v_row"]])
        pB[7].read(vt)
        V_(lambda e: e.scalar_tensor_tensor(
            r_row, m_row, -1.0, m_row, OP.mult, OP.mult),
           reads=[B["m_row"]], writes=[B["r_row"]])
        V_(lambda e: e.tensor_tensor(v_row, v_row, r_row, OP.add),
           reads=[B["r_row"], B["v_row"]], writes=[B["v_row"]])
        S_(lambda e: e.activation(v_row, v_row, AF.Sqrt,
                                  bias=cst[0:1, 0:1]),
           reads=[B["v_row"], B["consts"]], writes=[B["v_row"]])
        rc = V_(lambda e: e.reciprocal(r_row, v_row),
                reads=[B["v_row"]], writes=[B["r_row"]])
        B["updt"].read(rc)

    xnj_toks = [None] * NJ

    def ln_apply(g, xn_war_tok=None):
        col0 = g * GB
        p6, p7 = bank(6), bank(7)
        tm = T_(lambda e: e.matmul(p6[:], ones_row[:], m_row,
                                   start=True, stop=True),
                reads=[B["m_row"], B["consts"]], writes=[pB[6]])
        B["lnt"].read(tm)
        tr = T_(lambda e: e.matmul(p7[:], ones_row[:], r_row,
                                   start=True, stop=True),
                reads=[B["r_row"], B["consts"]], writes=[pB[7]])
        B["xsq"].read(tr)
        for j in range(NJ):
            xs = xres[:, j * T + col0: j * T + col0 + GB]
            V_(lambda e, xs=xs: e.tensor_tensor(lnt[:], xs, p6[:],
                                                OP.subtract),
               reads=[xg_toks[g][j], pB[6]], writes=[B["lnt"]])
            ex = [xn_war_tok] if (j == 0 and xn_war_tok is not None) else []
            vt = V_(lambda e, j=j: e.tensor_tensor(
                xn[:, j * GB:(j + 1) * GB], lnt[:], p7[:], OP.mult),
                reads=[B["lnt"], pB[7]] + ex,
                writes=[B["xn"]] if j == 0 else [])
            xnj_toks[j] = vt
            pB[6].read(vt)
            pB[7].read(vt)
        B["xn"].wrote(P.tok("V"))
        B["xres"].read(P.tok("V"))

    def layernorm_group(g, xn_war_tok=None):
        ln_stats(g)
        ln_apply(g, xn_war_tok=xn_war_tok)

    def ffn1_chunk(c, w1gate=None):
        bi = c % 2
        pt = bank(bi)
        pb = pB[bi]
        for j in range(NJ):
            lhs = w1T_t[:, j * FF + c * 128: j * FF + c * 128 + 128]
            rhs = xn[:, j * GB:(j + 1) * GB]
            T_(lambda e, pt=pt, lhs=lhs, rhs=rhs, j=j: e.matmul(
                pt, lhs, rhs, start=(j == 0), stop=(j == NJ - 1)),
               reads=[xnj_toks[j]]
               + ([w1gate if w1gate is not None else B["w1T"]]
                  if j == 0 else []),
               writes=([pb] if j == 0 else []), mark=(j == NJ - 1))
        pb.wrote(P.tok("T"))
        st = S_(lambda e, pt=pt, c=c: e.activation(
            ghT_t[:, c * GB:(c + 1) * GB], pt, AF.Gelu,
            scale=alpha1[:, c:c + 1], bias=b1s[:, c:c + 1]),
            reads=[pb, B["alpha1"], B["b1s"]],
            writes=[B["ghT"]] if c == 0 else [])
        pb.read(st)

    def ffn_sublayer(first_call, skip_g0_head=False):
        if not first_call:
            rl = []
            for hh in range(2):
                rl.append(DMA(("Y", "G")[hh], lambda e, hh=hh: e.dma_start(
                    w1T_t[:, hh * 4 * FF:(hh + 1) * 4 * FF],
                    w1t_d.ap()[:, hh * 4 * FF:(hh + 1) * 4 * FF]),
                    reads=[w1t_stok] if hh == 0 else [],
                    writes=[B["w1T"]] if hh == 0 else []))
            B["w1T"].wrote(rl)
        if debug and first_call and "tw1T" in dbg_d:
            dtw = DMA("G", lambda e: e.dma_start(dbg_d["tw1T"].ap(), w1T_t[:]),
                      reads=[B["w1T"]])
            B["w1T"].read(dtw)

        if not skip_g0_head:
            layernorm_group(0, xn_war_tok=P.tok("T"))
        for g in range(NG):
            if not (skip_g0_head and g == 0):
                for c in range(NF):
                    ffn1_chunk(c)
                B["ghT"].wrote(P.tok("S"))
            if debug and first_call and g == 0 and "tghT" in dbg_d:
                dtg = DMA("G", lambda e: e.dma_start(dbg_d["tghT"].ap(),
                                                     ghT_t[:]),
                          reads=[B["ghT"]])
                B["ghT"].read(dtg)

            if g + 1 < NG:
                ln_stats(g + 1)

            for w in range(2):
                for c in range(NF):
                    slot = (c // 4) % 2
                    ci = c % 4
                    sb = w2s[slot]
                    if ci == 0:
                        if first_call and g == 0:
                            q = "Y" if slot == 0 else "S"
                            tl_toks = []
                            for cc in range(4):
                                src = s2_d.ap()[w * 512:(w + 1) * 512,
                                                (c + cc) * 128:(c + cc + 1) * 128]
                                d1 = w2s_t[:, slot * 2048 + cc * 512:
                                           slot * 2048 + cc * 512 + 512]
                                tl_toks.append(DMA(
                                    q, lambda e, d1=d1, src=src:
                                    e.dma_start_transpose(d1, src),
                                    reads=([s2_done, B["a2h"], B["w1T"]]
                                           if cc == 0 else []),
                                    writes=[sb]))
                            sb.wrote(tl_toks)
                            if debug and ((w == 0 and c == 0) or
                                          (w == 1 and c == 28)) and \
                                    "tW20" in dbg_d:
                                nmw = "tW20" if w == 0 else "tW27"
                                dtw2 = DMA("S", lambda e, nmw=nmw, slot=slot:
                                           e.dma_start(
                                               dbg_d[nmw].ap(),
                                               w2s_t[:, slot * 2048:
                                                     slot * 2048 + 2048]),
                                           reads=[sb])
                                sb.read(dtw2)
                            wdst = w2t_v[:, c:c + 4, w * 512:(w + 1) * 512]
                            wsrc = w2s_t.rearrange(
                                "p (s c dd) -> p s c dd", s=2, dd=512)[
                                :, slot, :, :]
                            dt = DMA("G", lambda e, wdst=wdst, wsrc=wsrc:
                                     e.dma_start(wdst, wsrc), reads=[sb])
                            sb.read(dt)
                        else:
                            q = "Y" if slot == 0 else "S"
                            src = w2t_v[:, c:c + 4, w * 512:(w + 1) * 512]
                            dst = w2s_t.rearrange(
                                "p (s c dd) -> p s c dd", s=2, dd=512)[
                                :, slot, :, :]
                            DMA(q, lambda e, dst=dst, src=src:
                                e.dma_start(dst, src),
                                reads=[B["w2t"]], writes=[sb])
                    for ji in range(4):
                        j = 4 * w + ji
                        bi = 2 + ji
                        pt = bank(bi)
                        pb = pB[bi]
                        lhs = w2s_t[:, slot * 2048 + ci * 512 + ji * 128:
                                    slot * 2048 + ci * 512 + ji * 128 + 128]
                        rhs = ghT_t[:, c * GB:(c + 1) * GB]
                        T_(lambda e, pt=pt, lhs=lhs, rhs=rhs, c=c: e.matmul(
                            pt, lhs, rhs, start=(c == 0), stop=(c == NF - 1)),
                           reads=([sb, B["ghT"]] if ji == 0 else [sb]),
                           writes=([pb] if c == 0 else []),
                           mark=(ji == 3))
                ytok = P.tok("T")
                B["ghT"].read(ytok)
                for ji in range(4):
                    pB[2 + ji].wrote(ytok)
                for ji in range(4):
                    j = 4 * w + ji
                    pt = bank(2 + ji)
                    pb = pB[2 + ji]
                    st = S_(lambda e, pt=pt, j=j: e.activation(
                        updt[:], pt, AF.Identity,
                        scale=a2h[:, j:j + 1], bias=b2h[:, j:j + 1]),
                        reads=[pb, B["a2h"], B["b2h"]], writes=[B["updt"]])
                    pb.read(st)
                    if debug and first_call and g == 0 and \
                            ((w == 0 and ji == 0) or (w == 1 and ji == 3)) \
                            and "tF0" in dbg_d:
                        nmf = "tF0" if w == 0 else "tF7"
                        dtf = DMA("G", lambda e, nmf=nmf: e.dma_start(
                            dbg_d[nmf].ap(), updt[:]), reads=[B["updt"]])
                        B["updt"].read(dtf)
                    xs = xres[:, j * T + g * GB: j * T + g * GB + GB]
                    vt = V_(lambda e, xs=xs: e.tensor_tensor(
                        xs, xs, updt[:], OP.add),
                        reads=[B["updt"]], writes=[B["xres"]])
                    B["updt"].read(vt)
                if w == 0 and g + 1 < NG:
                    ln_apply(g + 1, xn_war_tok=P.tok("T"))
            if first_call and g == 0:
                B["w2t"].wrote(P.qdone("G"))

    layernorm_group(0, xn_war_tok=P.tok("T"))
    w1ctoks = []
    for c in range(NF + 1):
        if c < NF:
            b = c % 2
            w1g, s1g = w1stage[b], s1stage[b]
            wb, sb_ = w1st[b], s1st[b]
            ld = DMA("Y", lambda e, c=c, w1g=w1g: e.dma_start(
                w1g[:], w1_d.ap()[ts(c), :]), writes=[wb])
            rt = V_(lambda e, c=c, w1g=w1g: e.reduce_sum(
                alpha1[:, c:c + 1], w1g[:], axis=AX.X,
                apply_absolute_value=True),
                reads=[wb, ld], writes=[])
            S_(lambda e, c=c: e.activation(
                alpha1[:, c:c + 1], alpha1[:, c:c + 1], AF.Identity,
                scale=inv_c[:, 0:1], bias=cst[:, 3:4]),
               reads=[rt, B["consts"]], writes=[])
            st = S_(lambda e, w1g=w1g, s1g=s1g: e.activation(
                s1g[:], w1g[:], AF.Sign),
                reads=[wb, ld], writes=[sb_])
            wb.read(rt)
            wb.read(st)
            for j in range(NJ):
                half = j // 4
                pt2 = (psA if half == 0 else psB)[:, 1024 + (j % 4) * 128:
                                                  1024 + (j % 4) * 128 + 128]
                pbh = pB[2] if half == 0 else pB[5]
                T_(lambda e, pt2=pt2, s1g=s1g, j=j: e.transpose(
                    pt2, s1g[:, ts(j)], ident[:]),
                   reads=([sb_, tid] if j == 0 else []),
                   writes=([pbh] if j % 4 == 0 else []),
                   mark=(j % 4 == 3))
            ttok = P.tok("T")
            pB[2].wrote(ttok)
            pB[5].wrote(ttok)
            sb_.read(ttok)
            ctl = None
            for half, pst in ((0, psA), (1, psB)):
                w1dst = bass.AP(w1T_t, half * 4 * FF + c * 128,
                                [[NJ * FF, 128], [FF, 4], [1, 128]])
                psrc_ = bass.AP(pst.ap().tensor, pst.ap().offset + 1024,
                                [[1536, 128], [128, 4], [1, 128]])
                pbh = pB[2] if half == 0 else pB[5]
                ctl = S_(lambda e, w1dst=w1dst, psrc_=psrc_: e.copy(
                    w1dst, psrc_), reads=[pbh], writes=[B["w1T"]])
                pbh.read(ctl)
            w1ctoks.append(P.tok("S"))
            stage_w2_chunk(c)
        if c >= 1:
            ffn1_chunk(c - 1, w1gate=w1ctoks[c - 1])
    B["ghT"].wrote(P.tok("S"))
    B["alpha1"].wrote(P.tok("S"))
    s2_done = P.qdone("S")
    emit_a2h_tail()
    s1_done = P.qdone("Y")

    ffn_sublayer(first_call=True, skip_g0_head=True)
    if debug:
        dt = DMA("G", lambda e: e.dma_start(dbg_d["dbg1"].ap(), xres[:]),
                 reads=[B["xres"]])
        B["xres"].read(dt)

    P.barrier()

    w1t_stok = DMA("G", lambda e: e.dma_start(w1t_d.ap(), w1T_t[:]),
                   reads=[B["w1T"]])
    B["w1T"].read(w1t_stok)

    for nm, srcd, dstt in (("wqs", wq_d, wqs_t), ("wks", wk_d, wks_t),
                           ("wvs", wv_d, wvs_t), ("wos", wo_d, wos_t)):
        src_v = srcd.ap().rearrange("(j p) e -> p j e", p=128)
        DMA("G", lambda e, dstt=dstt, src_v=src_v: e.dma_start(dstt[:], src_v),
            writes=[B[nm]])

    for _pi, _pt in ((0, psA), (1, psB)):
        vt = V_(lambda e, _pt=_pt: e.memset(_pt[:], 0.0),
                writes=[pB[3 * _pi], pB[3 * _pi + 1], pB[3 * _pi + 2]])
        for _b in range(3):
            pB[3 * _pi + _b].wrote(vt)
    V_(lambda e: e.memset(Sp_t[:], 0.0), writes=[B["S"]])
    V_(lambda e: e.memset(Cp_t[:], 0.0), writes=[B["C"]])
    V_(lambda e: e.memset(Gp_t[:], 0.0), writes=[B["G"]])

    def proj_group_T(g, Wbuf, Wt, dstbuf, dst_t):
        for jE in range(8):
            bi = jE % 4
            pt = bank(bi)
            pb = pB[bi]
            for j in range(NJ):
                lhs = Wt[:, j * D + jE * 128: j * D + jE * 128 + 128]
                rhs = xn[:, j * GB:(j + 1) * GB]
                T_(lambda e, pt=pt, lhs=lhs, rhs=rhs, j=j: e.matmul(
                    pt, lhs, rhs, start=(j == 0), stop=(j == NJ - 1)),
                   reads=[xnj_toks[j]] + ([Wbuf] if j == 0 else []),
                   writes=([pb] if j == 0 else []), mark=(j == NJ - 1))
            pb.wrote(P.tok("T"))
            if jE % 2 == 0:
                st = S_(lambda e, pt=pt, jE=jE: e.copy(
                    dst_t[:, jE * GB:(jE + 1) * GB], pt),
                    reads=[pb], writes=[dstbuf] if jE == 0 else [])
            else:
                st = V_(lambda e, pt=pt, jE=jE: e.tensor_copy(
                    dst_t[:, jE * GB:(jE + 1) * GB], pt),
                    reads=[pb], writes=[dstbuf] if jE == 0 else [])
            pb.read(st)
        dstbuf.wrote(P.tok("S"))
        dstbuf.wrote(P.tok("V"))

    def proj_chunk_nat(c, Wbuf, Wt, dstbuf, dst_t):
        tc = c % CPG
        for h2 in range(2):
            bi = h2
            pt = bank(bi)
            pb = pB[bi]
            for j in range(NJ):
                lhs = xn[:, j * GB + tc * 128: j * GB + tc * 128 + 128]
                rhs = Wt[:, j * D + h2 * 512: j * D + h2 * 512 + 512]
                T_(lambda e, pt=pt, lhs=lhs, rhs=rhs, j=j: e.matmul(
                    pt, lhs, rhs, start=(j == 0), stop=(j == NJ - 1)),
                   reads=[xnj_toks[j]] + ([Wbuf] if j == 0 else []),
                   writes=([pb] if j == 0 else []), mark=(j == NJ - 1))
            pb.wrote(P.tok("T"))
            st = S_(lambda e, pt=pt, h2=h2: e.copy(
                dst_t[:, h2 * 512:(h2 + 1) * 512], pt),
                reads=[pb], writes=[dstbuf] if h2 == 0 else [])
            pb.read(st)
        dstbuf.wrote(P.tok("S"))

    def evac_grid8(dst_t, set3, cw, engine_op, reads_extra=(), out_bf=None,
                   csub=128):
        base = psA if set3 == "A" else psB
        ops = []
        src1 = bass.AP(base.ap().tensor, base.ap().offset,
                       [[1536, 128], [512, 3], [128, 2], [1, csub]])
        dst1 = bass.AP(dst_t, 0, [[8 * cw, 128], [cw, 3], [3 * cw, 2], [1, cw]])
        src2 = bass.AP(base.ap().tensor, base.ap().offset + 256,
                       [[1536, 128], [512, 2], [1, csub]])
        dst2 = bass.AP(dst_t, 6 * cw, [[8 * cw, 128], [cw, 2], [1, cw]])
        return (dst1, src1), (dst2, src2)

    rot = {"A": 0, "B": 0}

    def sect(setn, ncells, cellw=128):
        r = rot[setn]
        rot[setn] = (r + ncells) % 3
        base = 0 if setn == "A" else 3
        cells = []
        for l in range(ncells):
            bl = (r + l) % 3
            bi = base + bl
            col = (l // 3) * cellw
            cells.append((bi, col, l < 3, l >= ncells - 3))
        evacs = []
        for bl in range(3):
            l0 = (bl - r) % 3
            if l0 >= ncells:
                continue
            nb = (ncells - l0 + 2) // 3
            evacs.append((base + bl, l0, nb))
        return cells, evacs

    def psrc(bi, colw, nb, subw):
        tens = psA if bi < 3 else psB
        b = bi % 3
        return bass.AP(tens.ap().tensor, tens.ap().offset + b * 512,
                       [[1536, 128], [colw, nb], [1, subw]])

    def ddst(dst_t, stride_row, l0, nb, cw, subw):
        return bass.AP(dst_t, l0 * cw,
                       [[stride_row, 128], [3 * cw, nb], [1, subw]])

    def pcell(bi, col, subw):
        tens = psA if bi < 3 else psB
        b = bi % 3
        return bass.AP(tens.ap().tensor, tens.ap().offset + b * 512 + col,
                       [[1536, 128], [1, subw]])

    def pbank2(bi, width):
        tens = psA if bi < 3 else psB
        b = bi % 3
        return bass.AP(tens.ap().tensor, tens.ap().offset + b * 512,
                       [[1536, 128], [1, width]])

    def slot3(dst_t, row_stride, l0, nb, cw=128):
        return bass.AP(dst_t, l0 * cw, [[row_stride, 128], [3 * cw, nb],
                                        [1, cw]])

    def proj_t2n(c, srcT_t, srcbuf, dstbuf, dst_t, setn):
        tc = c % CPG
        cells, evacs = sect(setn, 8)
        for l in range(8):
            bi, col, st_, sp_ = cells[l]
            cell = bank(bi)[:, col:col + 128]
            srcs = srcT_t[:, l * GB + tc * 128: l * GB + tc * 128 + 128]
            T_(lambda e, cell=cell, srcs=srcs: e.matmul(
                cell, srcs, identb[:], start=True, stop=True,
                skip_group_check=True),
               reads=([srcbuf, B["masks"]] if l == 0 else []),
               writes=[pB[bi]] if st_ else [], mark=sp_)
        ttok = P.tok("T")
        for bi, l0, nb in evacs:
            pB[bi].wrote(ttok)
        for _k, (bi, l0, nb) in enumerate(evacs):
            sap = psrc(bi, 128, nb, 128)
            dap = ddst(dst_t, 1024, l0, nb, 128, 128)
            SE_(lambda e, dap=dap, sap=sap: e.copy(dap, sap),
               reads=[pB[bi]], writes=[dstbuf] if _k == 0 else [])
        dstbuf.wrote(P.tok("S"))
        for bi, l0, nb in evacs:
            pB[bi].read(P.tok("S"))

    def scan_chunk(c):
        tc = c % CPG
        first = (c == 0)
        if not first:
            vt = VE_(lambda e: e.tensor_tensor(
                S0bd_t.rearrange("p (r c) -> p r c", c=128),
                Sp_t.rearrange("p (r c) -> p r c", c=128),
                mrep(bdm, 8), OP.mult),
                reads=[B["S"], B["masks"]], writes=[B["S0bd"]])
            B["S"].read(vt)
            vt = VE_(lambda e: e.tensor_tensor(
                C0bd_t.rearrange("p (r c) -> p r c", c=128),
                Cp_t.rearrange("p (r c) -> p r c", c=128),
                mrep(bdm, 8), OP.mult),
                reads=[B["C"], B["masks"]], writes=[B["C0bd"]])
            B["C"].read(vt)
        def emit_s0(w):
            if sgate & 1024:
                src3q = bass.AP(QTg_t, (4 * w) * GB + tc * 128,
                                [[8 * GB, 128], [GB, 4], [1, 128]])
                src3k = bass.AP(KTg_t, (4 * w) * GB + tc * 128,
                                [[8 * GB, 128], [GB, 4], [1, 128]])
                for qq in range(2):
                    pm3 = bass.AP(pm, qq * 128, [[256, 128], [0, 4], [1, 128]])
                    qdst = bass.AP(QTz_t, qq * 128, [[1024, 128], [256, 4], [1, 128]])
                    kdst = bass.AP(KTz_t, qq * 128, [[1024, 128], [256, 4], [1, 128]])
                    V_(lambda e, qdst=qdst, src3q=src3q, pm3=pm3: e.tensor_tensor(
                        qdst, src3q, pm3, OP.mult),
                       reads=[B["QTg"], B["masks"]], writes=[B["QTz"]])
                    V_(lambda e, kdst=kdst, src3k=src3k, pm3=pm3: e.tensor_tensor(
                        kdst, src3k, pm3, OP.mult),
                       reads=[B["KTg"], B["masks"]], writes=[B["KTz"]])

        emit_s0(0)
        for w in range(2):

            cells, evacs = sect("B", 4)
            for u in (range(4) if sgate & 128 else []):
                p = 4 * w + u
                bi, col, st_, sp_ = cells[u]
                cell = bank(bi)[:, col:col + 128]
                kns = kn_t[:, p * 128:(p + 1) * 128]
                T_(lambda e, cell=cell, kns=kns, st_=st_, sp_=sp_: e.matmul(
                    cell, kns, kns, start=st_, stop=sp_,
                    skip_group_check=True),
                   reads=([B["kn"]] if u == 0 else []),
                   writes=[pB[bi]] if st_ else [], mark=sp_)
            for bi, l0, nb in evacs:
                pB[bi].wrote(P.tok("T"))
            for _k, (bi, l0, nb) in enumerate(
                    evacs if (sgate & 128 and sgate & 131072) else []):
                VE_(lambda e, bi=bi, l0=l0, nb=nb: e.tensor_tensor(
                    slot3(KKbd_t, 512, l0, nb), pbank2(bi, nb * 128),
                    mrep(bdm, nb), OP.mult),
                   reads=[pB[bi]] + ([B["masks"]] if _k == 0 else []),
                   writes=[B["KKbd"]] if _k == 0 else [])
                sdst = bass.AP(Sp_t, (4 * w + l0) * 128,
                               [[1024, 128], [3 * 128, nb], [1, 128]])
                VE_(lambda e, bi=bi, nb=nb, sdst=sdst: e.tensor_tensor(
                    sdst, sdst, pbank2(bi, nb * 128), OP.add),
                   reads=[pB[bi]], writes=[B["S"]] if _k == 0 else [])
            B["KKbd"].wrote(P.tok("V"))
            B["S"].wrote(P.tok("V"))
            for bi, l0, nb in evacs:
                pB[bi].read(P.tok("V"))

            cells, evacs = sect("A", 4)
            for u in (range(4) if sgate & 256 else []):
                p = 4 * w + u
                bi, col, st_, sp_ = cells[u]
                cell = bank(bi)[:, col:col + 128]
                qns = qn_t[:, p * 128:(p + 1) * 128]
                vns = vn_t[:, p * 128:(p + 1) * 128]
                T_(lambda e, cell=cell, qns=qns, vns=vns, st_=st_, sp_=sp_:
                   e.matmul(cell, qns, vns, start=st_, stop=sp_,
                            skip_group_check=True),
                   reads=([B["qn"]] if u == 0 else []),
                   writes=[pB[bi]] if st_ else [], mark=sp_)
            for bi, l0, nb in evacs:
                pB[bi].wrote(P.tok("T"))
            for _k, (bi, l0, nb) in enumerate(
                    evacs if (sgate & 256 and sgate & 262144) else []):
                cdst = bass.AP(Cp_t, (4 * w + l0) * 128,
                               [[1024, 128], [3 * 128, nb], [1, 128]])
                VE_(lambda e, bi=bi, nb=nb, cdst=cdst: e.tensor_tensor(
                    cdst, cdst, pbank2(bi, nb * 128), OP.add),
                   reads=[pB[bi]], writes=[B["C"]] if _k == 0 else [])
            B["C"].wrote(P.tok("V"))
            for bi, l0, nb in evacs:
                pB[bi].read(P.tok("V"))

            cells, evacs = sect("A", 8)
            for l in (range(8) if sgate & 1 else []):
                h = 8 * w + l
                p = h // 2
                bi, col, st_, sp_ = cells[l]
                cell = bank(bi)[:, col:col + 128]
                QTc = QTg_t[:, p * GB + tc * 128: p * GB + tc * 128 + 128]
                KTzs = KTz_t[:, ts(l)]
                T_(lambda e, cell=cell, KTzs=KTzs, QTc=QTc, st_=st_, sp_=sp_:
                   e.matmul(cell, KTzs, QTc, start=st_, stop=sp_,
                            skip_group_check=True),
                   reads=([B["QTg"], B["KTz"]] if l == 0 else []),
                   writes=[pB[bi]] if st_ else [], mark=sp_)
            at_tok = P.tok("T")
            for bi, l0, nb in evacs:
                pB[bi].wrote(at_tok)
            for _k, (bi, l0, nb) in enumerate(evacs):
                sap = psrc(bi, 128, nb, 128)
                dap = ddst(Atb_t, 1024, l0, nb, 128, 128)
                SE_(lambda e, dap=dap, sap=sap: e.copy(dap, sap),
                   reads=[pB[bi]], writes=[B["Atb"]] if _k == 0 else [])
            B["Atb"].wrote(P.tok("S"))
            for bi, l0, nb in evacs:
                pB[bi].read(P.tok("S"))
            for _k, (bi, l0, nb) in enumerate(
                    evacs if (sgate & 1 and sgate & 8192) else []):
                VE_(lambda e, bi=bi, l0=l0, nb=nb: e.tensor_tensor(
                    slot3(Atr_t, 1024, l0, nb), pbank2(bi, nb * 128),
                    mrep(triu, nb), OP.mult),
                   reads=[pB[bi]] + ([B["masks"], B["Atb"]] if _k == 0
                                     else []),
                   writes=[B["Atr"]] if _k == 0 else [])
            B["Atr"].wrote(P.tok("V"))
            for bi, l0, nb in evacs:
                pB[bi].read(P.tok("V"))

            cells, evacs = sect("B", 8)
            for l in (range(8) if sgate & 2 else []):
                h = 8 * w + l
                p = h // 2
                bi, col, st_, sp_ = cells[l]
                cell = bank(bi)[:, col:col + 128]
                KTc = KTg_t[:, p * GB + tc * 128: p * GB + tc * 128 + 128]
                QTzs = QTz_t[:, ts(l)]
                T_(lambda e, cell=cell, QTzs=QTzs, KTc=KTc, st_=st_, sp_=sp_:
                   e.matmul(cell, QTzs, KTc, start=st_, stop=sp_,
                            skip_group_check=True),
                   reads=([B["QTz"], B["KTg"]] if l == 0 else []),
                   writes=[pB[bi]] if st_ else [], mark=sp_)
            for bi, l0, nb in evacs:
                pB[bi].wrote(P.tok("T"))
            for _k, (bi, l0, nb) in enumerate(
                    evacs if (sgate & 2 and sgate & 16384) else []):
                VE_(lambda e, bi=bi, l0=l0, nb=nb: e.tensor_tensor(
                    slot3(Aqs_t, 1024, l0, nb), pbank2(bi, nb * 128),
                    mrep(striu, nb), OP.mult),
                   reads=[pB[bi]] + ([B["masks"]] if _k == 0 else []),
                   writes=[B["Aqs"]] if _k == 0 else [])
            B["Aqs"].wrote(P.tok("V"))
            for bi, l0, nb in evacs:
                pB[bi].read(P.tok("V"))

            if (not first) and (sgate & 4):
                cells, evacs = sect("A", 4)
                for u in range(4):
                    p = 4 * w + u
                    bi, col, st_, sp_ = cells[u]
                    cell = bank(bi)[:, col:col + 128]
                    S0s = S0bd_t[:, p * 128:(p + 1) * 128]
                    QTc = QTg_t[:, p * GB + tc * 128: p * GB + tc * 128 + 128]
                    T_(lambda e, cell=cell, S0s=S0s, QTc=QTc, st_=st_,
                       sp_=sp_: e.matmul(cell, S0s, QTc, start=st_, stop=sp_,
                                         skip_group_check=True),
                       reads=([B["S0bd"]] if u == 0 else []),
                       writes=[pB[bi]] if st_ else [], mark=sp_)
                for bi, l0, nb in evacs:
                    pB[bi].wrote(P.tok("T"))
                for _k, (bi, l0, nb) in enumerate(evacs):
                    sap = psrc(bi, 128, nb, 128)
                    dap = ddst(Ybf_t, 512, l0, nb, 128, 128)
                    SE_(lambda e, dap=dap, sap=sap: e.copy(dap, sap),
                       reads=[pB[bi]], writes=[B["Ybf"]] if _k == 0 else [])
                B["Ybf"].wrote(P.tok("S"))
                for bi, l0, nb in evacs:
                    pB[bi].read(P.tok("S"))
                cells, evacs = sect("B", 4)
                for u in range(4):
                    p = 4 * w + u
                    bi, col, st_, sp_ = cells[u]
                    cell = bank(bi)[:, col:col + 128]
                    S0s = S0bd_t[:, p * 128:(p + 1) * 128]
                    C0s = C0bd_t[:, p * 128:(p + 1) * 128]
                    T_(lambda e, cell=cell, S0s=S0s, C0s=C0s, st_=st_,
                       sp_=sp_: e.matmul(cell, S0s, C0s, start=st_, stop=sp_,
                                         skip_group_check=True),
                       reads=([B["C0bd"]] if u == 0 else []),
                       writes=[pB[bi]] if st_ else [], mark=sp_)
                for bi, l0, nb in evacs:
                    pB[bi].wrote(P.tok("T"))
                for _k, (bi, l0, nb) in enumerate(
                        evacs if sgate & 32768 else []):
                    gsrc = bass.AP(Gp_t, (4 * w + l0) * 128,
                                   [[1024, 128], [3 * 128, nb], [1, 128]])
                    VE_(lambda e, bi=bi, l0=l0, nb=nb, gsrc=gsrc:
                        e.tensor_tensor(
                            slot3(W0b_t, 512, l0, nb), pbank2(bi, nb * 128),
                            gsrc, OP.subtract),
                       reads=[pB[bi]] + ([B["G"]] if _k == 0 else []),
                       writes=[B["W0b"]] if _k == 0 else [])
                B["W0b"].wrote(P.tok("V"))
                B["G"].read(P.tok("V"))
                for bi, l0, nb in evacs:
                    pB[bi].read(P.tok("V"))

            cellsM, evacsM = sect("A", 8)
            for l in (range(8) if sgate & 8 else []):
                bi, col, st_, sp_ = cellsM[l]
                cell = bank(bi)[:, col:col + 128]
                Atrs = Atr_t[:, ts(l)]
                Atbs = Atb_t[:, ts(l)]
                T_(lambda e, cell=cell, Atrs=Atrs, Atbs=Atbs, st_=st_,
                   sp_=sp_, first=first: e.matmul(
                       cell, Atrs, Atbs, start=st_,
                       stop=(first and sp_), skip_group_check=True),
                   reads=([B["Atr"], B["Atb"]] if l == 0 else []),
                   writes=[pB[bi]] if st_ else [], mark=(first and l == 7))
            cellsZ, evacsZ = sect("B", 8)
            for l in (range(8) if sgate & 16 else []):
                h = 8 * w + l
                Aqss = Aqs_t[:, ts(l)]
                vns = vn_t[:, h * 64: h * 64 + 64]
                bi, col, st_, sp_ = cellsZ[l]
                cell = bank(bi)[:, col:col + 64]
                T_(lambda e, cell=cell, Aqss=Aqss, vns=vns, st_=st_, sp_=sp_:
                   e.matmul(cell, Aqss, vns, start=st_, stop=sp_,
                            skip_group_check=True),
                   reads=([B["Aqs"], B["vn"]] if l == 0 else []),
                   writes=[pB[bi]] if st_ else [], mark=sp_)
            for bi, l0, nb in evacsZ:
                pB[bi].wrote(P.tok("T"))
            for _k, (bi, l0, nb) in enumerate(evacsZ):
                sap = psrc(bi, 128, nb, 64)
                dap = ddst(Zb_t, 512, l0, nb, 64, 64)
                SE_(lambda e, dap=dap, sap=sap: e.copy(dap, sap),
                   reads=[pB[bi]], writes=[B["Zb"]] if _k == 0 else [])
            B["Zb"].wrote(P.tok("S"))
            for bi, l0, nb in evacsZ:
                pB[bi].read(P.tok("S"))
            if (not first) and (sgate & 32):
                for l in range(8):
                    u = l // 2
                    bi, col, st_, sp_ = cellsM[l]
                    cell = bank(bi)[:, col:col + 128]
                    QTzs = QTz_t[:, ts(l)]
                    Ys = Ybf_t[:, u * 128: u * 128 + 128]
                    T_(lambda e, cell=cell, QTzs=QTzs, Ys=Ys, sp_=sp_:
                       e.matmul(cell, QTzs, Ys, start=False, stop=sp_,
                                skip_group_check=True),
                       reads=([B["Ybf"], B["QTz"]] if l == 0 else []),
                       writes=[], mark=(l == 7))
            mt_tok = P.tok("T")
            for bi, l0, nb in evacsM:
                pB[bi].wrote(mt_tok)
            for _k, (bi, l0, nb) in enumerate(
                    evacsM if (sgate & 8 and sgate & 65536) else []):
                VE_(lambda e, bi=bi, l0=l0, nb=nb: e.tensor_tensor(
                    slot3(MtS_t, 1024, l0, nb), pbank2(bi, nb * 128),
                    mrep(triu, nb), OP.mult),
                   reads=[pB[bi]] + ([B["masks"]] if _k == 0 else []),
                   writes=[B["MtS"]] if _k == 0 else [])
            B["MtS"].wrote(P.tok("V"))
            for bi, l0, nb in evacsM:
                pB[bi].read(P.tok("V"))

            cellsO, evacsO = sect("A", 4)
            if (not first) and (sgate & 64):
                for u in range(4):
                    p = 4 * w + u
                    bi, col, st_, sp_ = cellsO[u]
                    cell = bank(bi)[:, col:col + 128]
                    W0s = W0b_t[:, u * 128:(u + 1) * 128]
                    QTc = QTg_t[:, p * GB + tc * 128: p * GB + tc * 128 + 128]
                    T_(lambda e, cell=cell, W0s=W0s, QTc=QTc, st_=st_:
                       e.matmul(cell, W0s, QTc, start=st_, stop=False,
                                skip_group_check=True),
                       reads=([B["W0b"]] if u == 0 else []),
                       writes=[pB[bi]] if st_ else [], mark=False)
            if w == 0:
                B["QTz"].read(P.tok("T"))
                B["KTz"].read(P.tok("T"))
                emit_s0(1)
            cells, evacs = sect("B", 4)
            if (not first) and (sgate & 512):
                for u in range(4):
                    p = 4 * w + u
                    bi, col, st_, sp_ = cells[u]
                    cell = bank(bi)[:, col:col + 128]
                    KKs = KKbd_t[:, u * 128:(u + 1) * 128]
                    C0s = C0bd_t[:, p * 128:(p + 1) * 128]
                    T_(lambda e, cell=cell, KKs=KKs, C0s=C0s, st_=st_:
                       e.matmul(cell, KKs, C0s, start=st_, stop=False,
                                skip_group_check=True),
                       reads=([B["KKbd"]] if u == 0 else []),
                       writes=[pB[bi]] if st_ else [], mark=False)
            for l in ([0, 2, 4, 6, 1, 3, 5, 7] if sgate & 64 else []):
                h = 8 * w + l
                q = l % 2
                u = l // 2
                psl = slice(64 * q, 64 * q + 64)
                bi, col, st_, sp_ = cellsO[u]
                cell = bank(bi)[psl, col:col + 128]
                vns = vn_t[:, h * 64: h * 64 + 64]
                MtSs = MtS_t[:, ts(l)]
                T_(lambda e, cell=cell, vns=vns, MtSs=MtSs, q=q, st_=st_,
                   sp_=sp_, first=first: e.matmul(
                       cell, vns, MtSs, start=(st_ if first else False),
                       stop=(sp_ and q == 1),
                       tile_position=(0, 64 * q), skip_group_check=True),
                   reads=([B["MtS"], B["vn"]] if l == 0 else []),
                   writes=[pB[bi]] if (first and st_ and q == 0) else [],
                   mark=(l == 7))
            ot_tok = P.tok("T")
            for bi, l0, nb in evacsO:
                pB[bi].wrote(ot_tok)
            for bi, l0, nb in evacsO:
                sap = psrc(bi, 128, nb, 128)
                dap = bass.AP(OTg_t, (4 * w + l0) * 512 + (tc % 4) * 128,
                              [[4096, 128], [3 * 512, nb], [1, 128]])
                SE_(lambda e, dap=dap, sap=sap: e.copy(dap, sap),
                   reads=[pB[bi]], writes=[B["OTg"]])
            for bi, l0, nb in evacsO:
                pB[bi].read(P.tok("S"))

            for u in (range(4) if sgate & 512 else []):
                p = 4 * w + u
                bi, col, st_, sp_ = cells[u]
                cell = bank(bi)[:, col:col + 128]
                kns = kn_t[:, p * 128:(p + 1) * 128]
                Zs = Zb_t[:, u * 128:(u + 1) * 128]
                T_(lambda e, cell=cell, kns=kns, Zs=Zs, st_=st_, sp_=sp_,
                   first=first: e.matmul(
                       cell, kns, Zs, start=(first and st_), stop=sp_,
                       skip_group_check=True),
                   reads=([B["Zb"], B["kn"]] if u == 0 else []),
                   writes=[pB[bi]] if (first and st_) else [],
                   mark=sp_)
            g_tok = P.tok("T")
            for nm in ("KKbd", "C0bd", "S0bd", "W0b", "Ybf", "MtS", "vn",
                       "Atr", "Atb", "Aqs", "Zb", "kn", "qn", "QTg", "KTg",
                       "QTz", "KTz"):
                B[nm].read(g_tok)
            for bi, l0, nb in evacs:
                pB[bi].wrote(g_tok)
            for _k, (bi, l0, nb) in enumerate(
                    evacs if (sgate & 512 and sgate & 524288) else []):
                VE_(lambda e, bi=bi, l0=l0, nb=nb: e.tensor_tensor(
                    slot3(updt, 512, l0, nb), pbank2(bi, nb * 128),
                    mrep(bdm, nb), OP.mult),
                   reads=[pB[bi]] + ([B["masks"]] if _k == 0 else []),
                   writes=[B["updt"]] if _k == 0 else [])
            B["updt"].wrote(P.tok("V"))
            for bi, l0, nb in evacs:
                pB[bi].read(P.tok("V"))
            vt = VE_(lambda e, w=w: e.tensor_tensor(
                Gp_t[:, 4 * w * 128: 4 * w * 128 + 512],
                Gp_t[:, 4 * w * 128: 4 * w * 128 + 512],
                updt[:], OP.add),
                reads=[B["updt"]], writes=[B["G"]])
            B["updt"].read(vt)

    def tap(nm, t_, buf):
        if debug and nm in dbg_d:
            dt = DMA("G", lambda e, nm=nm, t_=t_: e.dma_start(
                dbg_d[nm].ap(), t_[:]), reads=[buf])
            buf.read(dt)

    for g in range(NG):
        if g == 0:
            layernorm_group(0, xn_war_tok=P.tok("T"))
            tap("txn", xn, B["xn"])
        proj_group_T(g, B["wqs"], wqs_t, B["QTg"], QTg_t)
        proj_group_T(g, B["wks"], wks_t, B["KTg"], KTg_t)
        if g == 0:
            tap("tQT", QTg_t, B["QTg"])
            tap("tKT", KTg_t, B["KTg"])
        for ci in range(CPG):
            c = g * CPG + ci
            proj_t2n(c, QTg_t, B["QTg"], B["qn"], qn_t, "A")
            proj_t2n(c, KTg_t, B["KTg"], B["kn"], kn_t, "B")
            proj_chunk_nat(c, B["wvs"], wvs_t, B["vn"], vn_t)
            if ci == CPG - 1 and g + 1 < NG:
                ln_stats(g + 1)
            if c == 0:
                tap("tqn", qn_t, B["qn"])
                tap("tkn", kn_t, B["kn"])
                tap("tvn", vn_t, B["vn"])
            if c < nsc:
                scan_chunk(c)
            if ci == CPG - 1 and wo_on:
                for w2 in range(2):
                    for ji in range(4):
                        jd = 4 * w2 + ji
                        bi = ji
                        pt = bank(bi)
                        pb = pB[bi]
                        for jh in range(NJ):
                            lhs = wos_t[:, jh * D + jd * 128:
                                        jh * D + jd * 128 + 128]
                            rhs = OTg_t[:, jh * 512: jh * 512 + 512]
                            T_(lambda e, pt=pt, lhs=lhs, rhs=rhs, jh=jh:
                               e.matmul(pt, lhs, rhs,
                                        start=(jh == 0),
                                        stop=(jh == NJ - 1)),
                               reads=([B["wos"], B["OTg"]] if jh == 0
                                      else []),
                               writes=([pb] if jh == 0 else []),
                               mark=(jh == NJ - 1))
                        pb.wrote(P.tok("T"))
                        xs = xres[:, jd * T + g * GB:
                                  jd * T + g * GB + 512]
                        vt = V_(lambda e, xs=xs, pt=pt: e.tensor_tensor(
                            xs, xs, pt[:], OP.add),
                            reads=[pb], writes=[B["xres"]])
                        pb.read(vt)
                B["OTg"].read(P.tok("T"))
            if c == 0:
                tap("tS1", Sp_t, B["S"])
                tap("tC1", Cp_t, B["C"])
                tap("tG1", Gp_t, B["G"])
            if c == 1:
                tap("tS2", Sp_t, B["S"])
                tap("tC2", Cp_t, B["C"])
                tap("tG2", Gp_t, B["G"])
        if g + 1 < NG:
            ln_apply(g + 1, xn_war_tok=P.tok("T"))

    if debug:
        dt = DMA("G", lambda e: e.dma_start(dbg_d["dbg2"].ap(), xres[:]),
                 reads=[B["xres"]])
        B["xres"].read(dt)

    P.barrier()
    ffn_sublayer(first_call=False)

    for j in range(NJ):
        q = ("Y", "G")[j % 2]
        dt = DMA(q, lambda e, j=j: e.dma_start(
            out_d.ap()[j * 128:(j + 1) * 128, :], xres[:, j * T:(j + 1) * T]),
            reads=[B["xres"]])
        B["xres"].read(dt)

    P.wait("Y", P.qdone("Y"))
    P.wait("Y", P.qdone("G"))
    P.emit("Y", lambda e: e.nop(), mark=False)

    with nc.Block() as blk:
        blk.tensor(lambda e: P.run("T", e))
        blk.vector(lambda e: P.run("V", e))
        blk.scalar(lambda e: P.run("S", e))
        blk.gpsimd(lambda e: P.run("G", e))
        blk.sync(lambda e: P.run("Y", e))

    stack.close()
    return nc


_NC_CACHE = {}


def _get_nc(debug=False):
    key = bool(debug)
    if key not in _NC_CACHE:
        _NC_CACHE[key] = build(debug=debug)
    return _NC_CACHE[key]


def _host_constants():
    triu1 = np.triu(np.ones((128, 128), np.float32)).astype(ml_dtypes.bfloat16)
    striu1 = np.triu(np.ones((128, 128), np.float32), 1).astype(
        ml_dtypes.bfloat16)
    q = np.arange(128) // 64
    bdm1 = (q[:, None] == q[None, :]).astype(np.float32).astype(
        ml_dtypes.bfloat16)
    pm1 = np.zeros((128, 256), np.float32)
    pm1[0:64, 0:128] = 1.0
    pm1[64:128, 128:256] = 1.0
    pm1 = pm1.astype(ml_dtypes.bfloat16)
    ident = np.eye(128, dtype=np.float32)
    return triu1, striu1, bdm1, pm1, ident


def _in_maps(L0_attn_x, L0_attn_W_q, L0_attn_W_k, L0_attn_W_v, L0_attn_W_o,
             L0_ffn_w1, L0_ffn_b1, L0_ffn_w2, L0_ffn_b2):
    triu1, striu1, bdm1, pm1, ident = _host_constants()
    common = {
        "wq": np.ascontiguousarray(L0_attn_W_q, np.float32),
        "wk": np.ascontiguousarray(L0_attn_W_k, np.float32),
        "wv": np.ascontiguousarray(L0_attn_W_v, np.float32),
        "wo": np.ascontiguousarray(L0_attn_W_o, np.float32),
        "w1": np.ascontiguousarray(L0_ffn_w1, np.float32),
        "b1": np.ascontiguousarray(L0_ffn_b1, np.float32),
        "w2": np.ascontiguousarray(L0_ffn_w2, np.float32),
        "b2": np.ascontiguousarray(L0_ffn_b2, np.float32),
        "triu1": triu1, "striu1": striu1, "bdm1": bdm1, "pm1": pm1,
        "ident": ident,
    }
    x = np.ascontiguousarray(L0_attn_x, np.float32)
    return [dict(common, x=np.ascontiguousarray(x[i].T)) for i in range(8)]


def kernel(L0_attn_x, L0_attn_W_q, L0_attn_W_k, L0_attn_W_v, L0_attn_W_o,
           L0_ffn_w1, L0_ffn_b1, L0_ffn_w2, L0_ffn_b2):
    nc = _get_nc(debug=False)
    in_maps = _in_maps(L0_attn_x, L0_attn_W_q, L0_attn_W_k, L0_attn_W_v,
                       L0_attn_W_o, L0_ffn_w1, L0_ffn_b1, L0_ffn_w2,
                       L0_ffn_b2)
    res = run_bass_kernel_spmd(nc, in_maps, core_ids=list(range(8)))
    out = np.stack([np.asarray(r["out"]).T for r in res.results], axis=0)
    return np.ascontiguousarray(out, np.float32)


# version 16
# speedup vs baseline: 1.0159x; 1.0027x over previous
import numpy as np
import ml_dtypes
import concourse.bass as bass
import concourse.mybir as mybir
from concourse.bass_utils import run_bass_kernel_spmd

F32 = mybir.dt.float32
BF16 = mybir.dt.bfloat16
AF = mybir.ActivationFunctionType
OP = mybir.AluOpType
AX = mybir.AxisListType

D = 1024
T = 2048
H = 16
DK = 64
FF = 4096
NJ = D // 128
NF = FF // 128
NT = T // 128
GB = 512
NG = T // GB
CH = 128
NC_ = T // CH
CPG = GB // CH

ENGS = ("T", "V", "S", "G", "Y")


class Prog:
    NLANE = 6

    def __init__(self, nc):
        self.nc = nc
        self.prog = {e: [] for e in ENGS}
        self.count = {e: 0 for e in ENGS}
        self.hwm = {}
        self._pend = {e: [] for e in ENGS}
        self.sems = {}
        self.dsems = {}

    def make_sems(self, stack):
        for e in ENGS:
            self.sems[e] = stack.enter_context(self.nc.semaphore(f"ms{e}"))
        for e in ("Y", "G", "S", "V"):
            self.dsems[e] = [stack.enter_context(self.nc.semaphore(f"dm{e}{i}"))
                             for i in range(self.NLANE)]
        self.lane_cnt = {e: [0] * self.NLANE for e in ("Y", "G", "S", "V")}
        self.lane_idx = {e: 0 for e in ("Y", "G", "S", "V")}

    def wait(self, eng, tok):
        if tok is None:
            return
        if isinstance(tok, list):
            for t in tok:
                self.wait(eng, t)
            return
        prod, kind, val = tok
        if kind == "c":
            if prod == eng or val <= 0:
                return
            key = (eng, prod, "c")
            if self.hwm.get(key, 0) >= val:
                return
            self.hwm[key] = val
            self._pend[eng].append((prod, "c", val))
        else:
            lane, v = val
            if v <= 0:
                return
            key = (eng, prod, "d", lane)
            if self.hwm.get(key, 0) >= v:
                return
            self.hwm[key] = v
            self._pend[eng].append((prod, ("d", lane), v))

    def emit(self, eng, fn, mark=False, dma=False):
        tok = None
        if dma:
            lane = self.lane_idx[eng]
            self.lane_idx[eng] = (lane + 1) % self.NLANE
            prev = self.lane_cnt[eng][lane]
            if prev > 0:
                self.wait(eng, (eng, "d", (lane, prev)))
            self.lane_cnt[eng][lane] = prev + 16
            waits, self._pend[eng] = self._pend[eng], []
            incs = [(eng, ("d", lane), 16)]
            tok = (eng, "d", (lane, prev + 16))
            self.prog[eng].append((waits, fn, incs))
            return tok
        waits, self._pend[eng] = self._pend[eng], []
        incs = []
        if mark:
            self.count[eng] += 1
            incs.append((eng, "c", 1))
            tok = (eng, "c", self.count[eng])
        self.prog[eng].append((waits, fn, incs))
        return tok

    def tok(self, eng):
        return (eng, "c", self.count[eng])

    def qdone(self, q):
        return [(q, "d", (ln, c)) for ln, c in enumerate(self.lane_cnt[q])
                if c > 0]

    def barrier(self):
        snap_c = dict(self.count)
        for e in ENGS:
            for o in ENGS:
                if o != e and snap_c[o] > 0:
                    self.wait(e, (o, "c", snap_c[o]))
                if o in ("Y", "G", "S", "V"):
                    self.wait(e, self.qdone(o))

    def run(self, eng_name, engine):
        for waits, fn, incs in self.prog[eng_name]:
            for (prod, kind, val) in waits:
                if kind == "c":
                    engine.wait_ge(self.sems[prod], val)
                else:
                    engine.wait_ge(self.dsems[prod][kind[1]], val)
            instr = fn(engine)
            for (prod, kind, amt) in incs:
                if kind == "c":
                    instr.then_inc(self.sems[prod], amt)
                else:
                    instr.then_inc(self.dsems[prod][kind[1]], amt)


class Buf:
    def __init__(self, name=""):
        self.name = name
        self.w = None
        self.r = {}

    def wrote(self, tok):
        if tok is not None:
            self.w = tok
            self.r = {}

    def read(self, tok):
        if tok is not None:
            self.r[tok[0]] = tok

    def war(self):
        toks = list(self.r.values())
        if self.w is not None:
            toks.append(self.w)
        return toks


def ts(i, n=128):
    return slice(i * n, (i + 1) * n)


def build(debug=False, nsc=99, wo_on=True, sgate=(1 << 20) - 1):
    EV_S = bool(sgate & 2048)
    EV_V = bool(sgate & 4096)
    import contextlib
    nc = bass.Bass("TRN2", target_bir_lowering=False,
                   detect_race_conditions=False)
    stack = contextlib.ExitStack()
    P = Prog(nc)
    P.make_sems(stack)

    x_d = nc.dram_tensor("x", [D, T], F32, kind="ExternalInput")
    wq_d = nc.dram_tensor("wq", [D, D], F32, kind="ExternalInput")
    wk_d = nc.dram_tensor("wk", [D, D], F32, kind="ExternalInput")
    wv_d = nc.dram_tensor("wv", [D, D], F32, kind="ExternalInput")
    wo_d = nc.dram_tensor("wo", [D, D], F32, kind="ExternalInput")
    w1_d = nc.dram_tensor("w1", [FF, D], F32, kind="ExternalInput")
    b1_d = nc.dram_tensor("b1", [FF], F32, kind="ExternalInput")
    w2_d = nc.dram_tensor("w2", [D, FF], F32, kind="ExternalInput")
    b2_d = nc.dram_tensor("b2", [D], F32, kind="ExternalInput")
    triu_d = nc.dram_tensor("triu1", [128, 128], BF16, kind="ExternalInput")
    striu_d = nc.dram_tensor("striu1", [128, 128], BF16, kind="ExternalInput")
    bdm_d = nc.dram_tensor("bdm1", [128, 128], BF16, kind="ExternalInput")
    pm_d = nc.dram_tensor("pm1", [128, 256], BF16, kind="ExternalInput")
    ident_d = nc.dram_tensor("ident", [128, 128], F32, kind="ExternalInput")
    out_d = nc.dram_tensor("out", [D, T], F32, kind="ExternalOutput")
    s1_d = nc.dram_tensor("s1scr", [FF, D], BF16, kind="Internal")
    s2_d = nc.dram_tensor("s2scr", [D, FF], BF16, kind="Internal")
    w2t_d = nc.dram_tensor("w2tscr", [FF, D], BF16, kind="Internal")
    w1t_d = nc.dram_tensor("w1tscr", [128, NJ * FF], BF16, kind="Internal")
    dbg_d = {}
    if debug:
        dbg_d["dbg1"] = nc.dram_tensor("dbg1", [128, NJ * T], F32,
                                       kind="ExternalOutput")
        dbg_d["dbg2"] = nc.dram_tensor("dbg2", [128, NJ * T], F32,
                                       kind="ExternalOutput")
        for nm, wid, dt in (("txn", NJ * GB, BF16), ("tQT", 8 * GB, BF16),
                            ("tKT", 8 * GB, BF16), ("tqn", D, BF16),
                            ("tkn", D, BF16), ("tvn", D, BF16),
                            ("tUP", GB, F32),
                            ("tOTg", 4096, BF16), ("tS1", 1024, F32),
                            ("tC1", 1024, F32), ("tG1", 1024, F32),
                            ("tS2", 1024, F32), ("tC2", 1024, F32),
                            ("tG2", 1024, F32),
                            ("tw1T", NJ * FF, BF16),
                            ("tghT", NF * GB, BF16),
                            ("tF0", GB, F32), ("tF7", GB, F32),
                            ("tW20", 2048, BF16), ("tW27", 2048, BF16),
                            ("tA2h", NJ, F32), ("tA2t", 32, F32)):
            dbg_d[nm] = nc.dram_tensor(nm, [128, wid], dt,
                                       kind="ExternalOutput")

    xres = nc.alloc_sbuf_tensor("xres", [128, NJ * T], F32)
    xn = nc.alloc_sbuf_tensor("xn", [128, NJ * GB], BF16)
    triu = nc.alloc_sbuf_tensor("triu_s", [128, 128], BF16)
    striu = nc.alloc_sbuf_tensor("striu_s", [128, 128], BF16)
    bdm = nc.alloc_sbuf_tensor("bdm_s", [128, 128], BF16)
    pm = nc.alloc_sbuf_tensor("pm_s", [128, 256], BF16)
    ident = nc.alloc_sbuf_tensor("idents", [128, 128], F32)
    identb = nc.alloc_sbuf_tensor("identb", [128, 128], BF16)
    ones_col = nc.alloc_sbuf_tensor("ones_col", [128, 1], F32)
    ones_row = nc.alloc_sbuf_tensor("ones_row", [1, 128], F32)
    alpha1 = nc.alloc_sbuf_tensor("alpha1", [128, NF], F32)
    b1s = nc.alloc_sbuf_tensor("b1s", [128, NF], F32)
    a2h = nc.alloc_sbuf_tensor("a2h", [128, NJ], F32)
    b2h = nc.alloc_sbuf_tensor("b2h", [128, NJ], F32)
    a2t = nc.alloc_sbuf_tensor("a2t", [128, 32], F32)
    inv_c = nc.alloc_sbuf_tensor("inv_c", [128, 1], F32)
    xsq = nc.alloc_sbuf_tensor("xsq", [128, GB], F32)
    lnt = nc.alloc_sbuf_tensor("lnt", [128, GB], F32)
    updt = nc.alloc_sbuf_tensor("updt", [128, GB], F32)
    m_row = lnt[0:1, :]
    v_row = updt[0:1, :]
    r_row = xsq[0:1, :]
    cst = nc.alloc_sbuf_tensor("cst", [128, 4], F32)

    SCR0 = (nc._sbuf_addr_for_side("left") + 31) // 32 * 32
    SCRTOP = nc._sbuf_addr_for_side("right")
    AVAIL = SCRTOP - SCR0

    def at(name, shape, dtype, off):
        return nc.alloc_sbuf_tensor_at(name, shape, dtype, offset=SCR0 + off)

    w1T_t = at("w1T", [128, NJ * FF], BF16, 0)
    ghT_t = at("ghT", [128, NF * GB], BF16, 65536)
    w2s_t = at("w2s", [128, 2 * 2048], BF16, 98304)
    s1stage = [at(f"s1stage{i}", [128, 1024], F32, 98304 + i * 4096)
               for i in range(2)]
    w1stage = [at(f"w1stage{i}", [128, 1024], F32, 106496 + i * 4096)
               for i in range(2)]
    w2stage = [at(f"w2stage{i}", [128, 1024], F32, 114688 + i * 4096)
               for i in range(2)]
    s2stage = [at(f"s2stage{i}", [128, 1024], BF16, 122880 + i * 2048)
               for i in range(2)]
    bstage = at("bstage", [32, 128], F32, 126976)
    o = 0
    wqs_t = at("wqs", [128, NJ * D], BF16, o); o += 16384
    wks_t = at("wks", [128, NJ * D], BF16, o); o += 16384
    wvs_t = at("wvs", [128, NJ * D], BF16, o); o += 16384
    wos_t = at("wos", [128, NJ * D], BF16, o); o += 16384
    QTg_t = at("QTg", [128, 8 * GB], BF16, o); o += 8192
    KTg_t = at("KTg", [128, 8 * GB], BF16, o); o += 8192
    OTg_t = at("OTg", [128, 8 * 512], BF16, o); o += 8192
    QTz_t = at("QTz", [128, 8 * 128], BF16, o); o += 2048
    KTz_t = at("KTz", [128, 8 * 128], BF16, o); o += 2048
    qn_t = at("qn", [128, D], BF16, o); o += 2048
    kn_t = at("kn", [128, D], BF16, o); o += 2048
    vn_t = at("vn", [128, D], BF16, o); o += 2048
    Sp_t = at("Sp", [128, 8 * 128], F32, o); o += 4096
    Cp_t = at("Cp", [128, 8 * 128], F32, o); o += 4096
    Gp_t = at("Gp", [128, 8 * 128], F32, o); o += 4096
    S0bd_t = at("S0bd", [128, 8 * 128], BF16, o); o += 2048
    C0bd_t = at("C0bd", [128, 8 * 128], BF16, o); o += 2048
    Atb_t = at("Atb", [128, 8 * 128], BF16, o); o += 2048
    Atr_t = at("Atr", [128, 8 * 128], BF16, o); o += 2048
    Aqs_t = at("Aqs", [128, 8 * 128], BF16, o); o += 2048
    MtS_t = at("MtS", [128, 8 * 128], BF16, o); o += 2048
    Ybf_t = at("Ybf", [128, 4 * 128], BF16, o); o += 1024
    W0b_t = at("W0b", [128, 4 * 128], BF16, o); o += 1024
    KKbd_t = at("KKbd", [128, 4 * 128], BF16, o); o += 1024
    Zb_t = at("Zb", [128, 8 * 64], BF16, o); o += 1024
    assert o <= AVAIL, f"attn scratch {o} > {AVAIL}"

    psA = stack.enter_context(nc.psum_tensor("psA", [128, 1536], F32))
    psB = stack.enter_context(nc.psum_tensor("psB", [128, 1536], F32))
    pL = [stack.enter_context(nc.psum_tensor(f"pL{i}", [128, 512], F32))
          for i in range(2)]

    def bank(i):
        if i < 3:
            return psA[:, i * 512:(i + 1) * 512]
        if i < 6:
            return psB[:, (i - 3) * 512:(i - 3) * 512 + 512]
        return pL[i - 6][:]

    pB = [Buf(f"bank{i}") for i in range(8)]

    B = {}
    for name in ("xres", "xn", "xsq", "lnt", "m_row", "v_row", "r_row",
                 "updt", "alpha1", "b1s", "a2h", "b2h", "a2t", "consts",
                 "masks", "w1T", "ghT", "S", "C", "G", "S0bd", "C0bd",
                 "Atb", "Atr", "Aqs", "MtS", "Ybf", "W0b", "KKbd", "Zb",
                 "QTz", "KTz", "Aqb",
                 "QTg", "KTg", "OTg", "qn", "kn", "vn",
                 "wqs", "wks", "wvs", "wos", "w2t"):
        B[name] = Buf(name)
    w2s = [Buf("w2s0"), Buf("w2s1")]
    xst = [Buf("xst0"), Buf("xst1")]
    ost = [Buf("ost0"), Buf("ost1")]
    w1st = [Buf("w1st0"), Buf("w1st1")]
    s1st = [Buf("s1st0"), Buf("s1st1")]
    w2st = [Buf("w2st0"), Buf("w2st1")]
    s2st = [Buf("s2st0"), Buf("s2st1")]
    bst = Buf("bst")

    def _do(eng, fn, reads, writes, mark, dma=False):
        for r in reads:
            if isinstance(r, Buf):
                P.wait(eng, r.w)
            elif r is not None:
                P.wait(eng, r)
        for w in writes:
            for t in w.war():
                P.wait(eng, t)
        tok = P.emit(eng, fn, mark=mark, dma=dma)
        eff = tok if tok is not None else P.tok(eng)
        for r in reads:
            if isinstance(r, Buf):
                r.read(eff)
        for w in writes:
            w.wrote(tok)
        return tok

    def T_(fn, reads=(), writes=(), mark=True):
        return _do("T", fn, reads, writes, mark)

    def V_(fn, reads=(), writes=(), mark=True):
        return _do("V", fn, reads, writes, mark)

    def S_(fn, reads=(), writes=(), mark=True):
        return _do("S", fn, reads, writes, mark)

    def G_(fn, reads=(), writes=(), mark=True):
        return _do("G", fn, reads, writes, mark)

    def SE_(fn, reads=(), writes=(), mark=True):
        if not EV_S:
            return None
        return _do("S", fn, reads, writes, mark)

    def VE_(fn, reads=(), writes=(), mark=True):
        if not EV_V:
            return None
        return _do("V", fn, reads, writes, mark)

    def DMA(q, fn, reads=(), writes=()):
        return _do(q, fn, reads, writes, mark=False, dma=True)

    xres_v = xres.rearrange("p (j t) -> p j t", t=T)
    w2t_v = w2t_d.ap().rearrange("(c p) d -> p c d", p=128)

    def mrep(mask_t, n):
        return bass.AP(mask_t, 0, [[128, 128], [0, n], [1, 128]])

    def mrep2(mask_t, n1, n2):
        return bass.AP(mask_t, 0, [[128, 128], [0, n1], [0, n2], [1, 128]])

    V_(lambda e: e.memset(ones_col[:], 1.0), writes=[B["consts"]])
    V_(lambda e: e.memset(ones_row[:], 1.0), writes=[B["consts"]])
    V_(lambda e: e.memset(inv_c[:], 1.0 / 1024.0), writes=[B["consts"]])
    V_(lambda e: e.memset(cst[:, 0:1], 1e-5), writes=[B["consts"]])
    V_(lambda e: e.memset(cst[:, 1:2], 0.5), writes=[B["consts"]])
    V_(lambda e: e.memset(cst[:, 2:3], 0.5 / 4096.0), writes=[B["consts"]])
    V_(lambda e: e.memset(cst[:, 3:4], 0.0), writes=[B["consts"]])
    DMA("Y", lambda e: e.dma_start(triu[:], triu_d.ap()))
    DMA("Y", lambda e: e.dma_start(striu[:], striu_d.ap()))
    DMA("Y", lambda e: e.dma_start(bdm[:], bdm_d.ap()))
    DMA("Y", lambda e: e.dma_start(pm[:], pm_d.ap()))
    tid = DMA("Y", lambda e: e.dma_start(ident[:], ident_d.ap()))
    V_(lambda e: e.tensor_copy(identb[:], ident[:]),
       reads=[tid], writes=[B["masks"]])
    B["masks"].wrote(tid)

    xld_toks = []
    xg_toks = [[] for _ in range(NG)]
    for j in range(NJ):
        t_ = DMA("Y", lambda e, j=j: e.dma_start(
            xres[:, j * T: j * T + GB],
            x_d.ap()[j * 128:(j + 1) * 128, 0:GB]),
            writes=[B["xres"]] if j == 0 else [])
        xld_toks.append(t_)
        xg_toks[0].append(t_)

    def emit_x_rest():
        for g_ in range(1, NG):
            for j in range(NJ):
                t_ = DMA(("Y", "S")[j % 2], lambda e, j=j, g_=g_: e.dma_start(
                    xres[:, j * T + g_ * GB: j * T + (g_ + 1) * GB],
                    x_d.ap()[j * 128:(j + 1) * 128,
                             g_ * GB:(g_ + 1) * GB]), writes=[])
                xld_toks.append(t_)
                xg_toks[g_].append(t_)
        B["xres"].wrote(xld_toks)
    B["xres"].wrote(list(xld_toks))

    ld = DMA("Y", lambda e: e.dma_start(
        bstage[:], b1_d.ap().rearrange("(a b) -> a b", b=128)), writes=[bst])
    T_(lambda e: e.transpose(bank(4)[:, 0:32], bstage[:], ident[0:32, 0:32]),
       reads=[bst, ld, tid], writes=[pB[4]])
    ct = S_(lambda e: e.copy(b1s[:], bank(4)[:, 0:32]),
            reads=[pB[4]], writes=[B["b1s"]])
    pB[4].read(ct)
    bst.read(ct)
    ld = DMA("Y", lambda e: e.dma_start(
        bstage[0:8, :], b2_d.ap().rearrange("(a b) -> a b", b=128)),
        writes=[bst])
    T_(lambda e: e.transpose(bank(5)[:, 0:8], bstage[0:8, :], ident[0:8, 0:8]),
       reads=[bst, ld], writes=[pB[5]])
    ct = S_(lambda e: e.activation(b2h[:], bank(5)[:, 0:8], AF.Copy,
                                   scale=cst[:, 1:2]),
            reads=[pB[5], B["consts"]], writes=[B["b2h"]])
    pB[5].read(ct)
    bst.read(ct)


    def stage_w2_chunk(cc2):
        j, q4 = cc2 // 4, cc2 % 4
        b = cc2 % 2
        w2g, s2g = w2stage[b], s2stage[b]
        wb, sb_ = w2st[b], s2st[b]
        ld = DMA("S", lambda e, j=j, q4=q4, w2g=w2g: e.dma_start(
            w2g[:], w2_d.ap()[ts(j), q4 * 1024:(q4 + 1) * 1024]),
            writes=[wb])
        rt = V_(lambda e, j=j, q4=q4, w2g=w2g: e.reduce_sum(
            a2t[:, j * 4 + q4:j * 4 + q4 + 1], w2g[:], axis=AX.X,
            apply_absolute_value=True),
            reads=[wb, ld], writes=[B["a2t"]])
        st = S_(lambda e, w2g=w2g, s2g=s2g: e.activation(
            s2g[:], w2g[:], AF.Sign),
            reads=[wb, ld], writes=[sb_])
        dt = DMA("S", lambda e, j=j, q4=q4, s2g=s2g: e.dma_start(
            s2_d.ap()[ts(j), q4 * 1024:(q4 + 1) * 1024], s2g[:]),
            reads=[sb_])
        sb_.read(dt)
        wb.read(rt)
        wb.read(st)

    def emit_a2h_tail():
        for j in range(NJ):
            V_(lambda e, j=j: e.tensor_tensor(
                a2t[:, j * 4:j * 4 + 1], a2t[:, j * 4:j * 4 + 1],
                a2t[:, j * 4 + 1:j * 4 + 2], OP.add),
               reads=[B["a2t"]], writes=[B["a2t"]])
        for j in range(NJ):
            V_(lambda e, j=j: e.tensor_tensor(
                a2t[:, j * 4 + 2:j * 4 + 3], a2t[:, j * 4 + 2:j * 4 + 3],
                a2t[:, j * 4 + 3:j * 4 + 4], OP.add),
               reads=[B["a2t"]], writes=[B["a2t"]])
        for j in range(NJ):
            V_(lambda e, j=j: e.tensor_tensor(
                a2h[:, j:j + 1], a2t[:, j * 4:j * 4 + 1],
                a2t[:, j * 4 + 2:j * 4 + 3], OP.add),
               reads=[B["a2t"]], writes=[])
        B["a2h"].wrote(P.tok("V"))
        S_(lambda e: e.activation(a2h[:], a2h[:], AF.Identity,
                                  scale=cst[:, 2:3], bias=cst[:, 3:4]),
           reads=[B["a2h"], B["consts"]], writes=[B["a2h"]])
        if debug and "tA2h" in dbg_d:
            dta = DMA("G", lambda e: e.dma_start(dbg_d["tA2h"].ap(), a2h[:]),
                      reads=[B["a2h"]])
            B["a2h"].read(dta)
            dtb = DMA("G", lambda e: e.dma_start(dbg_d["tA2t"].ap(), a2t[:]),
                      reads=[B["a2t"]])
            B["a2t"].read(dtb)

    def ln_stats(g):
        col0 = g * GB
        p6, p7 = bank(6), bank(7)
        for j in range(NJ):
            xs = xres[:, j * T + col0: j * T + col0 + GB]
            S_(lambda e, xs=xs: e.square(xsq[:], xs),
               reads=[xg_toks[g][j]], writes=[B["xsq"]])
            T_(lambda e, xs=xs, j=j: e.matmul(
                p6[0:1, :], ones_col[:], xs,
                start=(j == 0), stop=(j == NJ - 1)),
               reads=[B["consts"], xg_toks[g][j]],
               writes=[pB[6]] if j == 0 else [], mark=(j == NJ - 1))
            tq = T_(lambda e, j=j: e.matmul(
                p7[0:1, :], ones_col[:], xsq[:],
                start=(j == 0), stop=(j == NJ - 1)),
                reads=[B["xsq"], B["consts"]],
                writes=[pB[7]] if j == 0 else [], mark=True)
            B["xsq"].read(tq)
        B["xres"].read(P.tok("S"))
        B["xres"].read(P.tok("T"))
        pB[6].wrote(P.tok("T"))
        pB[7].wrote(P.tok("T"))
        mt = V_(lambda e: e.tensor_scalar_mul(
            m_row, p6[0:1, :], inv_c[0:1, 0:1]),
            reads=[pB[6], B["consts"]], writes=[B["m_row"]])
        pB[6].read(mt)
        vt = V_(lambda e: e.tensor_scalar_mul(
            v_row, p7[0:1, :], inv_c[0:1, 0:1]),
            reads=[pB[7], B["consts"]], writes=[B["v_row"]])
        pB[7].read(vt)
        V_(lambda e: e.scalar_tensor_tensor(
            r_row, m_row, -1.0, m_row, OP.mult, OP.mult),
           reads=[B["m_row"]], writes=[B["r_row"]])
        V_(lambda e: e.tensor_tensor(v_row, v_row, r_row, OP.add),
           reads=[B["r_row"], B["v_row"]], writes=[B["v_row"]])
        S_(lambda e: e.activation(v_row, v_row, AF.Sqrt,
                                  bias=cst[0:1, 0:1]),
           reads=[B["v_row"], B["consts"]], writes=[B["v_row"]])
        rc = V_(lambda e: e.reciprocal(r_row, v_row),
                reads=[B["v_row"]], writes=[B["r_row"]])
        B["updt"].read(rc)

    xnj_toks = [None] * NJ

    def ln_apply(g, xn_war_tok=None):
        col0 = g * GB
        p6, p7 = bank(6), bank(7)
        tm = T_(lambda e: e.matmul(p6[:], ones_row[:], m_row,
                                   start=True, stop=True),
                reads=[B["m_row"], B["consts"]], writes=[pB[6]])
        B["lnt"].read(tm)
        tr = T_(lambda e: e.matmul(p7[:], ones_row[:], r_row,
                                   start=True, stop=True),
                reads=[B["r_row"], B["consts"]], writes=[pB[7]])
        B["xsq"].read(tr)
        for j in range(NJ):
            xs = xres[:, j * T + col0: j * T + col0 + GB]
            V_(lambda e, xs=xs: e.tensor_tensor(lnt[:], xs, p6[:],
                                                OP.subtract),
               reads=[xg_toks[g][j], pB[6]], writes=[B["lnt"]])
            ex = [xn_war_tok] if (j == 0 and xn_war_tok is not None) else []
            vt = V_(lambda e, j=j: e.tensor_tensor(
                xn[:, j * GB:(j + 1) * GB], lnt[:], p7[:], OP.mult),
                reads=[B["lnt"], pB[7]] + ex,
                writes=[B["xn"]] if j == 0 else [])
            xnj_toks[j] = vt
            pB[6].read(vt)
            pB[7].read(vt)
        B["xn"].wrote(P.tok("V"))
        B["xres"].read(P.tok("V"))

    def layernorm_group(g, xn_war_tok=None):
        ln_stats(g)
        ln_apply(g, xn_war_tok=xn_war_tok)

    def ffn1_chunk(c, w1gate=None):
        bi = c % 2
        pt = bank(bi)
        pb = pB[bi]
        for j in range(NJ):
            lhs = w1T_t[:, j * FF + c * 128: j * FF + c * 128 + 128]
            rhs = xn[:, j * GB:(j + 1) * GB]
            T_(lambda e, pt=pt, lhs=lhs, rhs=rhs, j=j: e.matmul(
                pt, lhs, rhs, start=(j == 0), stop=(j == NJ - 1)),
               reads=[xnj_toks[j]]
               + ([w1gate if w1gate is not None else B["w1T"]]
                  if j == 0 else []),
               writes=([pb] if j == 0 else []), mark=(j == NJ - 1))
        pb.wrote(P.tok("T"))
        st = S_(lambda e, pt=pt, c=c: e.activation(
            ghT_t[:, c * GB:(c + 1) * GB], pt, AF.Gelu,
            scale=alpha1[:, c:c + 1], bias=b1s[:, c:c + 1]),
            reads=[pb, B["alpha1"], B["b1s"]],
            writes=[B["ghT"]] if c == 0 else [])
        pb.read(st)

    def ffn_sublayer(first_call, skip_g0_head=False):
        if not first_call:
            rl = []
            for hh in range(2):
                rl.append(DMA(("Y", "G")[hh], lambda e, hh=hh: e.dma_start(
                    w1T_t[:, hh * 4 * FF:(hh + 1) * 4 * FF],
                    w1t_d.ap()[:, hh * 4 * FF:(hh + 1) * 4 * FF]),
                    reads=[w1t_stok] if hh == 0 else [],
                    writes=[B["w1T"]] if hh == 0 else []))
            B["w1T"].wrote(rl)
        if debug and first_call and "tw1T" in dbg_d:
            dtw = DMA("G", lambda e: e.dma_start(dbg_d["tw1T"].ap(), w1T_t[:]),
                      reads=[B["w1T"]])
            B["w1T"].read(dtw)

        if not skip_g0_head:
            layernorm_group(0, xn_war_tok=P.tok("T"))
        for g in range(NG):
            if not (skip_g0_head and g == 0):
                for c in range(NF):
                    ffn1_chunk(c)
                B["ghT"].wrote(P.tok("S"))
            if debug and first_call and g == 0 and "tghT" in dbg_d:
                dtg = DMA("G", lambda e: e.dma_start(dbg_d["tghT"].ap(),
                                                     ghT_t[:]),
                          reads=[B["ghT"]])
                B["ghT"].read(dtg)

            if g + 1 < NG:
                ln_stats(g + 1)

            for w in range(2):
                for c in range(NF):
                    slot = (c // 4) % 2
                    ci = c % 4
                    sb = w2s[slot]
                    if ci == 0:
                        if first_call and g == 0:
                            q = "Y" if slot == 0 else "S"
                            tl_toks = []
                            for cc in range(4):
                                src = s2_d.ap()[w * 512:(w + 1) * 512,
                                                (c + cc) * 128:(c + cc + 1) * 128]
                                d1 = w2s_t[:, slot * 2048 + cc * 512:
                                           slot * 2048 + cc * 512 + 512]
                                tl_toks.append(DMA(
                                    q, lambda e, d1=d1, src=src:
                                    e.dma_start_transpose(d1, src),
                                    reads=([s2_done, B["a2h"], B["w1T"]]
                                           if cc == 0 else []),
                                    writes=[sb]))
                            sb.wrote(tl_toks)
                            if debug and ((w == 0 and c == 0) or
                                          (w == 1 and c == 28)) and \
                                    "tW20" in dbg_d:
                                nmw = "tW20" if w == 0 else "tW27"
                                dtw2 = DMA("S", lambda e, nmw=nmw, slot=slot:
                                           e.dma_start(
                                               dbg_d[nmw].ap(),
                                               w2s_t[:, slot * 2048:
                                                     slot * 2048 + 2048]),
                                           reads=[sb])
                                sb.read(dtw2)
                            wdst = w2t_v[:, c:c + 4, w * 512:(w + 1) * 512]
                            wsrc = w2s_t.rearrange(
                                "p (s c dd) -> p s c dd", s=2, dd=512)[
                                :, slot, :, :]
                            dt = DMA("G", lambda e, wdst=wdst, wsrc=wsrc:
                                     e.dma_start(wdst, wsrc), reads=[sb])
                            sb.read(dt)
                        else:
                            q = "Y" if slot == 0 else "S"
                            src = w2t_v[:, c:c + 4, w * 512:(w + 1) * 512]
                            dst = w2s_t.rearrange(
                                "p (s c dd) -> p s c dd", s=2, dd=512)[
                                :, slot, :, :]
                            DMA(q, lambda e, dst=dst, src=src:
                                e.dma_start(dst, src),
                                reads=[B["w2t"]], writes=[sb])
                    for ji in range(4):
                        j = 4 * w + ji
                        bi = 2 + ji
                        pt = bank(bi)
                        pb = pB[bi]
                        lhs = w2s_t[:, slot * 2048 + ci * 512 + ji * 128:
                                    slot * 2048 + ci * 512 + ji * 128 + 128]
                        rhs = ghT_t[:, c * GB:(c + 1) * GB]
                        T_(lambda e, pt=pt, lhs=lhs, rhs=rhs, c=c: e.matmul(
                            pt, lhs, rhs, start=(c == 0), stop=(c == NF - 1)),
                           reads=([sb, B["ghT"]] if ji == 0 else [sb]),
                           writes=([pb] if c == 0 else []),
                           mark=(ji == 3))
                ytok = P.tok("T")
                B["ghT"].read(ytok)
                for ji in range(4):
                    pB[2 + ji].wrote(ytok)
                for ji in range(4):
                    j = 4 * w + ji
                    pt = bank(2 + ji)
                    pb = pB[2 + ji]
                    st = S_(lambda e, pt=pt, j=j: e.activation(
                        updt[:], pt, AF.Identity,
                        scale=a2h[:, j:j + 1], bias=b2h[:, j:j + 1]),
                        reads=[pb, B["a2h"], B["b2h"]], writes=[B["updt"]])
                    pb.read(st)
                    if debug and first_call and g == 0 and \
                            ((w == 0 and ji == 0) or (w == 1 and ji == 3)) \
                            and "tF0" in dbg_d:
                        nmf = "tF0" if w == 0 else "tF7"
                        dtf = DMA("G", lambda e, nmf=nmf: e.dma_start(
                            dbg_d[nmf].ap(), updt[:]), reads=[B["updt"]])
                        B["updt"].read(dtf)
                    xs = xres[:, j * T + g * GB: j * T + g * GB + GB]
                    vt = V_(lambda e, xs=xs: e.tensor_tensor(
                        xs, xs, updt[:], OP.add),
                        reads=[B["updt"]], writes=[B["xres"]])
                    B["updt"].read(vt)
                if w == 0 and g + 1 < NG:
                    ln_apply(g + 1, xn_war_tok=P.tok("T"))
            if first_call and g == 0:
                B["w2t"].wrote(P.qdone("G"))

    layernorm_group(0, xn_war_tok=P.tok("T"))
    w1ctoks = []
    for c in range(NF + 1):
        if c < NF:
            b = c % 2
            w1g, s1g = w1stage[b], s1stage[b]
            wb, sb_ = w1st[b], s1st[b]
            ld = DMA("Y", lambda e, c=c, w1g=w1g: e.dma_start(
                w1g[:], w1_d.ap()[ts(c), :]), writes=[wb])
            rt = V_(lambda e, c=c, w1g=w1g: e.reduce_sum(
                alpha1[:, c:c + 1], w1g[:], axis=AX.X,
                apply_absolute_value=True),
                reads=[wb, ld], writes=[])
            S_(lambda e, c=c: e.activation(
                alpha1[:, c:c + 1], alpha1[:, c:c + 1], AF.Identity,
                scale=inv_c[:, 0:1], bias=cst[:, 3:4]),
               reads=[rt, B["consts"]], writes=[])
            st = S_(lambda e, w1g=w1g, s1g=s1g: e.activation(
                s1g[:], w1g[:], AF.Sign),
                reads=[wb, ld], writes=[sb_])
            wb.read(rt)
            wb.read(st)
            for j in range(NJ):
                half = j // 4
                pt2 = (psA if half == 0 else psB)[:, 1024 + (j % 4) * 128:
                                                  1024 + (j % 4) * 128 + 128]
                pbh = pB[2] if half == 0 else pB[5]
                T_(lambda e, pt2=pt2, s1g=s1g, j=j: e.transpose(
                    pt2, s1g[:, ts(j)], ident[:]),
                   reads=([sb_, tid] if j == 0 else []),
                   writes=([pbh] if j % 4 == 0 else []),
                   mark=(j % 4 == 3))
            ttok = P.tok("T")
            pB[2].wrote(ttok)
            pB[5].wrote(ttok)
            sb_.read(ttok)
            ctl = None
            for half, pst in ((0, psA), (1, psB)):
                w1dst = bass.AP(w1T_t, half * 4 * FF + c * 128,
                                [[NJ * FF, 128], [FF, 4], [1, 128]])
                psrc_ = bass.AP(pst.ap().tensor, pst.ap().offset + 1024,
                                [[1536, 128], [128, 4], [1, 128]])
                pbh = pB[2] if half == 0 else pB[5]
                ctl = S_(lambda e, w1dst=w1dst, psrc_=psrc_: e.copy(
                    w1dst, psrc_), reads=[pbh], writes=[B["w1T"]])
                pbh.read(ctl)
            w1ctoks.append(P.tok("S"))
            stage_w2_chunk(c)
        if c >= 1:
            ffn1_chunk(c - 1, w1gate=w1ctoks[c - 1])
    B["ghT"].wrote(P.tok("S"))
    B["alpha1"].wrote(P.tok("S"))
    s2_done = P.qdone("S")
    emit_x_rest()
    emit_a2h_tail()
    s1_done = P.qdone("Y")

    ffn_sublayer(first_call=True, skip_g0_head=True)
    if debug:
        dt = DMA("G", lambda e: e.dma_start(dbg_d["dbg1"].ap(), xres[:]),
                 reads=[B["xres"]])
        B["xres"].read(dt)

    P.barrier()

    w1t_stok = DMA("G", lambda e: e.dma_start(w1t_d.ap(), w1T_t[:]),
                   reads=[B["w1T"]])
    B["w1T"].read(w1t_stok)

    for nm, srcd, dstt in (("wqs", wq_d, wqs_t), ("wks", wk_d, wks_t),
                           ("wvs", wv_d, wvs_t), ("wos", wo_d, wos_t)):
        src_v = srcd.ap().rearrange("(j p) e -> p j e", p=128)
        DMA("G", lambda e, dstt=dstt, src_v=src_v: e.dma_start(dstt[:], src_v),
            writes=[B[nm]])

    for _pi, _pt in ((0, psA), (1, psB)):
        vt = V_(lambda e, _pt=_pt: e.memset(_pt[:], 0.0),
                writes=[pB[3 * _pi], pB[3 * _pi + 1], pB[3 * _pi + 2]])
        for _b in range(3):
            pB[3 * _pi + _b].wrote(vt)
    V_(lambda e: e.memset(Sp_t[:], 0.0), writes=[B["S"]])
    V_(lambda e: e.memset(Cp_t[:], 0.0), writes=[B["C"]])
    V_(lambda e: e.memset(Gp_t[:], 0.0), writes=[B["G"]])

    def proj_group_T(g, Wbuf, Wt, dstbuf, dst_t):
        for jE in range(8):
            bi = jE % 4
            pt = bank(bi)
            pb = pB[bi]
            for j in range(NJ):
                lhs = Wt[:, j * D + jE * 128: j * D + jE * 128 + 128]
                rhs = xn[:, j * GB:(j + 1) * GB]
                T_(lambda e, pt=pt, lhs=lhs, rhs=rhs, j=j: e.matmul(
                    pt, lhs, rhs, start=(j == 0), stop=(j == NJ - 1)),
                   reads=[xnj_toks[j]] + ([Wbuf] if j == 0 else []),
                   writes=([pb] if j == 0 else []), mark=(j == NJ - 1))
            pb.wrote(P.tok("T"))
            if jE % 2 == 0:
                st = S_(lambda e, pt=pt, jE=jE: e.copy(
                    dst_t[:, jE * GB:(jE + 1) * GB], pt),
                    reads=[pb], writes=[dstbuf] if jE == 0 else [])
            else:
                st = V_(lambda e, pt=pt, jE=jE: e.tensor_copy(
                    dst_t[:, jE * GB:(jE + 1) * GB], pt),
                    reads=[pb], writes=[dstbuf] if jE == 0 else [])
            pb.read(st)
        dstbuf.wrote(P.tok("S"))
        dstbuf.wrote(P.tok("V"))

    def proj_chunk_nat(c, Wbuf, Wt, dstbuf, dst_t):
        tc = c % CPG
        for h2 in range(2):
            bi = h2
            pt = bank(bi)
            pb = pB[bi]
            for j in range(NJ):
                lhs = xn[:, j * GB + tc * 128: j * GB + tc * 128 + 128]
                rhs = Wt[:, j * D + h2 * 512: j * D + h2 * 512 + 512]
                T_(lambda e, pt=pt, lhs=lhs, rhs=rhs, j=j: e.matmul(
                    pt, lhs, rhs, start=(j == 0), stop=(j == NJ - 1)),
                   reads=[xnj_toks[j]] + ([Wbuf] if j == 0 else []),
                   writes=([pb] if j == 0 else []), mark=(j == NJ - 1))
            pb.wrote(P.tok("T"))
            st = S_(lambda e, pt=pt, h2=h2: e.copy(
                dst_t[:, h2 * 512:(h2 + 1) * 512], pt),
                reads=[pb], writes=[dstbuf] if h2 == 0 else [])
            pb.read(st)
        dstbuf.wrote(P.tok("S"))

    def evac_grid8(dst_t, set3, cw, engine_op, reads_extra=(), out_bf=None,
                   csub=128):
        base = psA if set3 == "A" else psB
        ops = []
        src1 = bass.AP(base.ap().tensor, base.ap().offset,
                       [[1536, 128], [512, 3], [128, 2], [1, csub]])
        dst1 = bass.AP(dst_t, 0, [[8 * cw, 128], [cw, 3], [3 * cw, 2], [1, cw]])
        src2 = bass.AP(base.ap().tensor, base.ap().offset + 256,
                       [[1536, 128], [512, 2], [1, csub]])
        dst2 = bass.AP(dst_t, 6 * cw, [[8 * cw, 128], [cw, 2], [1, cw]])
        return (dst1, src1), (dst2, src2)

    rot = {"A": 0, "B": 0}

    def sect(setn, ncells, cellw=128):
        r = rot[setn]
        rot[setn] = (r + ncells) % 3
        base = 0 if setn == "A" else 3
        cells = []
        for l in range(ncells):
            bl = (r + l) % 3
            bi = base + bl
            col = (l // 3) * cellw
            cells.append((bi, col, l < 3, l >= ncells - 3))
        evacs = []
        for bl in range(3):
            l0 = (bl - r) % 3
            if l0 >= ncells:
                continue
            nb = (ncells - l0 + 2) // 3
            evacs.append((base + bl, l0, nb))
        return cells, evacs

    def psrc(bi, colw, nb, subw):
        tens = psA if bi < 3 else psB
        b = bi % 3
        return bass.AP(tens.ap().tensor, tens.ap().offset + b * 512,
                       [[1536, 128], [colw, nb], [1, subw]])

    def ddst(dst_t, stride_row, l0, nb, cw, subw):
        return bass.AP(dst_t, l0 * cw,
                       [[stride_row, 128], [3 * cw, nb], [1, subw]])

    def pcell(bi, col, subw):
        tens = psA if bi < 3 else psB
        b = bi % 3
        return bass.AP(tens.ap().tensor, tens.ap().offset + b * 512 + col,
                       [[1536, 128], [1, subw]])

    def pbank2(bi, width):
        tens = psA if bi < 3 else psB
        b = bi % 3
        return bass.AP(tens.ap().tensor, tens.ap().offset + b * 512,
                       [[1536, 128], [1, width]])

    def slot3(dst_t, row_stride, l0, nb, cw=128):
        return bass.AP(dst_t, l0 * cw, [[row_stride, 128], [3 * cw, nb],
                                        [1, cw]])

    def proj_t2n(c, srcT_t, srcbuf, dstbuf, dst_t, setn):
        tc = c % CPG
        cells, evacs = sect(setn, 8)
        for l in range(8):
            bi, col, st_, sp_ = cells[l]
            cell = bank(bi)[:, col:col + 128]
            srcs = srcT_t[:, l * GB + tc * 128: l * GB + tc * 128 + 128]
            T_(lambda e, cell=cell, srcs=srcs: e.matmul(
                cell, srcs, identb[:], start=True, stop=True,
                skip_group_check=True),
               reads=([srcbuf, B["masks"]] if l == 0 else []),
               writes=[pB[bi]] if st_ else [], mark=sp_)
        ttok = P.tok("T")
        for bi, l0, nb in evacs:
            pB[bi].wrote(ttok)
        for _k, (bi, l0, nb) in enumerate(evacs):
            sap = psrc(bi, 128, nb, 128)
            dap = ddst(dst_t, 1024, l0, nb, 128, 128)
            SE_(lambda e, dap=dap, sap=sap: e.copy(dap, sap),
               reads=[pB[bi]], writes=[dstbuf] if _k == 0 else [])
        dstbuf.wrote(P.tok("S"))
        for bi, l0, nb in evacs:
            pB[bi].read(P.tok("S"))

    def scan_chunk(c):
        tc = c % CPG
        first = (c == 0)
        if not first:
            vt = VE_(lambda e: e.tensor_tensor(
                S0bd_t.rearrange("p (r c) -> p r c", c=128),
                Sp_t.rearrange("p (r c) -> p r c", c=128),
                mrep(bdm, 8), OP.mult),
                reads=[B["S"], B["masks"]], writes=[B["S0bd"]])
            B["S"].read(vt)
            vt = VE_(lambda e: e.tensor_tensor(
                C0bd_t.rearrange("p (r c) -> p r c", c=128),
                Cp_t.rearrange("p (r c) -> p r c", c=128),
                mrep(bdm, 8), OP.mult),
                reads=[B["C"], B["masks"]], writes=[B["C0bd"]])
            B["C"].read(vt)
        def emit_s0(w):
            if sgate & 1024:
                src3q = bass.AP(QTg_t, (4 * w) * GB + tc * 128,
                                [[8 * GB, 128], [GB, 4], [1, 128]])
                src3k = bass.AP(KTg_t, (4 * w) * GB + tc * 128,
                                [[8 * GB, 128], [GB, 4], [1, 128]])
                for qq in range(2):
                    pm3 = bass.AP(pm, qq * 128, [[256, 128], [0, 4], [1, 128]])
                    qdst = bass.AP(QTz_t, qq * 128, [[1024, 128], [256, 4], [1, 128]])
                    kdst = bass.AP(KTz_t, qq * 128, [[1024, 128], [256, 4], [1, 128]])
                    V_(lambda e, qdst=qdst, src3q=src3q, pm3=pm3: e.tensor_tensor(
                        qdst, src3q, pm3, OP.mult),
                       reads=[B["QTg"], B["masks"]], writes=[B["QTz"]])
                    V_(lambda e, kdst=kdst, src3k=src3k, pm3=pm3: e.tensor_tensor(
                        kdst, src3k, pm3, OP.mult),
                       reads=[B["KTg"], B["masks"]], writes=[B["KTz"]])

        emit_s0(0)
        for w in range(2):

            cells, evacs = sect("B", 4)
            for u in (range(4) if sgate & 128 else []):
                p = 4 * w + u
                bi, col, st_, sp_ = cells[u]
                cell = bank(bi)[:, col:col + 128]
                kns = kn_t[:, p * 128:(p + 1) * 128]
                T_(lambda e, cell=cell, kns=kns, st_=st_, sp_=sp_: e.matmul(
                    cell, kns, kns, start=st_, stop=sp_,
                    skip_group_check=True),
                   reads=([B["kn"]] if u == 0 else []),
                   writes=[pB[bi]] if st_ else [], mark=sp_)
            for bi, l0, nb in evacs:
                pB[bi].wrote(P.tok("T"))
            for _k, (bi, l0, nb) in enumerate(
                    evacs if (sgate & 128 and sgate & 131072) else []):
                VE_(lambda e, bi=bi, l0=l0, nb=nb: e.tensor_tensor(
                    slot3(KKbd_t, 512, l0, nb), pbank2(bi, nb * 128),
                    mrep(bdm, nb), OP.mult),
                   reads=[pB[bi]] + ([B["masks"]] if _k == 0 else []),
                   writes=[B["KKbd"]] if _k == 0 else [])
                sdst = bass.AP(Sp_t, (4 * w + l0) * 128,
                               [[1024, 128], [3 * 128, nb], [1, 128]])
                VE_(lambda e, bi=bi, nb=nb, sdst=sdst: e.tensor_tensor(
                    sdst, sdst, pbank2(bi, nb * 128), OP.add),
                   reads=[pB[bi]], writes=[B["S"]] if _k == 0 else [])
            B["KKbd"].wrote(P.tok("V"))
            B["S"].wrote(P.tok("V"))
            for bi, l0, nb in evacs:
                pB[bi].read(P.tok("V"))

            cells, evacs = sect("A", 4)
            for u in (range(4) if sgate & 256 else []):
                p = 4 * w + u
                bi, col, st_, sp_ = cells[u]
                cell = bank(bi)[:, col:col + 128]
                qns = qn_t[:, p * 128:(p + 1) * 128]
                vns = vn_t[:, p * 128:(p + 1) * 128]
                T_(lambda e, cell=cell, qns=qns, vns=vns, st_=st_, sp_=sp_:
                   e.matmul(cell, qns, vns, start=st_, stop=sp_,
                            skip_group_check=True),
                   reads=([B["qn"]] if u == 0 else []),
                   writes=[pB[bi]] if st_ else [], mark=sp_)
            for bi, l0, nb in evacs:
                pB[bi].wrote(P.tok("T"))
            for _k, (bi, l0, nb) in enumerate(
                    evacs if (sgate & 256 and sgate & 262144) else []):
                cdst = bass.AP(Cp_t, (4 * w + l0) * 128,
                               [[1024, 128], [3 * 128, nb], [1, 128]])
                VE_(lambda e, bi=bi, nb=nb, cdst=cdst: e.tensor_tensor(
                    cdst, cdst, pbank2(bi, nb * 128), OP.add),
                   reads=[pB[bi]], writes=[B["C"]] if _k == 0 else [])
            B["C"].wrote(P.tok("V"))
            for bi, l0, nb in evacs:
                pB[bi].read(P.tok("V"))

            cells, evacs = sect("A", 8)
            for l in (range(8) if sgate & 1 else []):
                h = 8 * w + l
                p = h // 2
                bi, col, st_, sp_ = cells[l]
                cell = bank(bi)[:, col:col + 128]
                QTc = QTg_t[:, p * GB + tc * 128: p * GB + tc * 128 + 128]
                KTzs = KTz_t[:, ts(l)]
                T_(lambda e, cell=cell, KTzs=KTzs, QTc=QTc, st_=st_, sp_=sp_:
                   e.matmul(cell, KTzs, QTc, start=st_, stop=sp_,
                            skip_group_check=True),
                   reads=([B["QTg"], B["KTz"]] if l == 0 else []),
                   writes=[pB[bi]] if st_ else [], mark=sp_)
            at_tok = P.tok("T")
            for bi, l0, nb in evacs:
                pB[bi].wrote(at_tok)
            for _k, (bi, l0, nb) in enumerate(evacs):
                sap = psrc(bi, 128, nb, 128)
                dap = ddst(Atb_t, 1024, l0, nb, 128, 128)
                SE_(lambda e, dap=dap, sap=sap: e.copy(dap, sap),
                   reads=[pB[bi]], writes=[B["Atb"]] if _k == 0 else [])
            B["Atb"].wrote(P.tok("S"))
            for bi, l0, nb in evacs:
                pB[bi].read(P.tok("S"))
            for _k, (bi, l0, nb) in enumerate(
                    evacs if (sgate & 1 and sgate & 8192) else []):
                VE_(lambda e, bi=bi, l0=l0, nb=nb: e.tensor_tensor(
                    slot3(Atr_t, 1024, l0, nb), pbank2(bi, nb * 128),
                    mrep(triu, nb), OP.mult),
                   reads=[pB[bi]] + ([B["masks"], B["Atb"]] if _k == 0
                                     else []),
                   writes=[B["Atr"]] if _k == 0 else [])
            B["Atr"].wrote(P.tok("V"))
            for bi, l0, nb in evacs:
                pB[bi].read(P.tok("V"))

            cells, evacs = sect("B", 8)
            for l in (range(8) if sgate & 2 else []):
                h = 8 * w + l
                p = h // 2
                bi, col, st_, sp_ = cells[l]
                cell = bank(bi)[:, col:col + 128]
                KTc = KTg_t[:, p * GB + tc * 128: p * GB + tc * 128 + 128]
                QTzs = QTz_t[:, ts(l)]
                T_(lambda e, cell=cell, QTzs=QTzs, KTc=KTc, st_=st_, sp_=sp_:
                   e.matmul(cell, QTzs, KTc, start=st_, stop=sp_,
                            skip_group_check=True),
                   reads=([B["QTz"], B["KTg"]] if l == 0 else []),
                   writes=[pB[bi]] if st_ else [], mark=sp_)
            for bi, l0, nb in evacs:
                pB[bi].wrote(P.tok("T"))
            for _k, (bi, l0, nb) in enumerate(
                    evacs if (sgate & 2 and sgate & 16384) else []):
                VE_(lambda e, bi=bi, l0=l0, nb=nb: e.tensor_tensor(
                    slot3(Aqs_t, 1024, l0, nb), pbank2(bi, nb * 128),
                    mrep(striu, nb), OP.mult),
                   reads=[pB[bi]] + ([B["masks"]] if _k == 0 else []),
                   writes=[B["Aqs"]] if _k == 0 else [])
            B["Aqs"].wrote(P.tok("V"))
            for bi, l0, nb in evacs:
                pB[bi].read(P.tok("V"))

            if (not first) and (sgate & 4):
                cells, evacs = sect("A", 4)
                for u in range(4):
                    p = 4 * w + u
                    bi, col, st_, sp_ = cells[u]
                    cell = bank(bi)[:, col:col + 128]
                    S0s = S0bd_t[:, p * 128:(p + 1) * 128]
                    QTc = QTg_t[:, p * GB + tc * 128: p * GB + tc * 128 + 128]
                    T_(lambda e, cell=cell, S0s=S0s, QTc=QTc, st_=st_,
                       sp_=sp_: e.matmul(cell, S0s, QTc, start=st_, stop=sp_,
                                         skip_group_check=True),
                       reads=([B["S0bd"]] if u == 0 else []),
                       writes=[pB[bi]] if st_ else [], mark=sp_)
                for bi, l0, nb in evacs:
                    pB[bi].wrote(P.tok("T"))
                for _k, (bi, l0, nb) in enumerate(evacs):
                    sap = psrc(bi, 128, nb, 128)
                    dap = ddst(Ybf_t, 512, l0, nb, 128, 128)
                    SE_(lambda e, dap=dap, sap=sap: e.copy(dap, sap),
                       reads=[pB[bi]], writes=[B["Ybf"]] if _k == 0 else [])
                B["Ybf"].wrote(P.tok("S"))
                for bi, l0, nb in evacs:
                    pB[bi].read(P.tok("S"))
                cells, evacs = sect("B", 4)
                for u in range(4):
                    p = 4 * w + u
                    bi, col, st_, sp_ = cells[u]
                    cell = bank(bi)[:, col:col + 128]
                    S0s = S0bd_t[:, p * 128:(p + 1) * 128]
                    C0s = C0bd_t[:, p * 128:(p + 1) * 128]
                    T_(lambda e, cell=cell, S0s=S0s, C0s=C0s, st_=st_,
                       sp_=sp_: e.matmul(cell, S0s, C0s, start=st_, stop=sp_,
                                         skip_group_check=True),
                       reads=([B["C0bd"]] if u == 0 else []),
                       writes=[pB[bi]] if st_ else [], mark=sp_)
                for bi, l0, nb in evacs:
                    pB[bi].wrote(P.tok("T"))
                for _k, (bi, l0, nb) in enumerate(
                        evacs if sgate & 32768 else []):
                    gsrc = bass.AP(Gp_t, (4 * w + l0) * 128,
                                   [[1024, 128], [3 * 128, nb], [1, 128]])
                    VE_(lambda e, bi=bi, l0=l0, nb=nb, gsrc=gsrc:
                        e.tensor_tensor(
                            slot3(W0b_t, 512, l0, nb), pbank2(bi, nb * 128),
                            gsrc, OP.subtract),
                       reads=[pB[bi]] + ([B["G"]] if _k == 0 else []),
                       writes=[B["W0b"]] if _k == 0 else [])
                B["W0b"].wrote(P.tok("V"))
                B["G"].read(P.tok("V"))
                for bi, l0, nb in evacs:
                    pB[bi].read(P.tok("V"))

            cellsM, evacsM = sect("A", 8)
            for l in (range(8) if sgate & 8 else []):
                bi, col, st_, sp_ = cellsM[l]
                cell = bank(bi)[:, col:col + 128]
                Atrs = Atr_t[:, ts(l)]
                Atbs = Atb_t[:, ts(l)]
                T_(lambda e, cell=cell, Atrs=Atrs, Atbs=Atbs, st_=st_,
                   sp_=sp_, first=first: e.matmul(
                       cell, Atrs, Atbs, start=st_,
                       stop=(first and sp_), skip_group_check=True),
                   reads=([B["Atr"], B["Atb"]] if l == 0 else []),
                   writes=[pB[bi]] if st_ else [], mark=(first and l == 7))
            cellsZ, evacsZ = sect("B", 8)
            for l in (range(8) if sgate & 16 else []):
                h = 8 * w + l
                Aqss = Aqs_t[:, ts(l)]
                vns = vn_t[:, h * 64: h * 64 + 64]
                bi, col, st_, sp_ = cellsZ[l]
                cell = bank(bi)[:, col:col + 64]
                T_(lambda e, cell=cell, Aqss=Aqss, vns=vns, st_=st_, sp_=sp_:
                   e.matmul(cell, Aqss, vns, start=st_, stop=sp_,
                            skip_group_check=True),
                   reads=([B["Aqs"], B["vn"]] if l == 0 else []),
                   writes=[pB[bi]] if st_ else [], mark=sp_)
            for bi, l0, nb in evacsZ:
                pB[bi].wrote(P.tok("T"))
            for _k, (bi, l0, nb) in enumerate(evacsZ):
                sap = psrc(bi, 128, nb, 64)
                dap = ddst(Zb_t, 512, l0, nb, 64, 64)
                SE_(lambda e, dap=dap, sap=sap: e.copy(dap, sap),
                   reads=[pB[bi]], writes=[B["Zb"]] if _k == 0 else [])
            B["Zb"].wrote(P.tok("S"))
            for bi, l0, nb in evacsZ:
                pB[bi].read(P.tok("S"))
            if (not first) and (sgate & 32):
                for l in range(8):
                    u = l // 2
                    bi, col, st_, sp_ = cellsM[l]
                    cell = bank(bi)[:, col:col + 128]
                    QTzs = QTz_t[:, ts(l)]
                    Ys = Ybf_t[:, u * 128: u * 128 + 128]
                    T_(lambda e, cell=cell, QTzs=QTzs, Ys=Ys, sp_=sp_:
                       e.matmul(cell, QTzs, Ys, start=False, stop=sp_,
                                skip_group_check=True),
                       reads=([B["Ybf"], B["QTz"]] if l == 0 else []),
                       writes=[], mark=(l == 7))
            mt_tok = P.tok("T")
            for bi, l0, nb in evacsM:
                pB[bi].wrote(mt_tok)
            for _k, (bi, l0, nb) in enumerate(
                    evacsM if (sgate & 8 and sgate & 65536) else []):
                VE_(lambda e, bi=bi, l0=l0, nb=nb: e.tensor_tensor(
                    slot3(MtS_t, 1024, l0, nb), pbank2(bi, nb * 128),
                    mrep(triu, nb), OP.mult),
                   reads=[pB[bi]] + ([B["masks"]] if _k == 0 else []),
                   writes=[B["MtS"]] if _k == 0 else [])
            B["MtS"].wrote(P.tok("V"))
            for bi, l0, nb in evacsM:
                pB[bi].read(P.tok("V"))

            cellsO, evacsO = sect("A", 4)
            if (not first) and (sgate & 64):
                for u in range(4):
                    p = 4 * w + u
                    bi, col, st_, sp_ = cellsO[u]
                    cell = bank(bi)[:, col:col + 128]
                    W0s = W0b_t[:, u * 128:(u + 1) * 128]
                    QTc = QTg_t[:, p * GB + tc * 128: p * GB + tc * 128 + 128]
                    T_(lambda e, cell=cell, W0s=W0s, QTc=QTc, st_=st_:
                       e.matmul(cell, W0s, QTc, start=st_, stop=False,
                                skip_group_check=True),
                       reads=([B["W0b"]] if u == 0 else []),
                       writes=[pB[bi]] if st_ else [], mark=False)
            if w == 0:
                B["QTz"].read(P.tok("T"))
                B["KTz"].read(P.tok("T"))
                emit_s0(1)
            cells, evacs = sect("B", 4)
            if (not first) and (sgate & 512):
                for u in range(4):
                    p = 4 * w + u
                    bi, col, st_, sp_ = cells[u]
                    cell = bank(bi)[:, col:col + 128]
                    KKs = KKbd_t[:, u * 128:(u + 1) * 128]
                    C0s = C0bd_t[:, p * 128:(p + 1) * 128]
                    T_(lambda e, cell=cell, KKs=KKs, C0s=C0s, st_=st_:
                       e.matmul(cell, KKs, C0s, start=st_, stop=False,
                                skip_group_check=True),
                       reads=([B["KKbd"]] if u == 0 else []),
                       writes=[pB[bi]] if st_ else [], mark=False)
            for l in ([0, 2, 4, 6, 1, 3, 5, 7] if sgate & 64 else []):
                h = 8 * w + l
                q = l % 2
                u = l // 2
                psl = slice(64 * q, 64 * q + 64)
                bi, col, st_, sp_ = cellsO[u]
                cell = bank(bi)[psl, col:col + 128]
                vns = vn_t[:, h * 64: h * 64 + 64]
                MtSs = MtS_t[:, ts(l)]
                T_(lambda e, cell=cell, vns=vns, MtSs=MtSs, q=q, st_=st_,
                   sp_=sp_, first=first: e.matmul(
                       cell, vns, MtSs, start=(st_ if first else False),
                       stop=(sp_ and q == 1),
                       tile_position=(0, 64 * q), skip_group_check=True),
                   reads=([B["MtS"], B["vn"]] if l == 0 else []),
                   writes=[pB[bi]] if (first and st_ and q == 0) else [],
                   mark=(l == 7))
            ot_tok = P.tok("T")
            for bi, l0, nb in evacsO:
                pB[bi].wrote(ot_tok)
            for bi, l0, nb in evacsO:
                sap = psrc(bi, 128, nb, 128)
                dap = bass.AP(OTg_t, (4 * w + l0) * 512 + (tc % 4) * 128,
                              [[4096, 128], [3 * 512, nb], [1, 128]])
                SE_(lambda e, dap=dap, sap=sap: e.copy(dap, sap),
                   reads=[pB[bi]], writes=[B["OTg"]])
            for bi, l0, nb in evacsO:
                pB[bi].read(P.tok("S"))

            for u in (range(4) if sgate & 512 else []):
                p = 4 * w + u
                bi, col, st_, sp_ = cells[u]
                cell = bank(bi)[:, col:col + 128]
                kns = kn_t[:, p * 128:(p + 1) * 128]
                Zs = Zb_t[:, u * 128:(u + 1) * 128]
                T_(lambda e, cell=cell, kns=kns, Zs=Zs, st_=st_, sp_=sp_,
                   first=first: e.matmul(
                       cell, kns, Zs, start=(first and st_), stop=sp_,
                       skip_group_check=True),
                   reads=([B["Zb"], B["kn"]] if u == 0 else []),
                   writes=[pB[bi]] if (first and st_) else [],
                   mark=sp_)
            g_tok = P.tok("T")
            for nm in ("KKbd", "C0bd", "S0bd", "W0b", "Ybf", "MtS", "vn",
                       "Atr", "Atb", "Aqs", "Zb", "kn", "qn", "QTg", "KTg",
                       "QTz", "KTz"):
                B[nm].read(g_tok)
            for bi, l0, nb in evacs:
                pB[bi].wrote(g_tok)
            for _k, (bi, l0, nb) in enumerate(
                    evacs if (sgate & 512 and sgate & 524288) else []):
                VE_(lambda e, bi=bi, l0=l0, nb=nb: e.tensor_tensor(
                    slot3(updt, 512, l0, nb), pbank2(bi, nb * 128),
                    mrep(bdm, nb), OP.mult),
                   reads=[pB[bi]] + ([B["masks"]] if _k == 0 else []),
                   writes=[B["updt"]] if _k == 0 else [])
            B["updt"].wrote(P.tok("V"))
            for bi, l0, nb in evacs:
                pB[bi].read(P.tok("V"))
            vt = VE_(lambda e, w=w: e.tensor_tensor(
                Gp_t[:, 4 * w * 128: 4 * w * 128 + 512],
                Gp_t[:, 4 * w * 128: 4 * w * 128 + 512],
                updt[:], OP.add),
                reads=[B["updt"]], writes=[B["G"]])
            B["updt"].read(vt)

    def tap(nm, t_, buf):
        if debug and nm in dbg_d:
            dt = DMA("G", lambda e, nm=nm, t_=t_: e.dma_start(
                dbg_d[nm].ap(), t_[:]), reads=[buf])
            buf.read(dt)

    for g in range(NG):
        if g == 0:
            layernorm_group(0, xn_war_tok=P.tok("T"))
            tap("txn", xn, B["xn"])
        proj_group_T(g, B["wqs"], wqs_t, B["QTg"], QTg_t)
        proj_group_T(g, B["wks"], wks_t, B["KTg"], KTg_t)
        if g == 0:
            tap("tQT", QTg_t, B["QTg"])
            tap("tKT", KTg_t, B["KTg"])
        for ci in range(CPG):
            c = g * CPG + ci
            proj_t2n(c, QTg_t, B["QTg"], B["qn"], qn_t, "A")
            proj_t2n(c, KTg_t, B["KTg"], B["kn"], kn_t, "B")
            proj_chunk_nat(c, B["wvs"], wvs_t, B["vn"], vn_t)
            if ci == CPG - 1 and g + 1 < NG:
                ln_stats(g + 1)
            if c == 0:
                tap("tqn", qn_t, B["qn"])
                tap("tkn", kn_t, B["kn"])
                tap("tvn", vn_t, B["vn"])
            if c < nsc:
                scan_chunk(c)
            if ci == CPG - 1 and wo_on:
                for w2 in range(2):
                    for ji in range(4):
                        jd = 4 * w2 + ji
                        bi = ji
                        pt = bank(bi)
                        pb = pB[bi]
                        for jh in range(NJ):
                            lhs = wos_t[:, jh * D + jd * 128:
                                        jh * D + jd * 128 + 128]
                            rhs = OTg_t[:, jh * 512: jh * 512 + 512]
                            T_(lambda e, pt=pt, lhs=lhs, rhs=rhs, jh=jh:
                               e.matmul(pt, lhs, rhs,
                                        start=(jh == 0),
                                        stop=(jh == NJ - 1)),
                               reads=([B["wos"], B["OTg"]] if jh == 0
                                      else []),
                               writes=([pb] if jh == 0 else []),
                               mark=(jh == NJ - 1))
                        pb.wrote(P.tok("T"))
                        xs = xres[:, jd * T + g * GB:
                                  jd * T + g * GB + 512]
                        vt = V_(lambda e, xs=xs, pt=pt: e.tensor_tensor(
                            xs, xs, pt[:], OP.add),
                            reads=[pb], writes=[B["xres"]])
                        pb.read(vt)
                B["OTg"].read(P.tok("T"))
            if c == 0:
                tap("tS1", Sp_t, B["S"])
                tap("tC1", Cp_t, B["C"])
                tap("tG1", Gp_t, B["G"])
            if c == 1:
                tap("tS2", Sp_t, B["S"])
                tap("tC2", Cp_t, B["C"])
                tap("tG2", Gp_t, B["G"])
        if g + 1 < NG:
            ln_apply(g + 1, xn_war_tok=P.tok("T"))

    if debug:
        dt = DMA("G", lambda e: e.dma_start(dbg_d["dbg2"].ap(), xres[:]),
                 reads=[B["xres"]])
        B["xres"].read(dt)

    P.barrier()
    ffn_sublayer(first_call=False)

    for j in range(NJ):
        q = ("Y", "G")[j % 2]
        dt = DMA(q, lambda e, j=j: e.dma_start(
            out_d.ap()[j * 128:(j + 1) * 128, :], xres[:, j * T:(j + 1) * T]),
            reads=[B["xres"]])
        B["xres"].read(dt)

    P.wait("Y", P.qdone("Y"))
    P.wait("Y", P.qdone("G"))
    P.emit("Y", lambda e: e.nop(), mark=False)

    with nc.Block() as blk:
        blk.tensor(lambda e: P.run("T", e))
        blk.vector(lambda e: P.run("V", e))
        blk.scalar(lambda e: P.run("S", e))
        blk.gpsimd(lambda e: P.run("G", e))
        blk.sync(lambda e: P.run("Y", e))

    stack.close()
    return nc


_NC_CACHE = {}


def _get_nc(debug=False):
    key = bool(debug)
    if key not in _NC_CACHE:
        _NC_CACHE[key] = build(debug=debug)
    return _NC_CACHE[key]


def _host_constants():
    triu1 = np.triu(np.ones((128, 128), np.float32)).astype(ml_dtypes.bfloat16)
    striu1 = np.triu(np.ones((128, 128), np.float32), 1).astype(
        ml_dtypes.bfloat16)
    q = np.arange(128) // 64
    bdm1 = (q[:, None] == q[None, :]).astype(np.float32).astype(
        ml_dtypes.bfloat16)
    pm1 = np.zeros((128, 256), np.float32)
    pm1[0:64, 0:128] = 1.0
    pm1[64:128, 128:256] = 1.0
    pm1 = pm1.astype(ml_dtypes.bfloat16)
    ident = np.eye(128, dtype=np.float32)
    return triu1, striu1, bdm1, pm1, ident


def _in_maps(L0_attn_x, L0_attn_W_q, L0_attn_W_k, L0_attn_W_v, L0_attn_W_o,
             L0_ffn_w1, L0_ffn_b1, L0_ffn_w2, L0_ffn_b2):
    triu1, striu1, bdm1, pm1, ident = _host_constants()
    common = {
        "wq": np.ascontiguousarray(L0_attn_W_q, np.float32),
        "wk": np.ascontiguousarray(L0_attn_W_k, np.float32),
        "wv": np.ascontiguousarray(L0_attn_W_v, np.float32),
        "wo": np.ascontiguousarray(L0_attn_W_o, np.float32),
        "w1": np.ascontiguousarray(L0_ffn_w1, np.float32),
        "b1": np.ascontiguousarray(L0_ffn_b1, np.float32),
        "w2": np.ascontiguousarray(L0_ffn_w2, np.float32),
        "b2": np.ascontiguousarray(L0_ffn_b2, np.float32),
        "triu1": triu1, "striu1": striu1, "bdm1": bdm1, "pm1": pm1,
        "ident": ident,
    }
    x = np.ascontiguousarray(L0_attn_x, np.float32)
    return [dict(common, x=np.ascontiguousarray(x[i].T)) for i in range(8)]


def kernel(L0_attn_x, L0_attn_W_q, L0_attn_W_k, L0_attn_W_v, L0_attn_W_o,
           L0_ffn_w1, L0_ffn_b1, L0_ffn_w2, L0_ffn_b2):
    nc = _get_nc(debug=False)
    in_maps = _in_maps(L0_attn_x, L0_attn_W_q, L0_attn_W_k, L0_attn_W_v,
                       L0_attn_W_o, L0_ffn_w1, L0_ffn_b1, L0_ffn_w2,
                       L0_ffn_b2)
    res = run_bass_kernel_spmd(nc, in_maps, core_ids=list(range(8)))
    out = np.stack([np.asarray(r["out"]).T for r in res.results], axis=0)
    return np.ascontiguousarray(out, np.float32)


# version 18 (final)
# speedup vs baseline: 1.0677x; 1.0510x over previous
import numpy as np
import ml_dtypes
import concourse.bass as bass
import concourse.mybir as mybir
from concourse.bass_utils import run_bass_kernel_spmd

F32 = mybir.dt.float32
BF16 = mybir.dt.bfloat16
AF = mybir.ActivationFunctionType
OP = mybir.AluOpType
AX = mybir.AxisListType

D = 1024
T = 2048
H = 16
DK = 64
FF = 4096
NJ = D // 128
NF = FF // 128
NT = T // 128
GB = 512
NG = T // GB
CH = 128
NC_ = T // CH
CPG = GB // CH

ENGS = ("T", "V", "S", "G", "Y")


class Prog:
    NLANE = 6

    def __init__(self, nc):
        self.nc = nc
        self.prog = {e: [] for e in ENGS}
        self.count = {e: 0 for e in ENGS}
        self.hwm = {}
        self._pend = {e: [] for e in ENGS}
        self.sems = {}
        self.dsems = {}

    def make_sems(self, stack):
        for e in ENGS:
            self.sems[e] = stack.enter_context(self.nc.semaphore(f"ms{e}"))
        for e in ("Y", "G", "S", "V"):
            self.dsems[e] = [stack.enter_context(self.nc.semaphore(f"dm{e}{i}"))
                             for i in range(self.NLANE)]
        self.lane_cnt = {e: [0] * self.NLANE for e in ("Y", "G", "S", "V")}
        self.lane_idx = {e: 0 for e in ("Y", "G", "S", "V")}

    def wait(self, eng, tok):
        if tok is None:
            return
        if isinstance(tok, list):
            for t in tok:
                self.wait(eng, t)
            return
        prod, kind, val = tok
        if kind == "c":
            if prod == eng or val <= 0:
                return
            key = (eng, prod, "c")
            if self.hwm.get(key, 0) >= val:
                return
            self.hwm[key] = val
            self._pend[eng].append((prod, "c", val))
        else:
            lane, v = val
            if v <= 0:
                return
            key = (eng, prod, "d", lane)
            if self.hwm.get(key, 0) >= v:
                return
            self.hwm[key] = v
            self._pend[eng].append((prod, ("d", lane), v))

    def emit(self, eng, fn, mark=False, dma=False):
        tok = None
        if dma:
            lane = self.lane_idx[eng]
            self.lane_idx[eng] = (lane + 1) % self.NLANE
            prev = self.lane_cnt[eng][lane]
            if prev > 0:
                self.wait(eng, (eng, "d", (lane, prev)))
            self.lane_cnt[eng][lane] = prev + 16
            waits, self._pend[eng] = self._pend[eng], []
            incs = [(eng, ("d", lane), 16)]
            tok = (eng, "d", (lane, prev + 16))
            self.prog[eng].append((waits, fn, incs))
            return tok
        waits, self._pend[eng] = self._pend[eng], []
        incs = []
        if mark:
            self.count[eng] += 1
            incs.append((eng, "c", 1))
            tok = (eng, "c", self.count[eng])
        self.prog[eng].append((waits, fn, incs))
        return tok

    def tok(self, eng):
        return (eng, "c", self.count[eng])

    def qdone(self, q):
        return [(q, "d", (ln, c)) for ln, c in enumerate(self.lane_cnt[q])
                if c > 0]

    def barrier(self):
        snap_c = dict(self.count)
        for e in ENGS:
            for o in ENGS:
                if o != e and snap_c[o] > 0:
                    self.wait(e, (o, "c", snap_c[o]))
                if o in ("Y", "G", "S", "V"):
                    self.wait(e, self.qdone(o))

    def run(self, eng_name, engine):
        for waits, fn, incs in self.prog[eng_name]:
            for (prod, kind, val) in waits:
                if kind == "c":
                    engine.wait_ge(self.sems[prod], val)
                else:
                    engine.wait_ge(self.dsems[prod][kind[1]], val)
            instr = fn(engine)
            for (prod, kind, amt) in incs:
                if kind == "c":
                    instr.then_inc(self.sems[prod], amt)
                else:
                    instr.then_inc(self.dsems[prod][kind[1]], amt)


class Buf:
    def __init__(self, name=""):
        self.name = name
        self.w = None
        self.r = {}

    def wrote(self, tok):
        if tok is not None:
            self.w = tok
            self.r = {}

    def read(self, tok):
        if tok is not None:
            self.r[tok[0]] = tok

    def war(self):
        toks = list(self.r.values())
        if self.w is not None:
            toks.append(self.w)
        return toks


def ts(i, n=128):
    return slice(i * n, (i + 1) * n)


def build(debug=False, nsc=99, wo_on=True, sgate=(1 << 20) - 1):
    EV_S = bool(sgate & 2048)
    EV_V = bool(sgate & 4096)
    import contextlib
    nc = bass.Bass("TRN2", target_bir_lowering=False,
                   detect_race_conditions=False)
    stack = contextlib.ExitStack()
    P = Prog(nc)
    P.make_sems(stack)

    x_d = nc.dram_tensor("x", [D, T], F32, kind="ExternalInput")
    wq_d = nc.dram_tensor("wq", [D, D], F32, kind="ExternalInput")
    wk_d = nc.dram_tensor("wk", [D, D], F32, kind="ExternalInput")
    wv_d = nc.dram_tensor("wv", [D, D], F32, kind="ExternalInput")
    wo_d = nc.dram_tensor("wo", [D, D], F32, kind="ExternalInput")
    w1_d = nc.dram_tensor("w1", [FF, D], F32, kind="ExternalInput")
    b1_d = nc.dram_tensor("b1", [FF], F32, kind="ExternalInput")
    w2_d = nc.dram_tensor("w2", [D, FF], F32, kind="ExternalInput")
    b2_d = nc.dram_tensor("b2", [D], F32, kind="ExternalInput")
    triu_d = nc.dram_tensor("triu1", [128, 128], BF16, kind="ExternalInput")
    striu_d = nc.dram_tensor("striu1", [128, 128], BF16, kind="ExternalInput")
    bdm_d = nc.dram_tensor("bdm1", [128, 128], BF16, kind="ExternalInput")
    pm_d = nc.dram_tensor("pm1", [128, 256], BF16, kind="ExternalInput")
    ident_d = nc.dram_tensor("ident", [128, 128], F32, kind="ExternalInput")
    out_d = nc.dram_tensor("out", [D, T], F32, kind="ExternalOutput")
    s1_d = nc.dram_tensor("s1scr", [FF, D], BF16, kind="Internal")
    s2_d = nc.dram_tensor("s2scr", [D, FF], BF16, kind="Internal")
    w2t_d = nc.dram_tensor("w2tscr", [FF, D], BF16, kind="Internal")
    w1t_d = nc.dram_tensor("w1tscr", [128, NJ * FF], BF16, kind="Internal")
    dbg_d = {}
    if debug:
        dbg_d["dbg1"] = nc.dram_tensor("dbg1", [128, NJ * T], F32,
                                       kind="ExternalOutput")
        dbg_d["dbg2"] = nc.dram_tensor("dbg2", [128, NJ * T], F32,
                                       kind="ExternalOutput")
        for nm, wid, dt in (("txn", NJ * GB, BF16), ("tQT", 8 * GB, BF16),
                            ("tKT", 8 * GB, BF16), ("tqn", D, BF16),
                            ("tkn", D, BF16), ("tvn", D, BF16),
                            ("tUP", GB, F32),
                            ("tOTg", 4096, BF16), ("tS1", 1024, F32),
                            ("tC1", 1024, F32), ("tG1", 1024, F32),
                            ("tS2", 1024, F32), ("tC2", 1024, F32),
                            ("tG2", 1024, F32),
                            ("tw1T", NJ * FF, BF16),
                            ("tghT", NF * GB, BF16),
                            ("tF0", GB, F32), ("tF7", GB, F32),
                            ("tW20", 2048, BF16), ("tW27", 2048, BF16),
                            ("tA2h", NJ, F32), ("tA2t", 32, F32)):
            dbg_d[nm] = nc.dram_tensor(nm, [128, wid], dt,
                                       kind="ExternalOutput")

    xres = nc.alloc_sbuf_tensor("xres", [128, NJ * T], F32)
    xn = nc.alloc_sbuf_tensor("xn", [128, NJ * GB], BF16)
    triu = nc.alloc_sbuf_tensor("triu_s", [128, 128], BF16)
    striu = nc.alloc_sbuf_tensor("striu_s", [128, 128], BF16)
    bdm = nc.alloc_sbuf_tensor("bdm_s", [128, 128], BF16)
    pm = nc.alloc_sbuf_tensor("pm_s", [128, 256], BF16)
    ident = nc.alloc_sbuf_tensor("idents", [128, 128], F32)
    identb = nc.alloc_sbuf_tensor("identb", [128, 128], BF16)
    ones_col = nc.alloc_sbuf_tensor("ones_col", [128, 1], F32)
    ones_row = nc.alloc_sbuf_tensor("ones_row", [1, 128], F32)
    alpha1 = nc.alloc_sbuf_tensor("alpha1", [128, NF], F32)
    b1s = nc.alloc_sbuf_tensor("b1s", [128, NF], F32)
    a2h = nc.alloc_sbuf_tensor("a2h", [128, NJ], F32)
    b2h = nc.alloc_sbuf_tensor("b2h", [128, NJ], F32)
    a2t = nc.alloc_sbuf_tensor("a2t", [128, 32], F32)
    inv_c = nc.alloc_sbuf_tensor("inv_c", [128, 1], F32)
    xsq = nc.alloc_sbuf_tensor("xsq", [128, GB], F32)
    lnt = nc.alloc_sbuf_tensor("lnt", [128, GB], F32)
    updt = nc.alloc_sbuf_tensor("updt", [128, GB], F32)
    m_row = lnt[0:1, :]
    v_row = updt[0:1, :]
    r_row = xsq[0:1, :]
    cst = nc.alloc_sbuf_tensor("cst", [128, 4], F32)

    SCR0 = (nc._sbuf_addr_for_side("left") + 31) // 32 * 32
    SCRTOP = nc._sbuf_addr_for_side("right")
    AVAIL = SCRTOP - SCR0

    def at(name, shape, dtype, off):
        return nc.alloc_sbuf_tensor_at(name, shape, dtype, offset=SCR0 + off)

    w1T_t = at("w1T", [128, NJ * FF], BF16, 0)
    ghT_t = at("ghT", [128, NF * GB], BF16, 65536)
    w2s_t = at("w2s", [128, 2 * 2048], BF16, 98304)
    s1stage = [at(f"s1stage{i}", [128, 1024], F32, 98304 + i * 4096)
               for i in range(2)]
    w1stage = [at(f"w1stage{i}", [128, 1024], F32, 106496 + i * 4096)
               for i in range(2)]
    w2stage = [at(f"w2stage{i}", [128, 1024], F32, 114688 + i * 4096)
               for i in range(2)]
    s2stage = [at(f"s2stage{i}", [128, 1024], BF16, 122880 + i * 2048)
               for i in range(2)]
    bstage = at("bstage", [32, 128], F32, 126976)
    o = 0
    wqs_t = at("wqs", [128, NJ * D], BF16, o); o += 16384
    wks_t = at("wks", [128, NJ * D], BF16, o); o += 16384
    wvs_t = at("wvs", [128, NJ * D], BF16, o); o += 16384
    wos_t = at("wos", [128, NJ * D], BF16, o); o += 16384
    QTg_t = at("QTg", [128, 8 * GB], BF16, o); o += 8192
    KTg_t = at("KTg", [128, 8 * GB], BF16, o); o += 8192
    OTg_t = at("OTg", [128, 8 * 512], BF16, o); o += 8192
    QTz_t = at("QTz", [128, 8 * 128], BF16, o); o += 2048
    KTz_t = at("KTz", [128, 8 * 128], BF16, o); o += 2048
    qn_t = at("qn", [128, D], BF16, o); o += 2048
    kn_t = at("kn", [128, D], BF16, o); o += 2048
    vn_t = at("vn", [128, D], BF16, o); o += 2048
    Sp_t = at("Sp", [128, 8 * 128], F32, o); o += 4096
    Cp_t = at("Cp", [128, 8 * 128], F32, o); o += 4096
    Gp_t = at("Gp", [128, 8 * 128], F32, o); o += 4096
    S0bd_t = at("S0bd", [128, 8 * 128], BF16, o); o += 2048
    C0bd_t = at("C0bd", [128, 8 * 128], BF16, o); o += 2048
    Atb_t = at("Atb", [128, 8 * 128], BF16, o); o += 2048
    Atr_t = at("Atr", [128, 8 * 128], BF16, o); o += 2048
    Aqs_t = at("Aqs", [128, 8 * 128], BF16, o); o += 2048
    MtS_t = at("MtS", [128, 8 * 128], BF16, o); o += 2048
    Ybf_t = at("Ybf", [128, 4 * 128], BF16, o); o += 1024
    W0b_t = at("W0b", [128, 4 * 128], BF16, o); o += 1024
    KKbd_t = at("KKbd", [128, 4 * 128], BF16, o); o += 1024
    Zb_t = at("Zb", [128, 8 * 64], BF16, o); o += 1024
    assert o <= AVAIL, f"attn scratch {o} > {AVAIL}"

    psA = stack.enter_context(nc.psum_tensor("psA", [128, 1536], F32))
    psB = stack.enter_context(nc.psum_tensor("psB", [128, 1536], F32))
    pL = [stack.enter_context(nc.psum_tensor(f"pL{i}", [128, 512], F32))
          for i in range(2)]

    def bank(i):
        if i < 3:
            return psA[:, i * 512:(i + 1) * 512]
        if i < 6:
            return psB[:, (i - 3) * 512:(i - 3) * 512 + 512]
        return pL[i - 6][:]

    pB = [Buf(f"bank{i}") for i in range(8)]

    B = {}
    for name in ("xres", "xn", "xsq", "lnt", "m_row", "v_row", "r_row",
                 "updt", "alpha1", "b1s", "a2h", "b2h", "a2t", "consts",
                 "masks", "w1T", "ghT", "S", "C", "G", "S0bd", "C0bd",
                 "Atb", "Atr", "Aqs", "MtS", "Ybf", "W0b", "KKbd", "Zb",
                 "QTz", "KTz", "Aqb",
                 "QTg", "KTg", "OTg", "qn", "kn", "vn",
                 "wqs", "wks", "wvs", "wos", "w2t"):
        B[name] = Buf(name)
    w2s = [Buf("w2s0"), Buf("w2s1")]
    xst = [Buf("xst0"), Buf("xst1")]
    ost = [Buf("ost0"), Buf("ost1")]
    w1st = [Buf("w1st0"), Buf("w1st1")]
    s1st = [Buf("s1st0"), Buf("s1st1")]
    w2st = [Buf("w2st0"), Buf("w2st1")]
    s2st = [Buf("s2st0"), Buf("s2st1")]
    bst = Buf("bst")

    def _do(eng, fn, reads, writes, mark, dma=False):
        for r in reads:
            if isinstance(r, Buf):
                P.wait(eng, r.w)
            elif r is not None:
                P.wait(eng, r)
        for w in writes:
            for t in w.war():
                P.wait(eng, t)
        tok = P.emit(eng, fn, mark=mark, dma=dma)
        eff = tok if tok is not None else P.tok(eng)
        for r in reads:
            if isinstance(r, Buf):
                r.read(eff)
        for w in writes:
            w.wrote(tok)
        return tok

    def T_(fn, reads=(), writes=(), mark=True):
        return _do("T", fn, reads, writes, mark)

    def V_(fn, reads=(), writes=(), mark=True):
        return _do("V", fn, reads, writes, mark)

    def S_(fn, reads=(), writes=(), mark=True):
        return _do("S", fn, reads, writes, mark)

    def G_(fn, reads=(), writes=(), mark=True):
        return _do("G", fn, reads, writes, mark)

    def SE_(fn, reads=(), writes=(), mark=True):
        if not EV_S:
            return None
        return _do("S", fn, reads, writes, mark)

    def VE_(fn, reads=(), writes=(), mark=True):
        if not EV_V:
            return None
        return _do("V", fn, reads, writes, mark)

    def DMA(q, fn, reads=(), writes=()):
        return _do(q, fn, reads, writes, mark=False, dma=True)

    xres_v = xres.rearrange("p (j t) -> p j t", t=T)
    w2t_v = w2t_d.ap().rearrange("(c p) d -> p c d", p=128)

    def mrep(mask_t, n):
        return bass.AP(mask_t, 0, [[128, 128], [0, n], [1, 128]])

    def mrep2(mask_t, n1, n2):
        return bass.AP(mask_t, 0, [[128, 128], [0, n1], [0, n2], [1, 128]])

    V_(lambda e: e.memset(ones_col[:], 1.0), writes=[B["consts"]])
    V_(lambda e: e.memset(ones_row[:], 1.0), writes=[B["consts"]])
    V_(lambda e: e.memset(inv_c[:], 1.0 / 1024.0), writes=[B["consts"]])
    V_(lambda e: e.memset(cst[:, 0:1], 1e-5), writes=[B["consts"]])
    V_(lambda e: e.memset(cst[:, 1:2], 0.5), writes=[B["consts"]])
    V_(lambda e: e.memset(cst[:, 2:3], 0.5 / 4096.0), writes=[B["consts"]])
    V_(lambda e: e.memset(cst[:, 3:4], 0.0), writes=[B["consts"]])
    DMA("Y", lambda e: e.dma_start(triu[:], triu_d.ap()))
    DMA("Y", lambda e: e.dma_start(striu[:], striu_d.ap()))
    DMA("Y", lambda e: e.dma_start(bdm[:], bdm_d.ap()))
    DMA("Y", lambda e: e.dma_start(pm[:], pm_d.ap()))
    tid = DMA("Y", lambda e: e.dma_start(ident[:], ident_d.ap()))
    V_(lambda e: e.tensor_copy(identb[:], ident[:]),
       reads=[tid], writes=[B["masks"]])
    B["masks"].wrote(tid)

    xld_toks = []
    xg_toks = [[] for _ in range(NG)]
    for j in range(NJ):
        t_ = DMA("Y", lambda e, j=j: e.dma_start(
            xres[:, j * T: j * T + GB],
            x_d.ap()[j * 128:(j + 1) * 128, 0:GB]),
            writes=[B["xres"]] if j == 0 else [])
        xld_toks.append(t_)
        xg_toks[0].append(t_)

    def emit_x_rest():
        for g_ in range(1, NG):
            for j in range(NJ):
                t_ = DMA(("Y", "S")[j % 2], lambda e, j=j, g_=g_: e.dma_start(
                    xres[:, j * T + g_ * GB: j * T + (g_ + 1) * GB],
                    x_d.ap()[j * 128:(j + 1) * 128,
                             g_ * GB:(g_ + 1) * GB]), writes=[])
                xld_toks.append(t_)
                xg_toks[g_].append(t_)
        B["xres"].wrote(xld_toks)
    B["xres"].wrote(list(xld_toks))

    ld = DMA("Y", lambda e: e.dma_start(
        bstage[:], b1_d.ap().rearrange("(a b) -> a b", b=128)), writes=[bst])
    T_(lambda e: e.transpose(bank(4)[:, 0:32], bstage[:], ident[0:32, 0:32]),
       reads=[bst, ld, tid], writes=[pB[4]])
    ct = S_(lambda e: e.copy(b1s[:], bank(4)[:, 0:32]),
            reads=[pB[4]], writes=[B["b1s"]])
    pB[4].read(ct)
    bst.read(ct)
    ld = DMA("Y", lambda e: e.dma_start(
        bstage[0:8, :], b2_d.ap().rearrange("(a b) -> a b", b=128)),
        writes=[bst])
    T_(lambda e: e.transpose(bank(5)[:, 0:8], bstage[0:8, :], ident[0:8, 0:8]),
       reads=[bst, ld], writes=[pB[5]])
    ct = S_(lambda e: e.activation(b2h[:], bank(5)[:, 0:8], AF.Copy,
                                   scale=cst[:, 1:2]),
            reads=[pB[5], B["consts"]], writes=[B["b2h"]])
    pB[5].read(ct)
    bst.read(ct)


    def stage_w2_chunk(cc2):
        j, q4 = cc2 // 4, cc2 % 4
        b = cc2 % 2
        w2g, s2g = w2stage[b], s2stage[b]
        wb, sb_ = w2st[b], s2st[b]
        ld = DMA("S", lambda e, j=j, q4=q4, w2g=w2g: e.dma_start(
            w2g[:], w2_d.ap()[ts(j), q4 * 1024:(q4 + 1) * 1024]),
            writes=[wb])
        rt = V_(lambda e, j=j, q4=q4, w2g=w2g: e.reduce_sum(
            a2t[:, j * 4 + q4:j * 4 + q4 + 1], w2g[:], axis=AX.X,
            apply_absolute_value=True),
            reads=[wb, ld], writes=[B["a2t"]])
        st = S_(lambda e, w2g=w2g, s2g=s2g: e.activation(
            s2g[:], w2g[:], AF.Sign),
            reads=[wb, ld], writes=[sb_])
        dt = DMA("S", lambda e, j=j, q4=q4, s2g=s2g: e.dma_start(
            s2_d.ap()[ts(j), q4 * 1024:(q4 + 1) * 1024], s2g[:]),
            reads=[sb_])
        sb_.read(dt)
        wb.read(rt)
        wb.read(st)

    def emit_a2h_tail():
        for j in range(NJ):
            V_(lambda e, j=j: e.tensor_tensor(
                a2t[:, j * 4:j * 4 + 1], a2t[:, j * 4:j * 4 + 1],
                a2t[:, j * 4 + 1:j * 4 + 2], OP.add),
               reads=[B["a2t"]], writes=[B["a2t"]])
        for j in range(NJ):
            V_(lambda e, j=j: e.tensor_tensor(
                a2t[:, j * 4 + 2:j * 4 + 3], a2t[:, j * 4 + 2:j * 4 + 3],
                a2t[:, j * 4 + 3:j * 4 + 4], OP.add),
               reads=[B["a2t"]], writes=[B["a2t"]])
        for j in range(NJ):
            V_(lambda e, j=j: e.tensor_tensor(
                a2h[:, j:j + 1], a2t[:, j * 4:j * 4 + 1],
                a2t[:, j * 4 + 2:j * 4 + 3], OP.add),
               reads=[B["a2t"]], writes=[])
        B["a2h"].wrote(P.tok("V"))
        S_(lambda e: e.activation(a2h[:], a2h[:], AF.Identity,
                                  scale=cst[:, 2:3], bias=cst[:, 3:4]),
           reads=[B["a2h"], B["consts"]], writes=[B["a2h"]])
        if debug and "tA2h" in dbg_d:
            dta = DMA("G", lambda e: e.dma_start(dbg_d["tA2h"].ap(), a2h[:]),
                      reads=[B["a2h"]])
            B["a2h"].read(dta)
            dtb = DMA("G", lambda e: e.dma_start(dbg_d["tA2t"].ap(), a2t[:]),
                      reads=[B["a2t"]])
            B["a2t"].read(dtb)

    def ln_stats(g):
        col0 = g * GB
        p6, p7 = bank(6), bank(7)
        for j in range(NJ):
            xs = xres[:, j * T + col0: j * T + col0 + GB]
            S_(lambda e, xs=xs: e.square(xsq[:], xs),
               reads=[xg_toks[g][j]], writes=[B["xsq"]])
            T_(lambda e, xs=xs, j=j: e.matmul(
                p6[0:1, :], ones_col[:], xs,
                start=(j == 0), stop=(j == NJ - 1)),
               reads=[B["consts"], xg_toks[g][j]],
               writes=[pB[6]] if j == 0 else [], mark=(j == NJ - 1))
            tq = T_(lambda e, j=j: e.matmul(
                p7[0:1, :], ones_col[:], xsq[:],
                start=(j == 0), stop=(j == NJ - 1)),
                reads=[B["xsq"], B["consts"]],
                writes=[pB[7]] if j == 0 else [], mark=True)
            B["xsq"].read(tq)
        B["xres"].read(P.tok("S"))
        B["xres"].read(P.tok("T"))
        pB[6].wrote(P.tok("T"))
        pB[7].wrote(P.tok("T"))
        mt = V_(lambda e: e.tensor_scalar_mul(
            m_row, p6[0:1, :], inv_c[0:1, 0:1]),
            reads=[pB[6], B["consts"]], writes=[B["m_row"]])
        pB[6].read(mt)
        vt = V_(lambda e: e.tensor_scalar_mul(
            v_row, p7[0:1, :], inv_c[0:1, 0:1]),
            reads=[pB[7], B["consts"]], writes=[B["v_row"]])
        pB[7].read(vt)
        V_(lambda e: e.scalar_tensor_tensor(
            r_row, m_row, -1.0, m_row, OP.mult, OP.mult),
           reads=[B["m_row"]], writes=[B["r_row"]])
        V_(lambda e: e.tensor_tensor(v_row, v_row, r_row, OP.add),
           reads=[B["r_row"], B["v_row"]], writes=[B["v_row"]])
        S_(lambda e: e.activation(v_row, v_row, AF.Sqrt,
                                  bias=cst[0:1, 0:1]),
           reads=[B["v_row"], B["consts"]], writes=[B["v_row"]])
        rc = V_(lambda e: e.reciprocal(r_row, v_row),
                reads=[B["v_row"]], writes=[B["r_row"]])
        B["updt"].read(rc)

    xnj_toks = [None] * NJ

    def ln_apply(g, xn_war_tok=None):
        col0 = g * GB
        p6, p7 = bank(6), bank(7)
        tm = T_(lambda e: e.matmul(p6[:], ones_row[:], m_row,
                                   start=True, stop=True),
                reads=[B["m_row"], B["consts"]], writes=[pB[6]])
        B["lnt"].read(tm)
        tr = T_(lambda e: e.matmul(p7[:], ones_row[:], r_row,
                                   start=True, stop=True),
                reads=[B["r_row"], B["consts"]], writes=[pB[7]])
        B["xsq"].read(tr)
        for j in range(NJ):
            xs = xres[:, j * T + col0: j * T + col0 + GB]
            V_(lambda e, xs=xs: e.tensor_tensor(lnt[:], xs, p6[:],
                                                OP.subtract),
               reads=[xg_toks[g][j], pB[6]], writes=[B["lnt"]])
            ex = [xn_war_tok] if (j == 0 and xn_war_tok is not None) else []
            vt = V_(lambda e, j=j: e.tensor_tensor(
                xn[:, j * GB:(j + 1) * GB], lnt[:], p7[:], OP.mult),
                reads=[B["lnt"], pB[7]] + ex,
                writes=[B["xn"]] if j == 0 else [])
            xnj_toks[j] = vt
            pB[6].read(vt)
            pB[7].read(vt)
        B["xn"].wrote(P.tok("V"))
        B["xres"].read(P.tok("V"))

    def layernorm_group(g, xn_war_tok=None):
        ln_stats(g)
        ln_apply(g, xn_war_tok=xn_war_tok)

    def ffn1_chunk(c, w1gate=None):
        bi = c % 2
        pt = bank(bi)
        pb = pB[bi]
        for j in range(NJ):
            lhs = w1T_t[:, j * FF + c * 128: j * FF + c * 128 + 128]
            rhs = xn[:, j * GB:(j + 1) * GB]
            T_(lambda e, pt=pt, lhs=lhs, rhs=rhs, j=j: e.matmul(
                pt, lhs, rhs, start=(j == 0), stop=(j == NJ - 1)),
               reads=[xnj_toks[j]]
               + ([w1gate if w1gate is not None else B["w1T"]]
                  if j == 0 else []),
               writes=([pb] if j == 0 else []), mark=(j == NJ - 1))
        pb.wrote(P.tok("T"))
        st = S_(lambda e, pt=pt, c=c: e.activation(
            ghT_t[:, c * GB:(c + 1) * GB], pt, AF.Gelu,
            scale=alpha1[:, c:c + 1], bias=b1s[:, c:c + 1]),
            reads=[pb, B["alpha1"], B["b1s"]],
            writes=[B["ghT"]] if c == 0 else [])
        pb.read(st)

    def ffn_sublayer(first_call, skip_g0_head=False):
        if not first_call:
            rl = []
            for hh in range(2):
                rl.append(DMA(("Y", "G")[hh], lambda e, hh=hh: e.dma_start(
                    w1T_t[:, hh * 4 * FF:(hh + 1) * 4 * FF],
                    w1t_d.ap()[:, hh * 4 * FF:(hh + 1) * 4 * FF]),
                    reads=[w1t_stok] if hh == 0 else [],
                    writes=[B["w1T"]] if hh == 0 else []))
            B["w1T"].wrote(rl)
        if debug and first_call and "tw1T" in dbg_d:
            dtw = DMA("G", lambda e: e.dma_start(dbg_d["tw1T"].ap(), w1T_t[:]),
                      reads=[B["w1T"]])
            B["w1T"].read(dtw)

        if not skip_g0_head:
            layernorm_group(0, xn_war_tok=P.tok("T"))
        for g in range(NG):
            if not (skip_g0_head and g == 0):
                for c in range(NF):
                    ffn1_chunk(c)
                B["ghT"].wrote(P.tok("S"))
            if debug and first_call and g == 0 and "tghT" in dbg_d:
                dtg = DMA("G", lambda e: e.dma_start(dbg_d["tghT"].ap(),
                                                     ghT_t[:]),
                          reads=[B["ghT"]])
                B["ghT"].read(dtg)

            if g + 1 < NG:
                ln_stats(g + 1)

            for w in range(2):
                for c in range(NF):
                    slot = (c // 4) % 2
                    ci = c % 4
                    sb = w2s[slot]
                    if ci == 0:
                        if first_call and g == 0:
                            q = "Y" if slot == 0 else "S"
                            tl_toks = []
                            for cc in range(4):
                                src = s2_d.ap()[w * 512:(w + 1) * 512,
                                                (c + cc) * 128:(c + cc + 1) * 128]
                                d1 = w2s_t[:, slot * 2048 + cc * 512:
                                           slot * 2048 + cc * 512 + 512]
                                tl_toks.append(DMA(
                                    q, lambda e, d1=d1, src=src:
                                    e.dma_start_transpose(d1, src),
                                    reads=([s2_done, B["a2h"], B["w1T"]]
                                           if cc == 0 else []),
                                    writes=[sb]))
                            sb.wrote(tl_toks)
                            if debug and ((w == 0 and c == 0) or
                                          (w == 1 and c == 28)) and \
                                    "tW20" in dbg_d:
                                nmw = "tW20" if w == 0 else "tW27"
                                dtw2 = DMA("S", lambda e, nmw=nmw, slot=slot:
                                           e.dma_start(
                                               dbg_d[nmw].ap(),
                                               w2s_t[:, slot * 2048:
                                                     slot * 2048 + 2048]),
                                           reads=[sb])
                                sb.read(dtw2)
                            wdst = w2t_v[:, c:c + 4, w * 512:(w + 1) * 512]
                            wsrc = w2s_t.rearrange(
                                "p (s c dd) -> p s c dd", s=2, dd=512)[
                                :, slot, :, :]
                            dt = DMA("G", lambda e, wdst=wdst, wsrc=wsrc:
                                     e.dma_start(wdst, wsrc), reads=[sb])
                            sb.read(dt)
                        else:
                            q = "Y" if slot == 0 else "S"
                            src = w2t_v[:, c:c + 4, w * 512:(w + 1) * 512]
                            dst = w2s_t.rearrange(
                                "p (s c dd) -> p s c dd", s=2, dd=512)[
                                :, slot, :, :]
                            DMA(q, lambda e, dst=dst, src=src:
                                e.dma_start(dst, src),
                                reads=[B["w2t"]], writes=[sb])
                    for ji in range(4):
                        j = 4 * w + ji
                        bi = 2 + ji
                        pt = bank(bi)
                        pb = pB[bi]
                        lhs = w2s_t[:, slot * 2048 + ci * 512 + ji * 128:
                                    slot * 2048 + ci * 512 + ji * 128 + 128]
                        rhs = ghT_t[:, c * GB:(c + 1) * GB]
                        T_(lambda e, pt=pt, lhs=lhs, rhs=rhs, c=c: e.matmul(
                            pt, lhs, rhs, start=(c == 0), stop=(c == NF - 1)),
                           reads=([sb, B["ghT"]] if ji == 0 else [sb]),
                           writes=([pb] if c == 0 else []),
                           mark=(ji == 3))
                ytok = P.tok("T")
                B["ghT"].read(ytok)
                for ji in range(4):
                    pB[2 + ji].wrote(ytok)
                for ji in range(4):
                    j = 4 * w + ji
                    pt = bank(2 + ji)
                    pb = pB[2 + ji]
                    st = S_(lambda e, pt=pt, j=j: e.activation(
                        updt[:], pt, AF.Identity,
                        scale=a2h[:, j:j + 1], bias=b2h[:, j:j + 1]),
                        reads=[pb, B["a2h"], B["b2h"]], writes=[B["updt"]])
                    pb.read(st)
                    if debug and first_call and g == 0 and \
                            ((w == 0 and ji == 0) or (w == 1 and ji == 3)) \
                            and "tF0" in dbg_d:
                        nmf = "tF0" if w == 0 else "tF7"
                        dtf = DMA("G", lambda e, nmf=nmf: e.dma_start(
                            dbg_d[nmf].ap(), updt[:]), reads=[B["updt"]])
                        B["updt"].read(dtf)
                    xs = xres[:, j * T + g * GB: j * T + g * GB + GB]
                    vt = V_(lambda e, xs=xs: e.tensor_tensor(
                        xs, xs, updt[:], OP.add),
                        reads=[B["updt"]], writes=[B["xres"]])
                    B["updt"].read(vt)
                if w == 0 and g + 1 < NG:
                    ln_apply(g + 1, xn_war_tok=P.tok("T"))
            if first_call and g == 0:
                B["w2t"].wrote(P.qdone("G"))

    layernorm_group(0, xn_war_tok=P.tok("T"))
    w1ctoks = []
    for c in range(NF + 1):
        if c < NF:
            b = c % 2
            w1g, s1g = w1stage[b], s1stage[b]
            wb, sb_ = w1st[b], s1st[b]
            ld = DMA("Y", lambda e, c=c, w1g=w1g: e.dma_start(
                w1g[:], w1_d.ap()[ts(c), :]), writes=[wb])
            rt = V_(lambda e, c=c, w1g=w1g: e.reduce_sum(
                alpha1[:, c:c + 1], w1g[:], axis=AX.X,
                apply_absolute_value=True),
                reads=[wb, ld], writes=[])
            S_(lambda e, c=c: e.activation(
                alpha1[:, c:c + 1], alpha1[:, c:c + 1], AF.Identity,
                scale=inv_c[:, 0:1], bias=cst[:, 3:4]),
               reads=[rt, B["consts"]], writes=[])
            st = S_(lambda e, w1g=w1g, s1g=s1g: e.activation(
                s1g[:], w1g[:], AF.Sign),
                reads=[wb, ld], writes=[sb_])
            wb.read(rt)
            wb.read(st)
            for j in range(NJ):
                half = j // 4
                pt2 = (psA if half == 0 else psB)[:, 1024 + (j % 4) * 128:
                                                  1024 + (j % 4) * 128 + 128]
                pbh = pB[2] if half == 0 else pB[5]
                T_(lambda e, pt2=pt2, s1g=s1g, j=j: e.transpose(
                    pt2, s1g[:, ts(j)], ident[:]),
                   reads=([sb_, tid] if j == 0 else []),
                   writes=([pbh] if j % 4 == 0 else []),
                   mark=(j % 4 == 3))
            ttok = P.tok("T")
            pB[2].wrote(ttok)
            pB[5].wrote(ttok)
            sb_.read(ttok)
            ctl = None
            for half, pst in ((0, psA), (1, psB)):
                w1dst = bass.AP(w1T_t, half * 4 * FF + c * 128,
                                [[NJ * FF, 128], [FF, 4], [1, 128]])
                psrc_ = bass.AP(pst.ap().tensor, pst.ap().offset + 1024,
                                [[1536, 128], [128, 4], [1, 128]])
                pbh = pB[2] if half == 0 else pB[5]
                ctl = V_(lambda e, w1dst=w1dst, psrc_=psrc_: e.tensor_copy(
                    w1dst, psrc_), reads=[pbh], writes=[B["w1T"]])
                pbh.read(ctl)
            w1ctoks.append(P.tok("V"))
            stage_w2_chunk(c)
        if c >= 1:
            ffn1_chunk(c - 1, w1gate=w1ctoks[c - 1])
    B["ghT"].wrote(P.tok("S"))
    B["alpha1"].wrote(P.tok("S"))
    s2_done = P.qdone("S")
    emit_x_rest()
    emit_a2h_tail()
    s1_done = P.qdone("Y")

    ffn_sublayer(first_call=True, skip_g0_head=True)
    if debug:
        dt = DMA("G", lambda e: e.dma_start(dbg_d["dbg1"].ap(), xres[:]),
                 reads=[B["xres"]])
        B["xres"].read(dt)

    P.barrier()

    w1t_stok = DMA("G", lambda e: e.dma_start(w1t_d.ap(), w1T_t[:]),
                   reads=[B["w1T"]])
    B["w1T"].read(w1t_stok)

    for nm, srcd, dstt in (("wqs", wq_d, wqs_t), ("wks", wk_d, wks_t),
                           ("wvs", wv_d, wvs_t), ("wos", wo_d, wos_t)):
        src_v = srcd.ap().rearrange("(j p) e -> p j e", p=128)
        DMA("G", lambda e, dstt=dstt, src_v=src_v: e.dma_start(dstt[:], src_v),
            writes=[B[nm]])

    for _pi, _pt in ((0, psA), (1, psB)):
        vt = V_(lambda e, _pt=_pt: e.memset(_pt[:], 0.0),
                writes=[pB[3 * _pi], pB[3 * _pi + 1], pB[3 * _pi + 2]])
        for _b in range(3):
            pB[3 * _pi + _b].wrote(vt)
    V_(lambda e: e.memset(Sp_t[:], 0.0), writes=[B["S"]])
    V_(lambda e: e.memset(Cp_t[:], 0.0), writes=[B["C"]])
    V_(lambda e: e.memset(Gp_t[:], 0.0), writes=[B["G"]])

    def proj_group_T(g, Wbuf, Wt, dstbuf, dst_t):
        for jE in range(8):
            bi = jE % 4
            pt = bank(bi)
            pb = pB[bi]
            for j in range(NJ):
                lhs = Wt[:, j * D + jE * 128: j * D + jE * 128 + 128]
                rhs = xn[:, j * GB:(j + 1) * GB]
                T_(lambda e, pt=pt, lhs=lhs, rhs=rhs, j=j: e.matmul(
                    pt, lhs, rhs, start=(j == 0), stop=(j == NJ - 1)),
                   reads=[xnj_toks[j]] + ([Wbuf] if j == 0 else []),
                   writes=([pb] if j == 0 else []), mark=(j == NJ - 1))
            pb.wrote(P.tok("T"))
            if jE % 2 == 0:
                st = S_(lambda e, pt=pt, jE=jE: e.copy(
                    dst_t[:, jE * GB:(jE + 1) * GB], pt),
                    reads=[pb], writes=[dstbuf] if jE == 0 else [])
            else:
                st = V_(lambda e, pt=pt, jE=jE: e.tensor_copy(
                    dst_t[:, jE * GB:(jE + 1) * GB], pt),
                    reads=[pb], writes=[dstbuf] if jE == 0 else [])
            pb.read(st)
        dstbuf.wrote(P.tok("S"))
        dstbuf.wrote(P.tok("V"))

    def proj_chunk_nat(c, Wbuf, Wt, dstbuf, dst_t):
        tc = c % CPG
        for h2 in range(2):
            bi = h2
            pt = bank(bi)
            pb = pB[bi]
            for j in range(NJ):
                lhs = xn[:, j * GB + tc * 128: j * GB + tc * 128 + 128]
                rhs = Wt[:, j * D + h2 * 512: j * D + h2 * 512 + 512]
                T_(lambda e, pt=pt, lhs=lhs, rhs=rhs, j=j: e.matmul(
                    pt, lhs, rhs, start=(j == 0), stop=(j == NJ - 1)),
                   reads=[xnj_toks[j]] + ([Wbuf] if j == 0 else []),
                   writes=([pb] if j == 0 else []), mark=(j == NJ - 1))
            pb.wrote(P.tok("T"))
            st = S_(lambda e, pt=pt, h2=h2: e.copy(
                dst_t[:, h2 * 512:(h2 + 1) * 512], pt),
                reads=[pb], writes=[dstbuf] if h2 == 0 else [])
            pb.read(st)
        dstbuf.wrote(P.tok("S"))

    def evac_grid8(dst_t, set3, cw, engine_op, reads_extra=(), out_bf=None,
                   csub=128):
        base = psA if set3 == "A" else psB
        ops = []
        src1 = bass.AP(base.ap().tensor, base.ap().offset,
                       [[1536, 128], [512, 3], [128, 2], [1, csub]])
        dst1 = bass.AP(dst_t, 0, [[8 * cw, 128], [cw, 3], [3 * cw, 2], [1, cw]])
        src2 = bass.AP(base.ap().tensor, base.ap().offset + 256,
                       [[1536, 128], [512, 2], [1, csub]])
        dst2 = bass.AP(dst_t, 6 * cw, [[8 * cw, 128], [cw, 2], [1, cw]])
        return (dst1, src1), (dst2, src2)

    rot = {"A": 0, "B": 0}

    def sect(setn, ncells, cellw=128):
        r = rot[setn]
        rot[setn] = (r + ncells) % 3
        base = 0 if setn == "A" else 3
        cells = []
        for l in range(ncells):
            bl = (r + l) % 3
            bi = base + bl
            col = (l // 3) * cellw
            cells.append((bi, col, l < 3, l >= ncells - 3))
        evacs = []
        for bl in range(3):
            l0 = (bl - r) % 3
            if l0 >= ncells:
                continue
            nb = (ncells - l0 + 2) // 3
            evacs.append((base + bl, l0, nb))
        return cells, evacs

    def psrc(bi, colw, nb, subw):
        tens = psA if bi < 3 else psB
        b = bi % 3
        return bass.AP(tens.ap().tensor, tens.ap().offset + b * 512,
                       [[1536, 128], [colw, nb], [1, subw]])

    def ddst(dst_t, stride_row, l0, nb, cw, subw):
        return bass.AP(dst_t, l0 * cw,
                       [[stride_row, 128], [3 * cw, nb], [1, subw]])

    def pcell(bi, col, subw):
        tens = psA if bi < 3 else psB
        b = bi % 3
        return bass.AP(tens.ap().tensor, tens.ap().offset + b * 512 + col,
                       [[1536, 128], [1, subw]])

    def pbank2(bi, width):
        tens = psA if bi < 3 else psB
        b = bi % 3
        return bass.AP(tens.ap().tensor, tens.ap().offset + b * 512,
                       [[1536, 128], [1, width]])

    def slot3(dst_t, row_stride, l0, nb, cw=128):
        return bass.AP(dst_t, l0 * cw, [[row_stride, 128], [3 * cw, nb],
                                        [1, cw]])

    def proj_t2n(c, srcT_t, srcbuf, dstbuf, dst_t, setn):
        tc = c % CPG
        cells, evacs = sect(setn, 8)
        for l in range(8):
            bi, col, st_, sp_ = cells[l]
            cell = bank(bi)[:, col:col + 128]
            srcs = srcT_t[:, l * GB + tc * 128: l * GB + tc * 128 + 128]
            T_(lambda e, cell=cell, srcs=srcs: e.matmul(
                cell, srcs, identb[:], start=True, stop=True,
                skip_group_check=True),
               reads=([srcbuf, B["masks"]] if l == 0 else []),
               writes=[pB[bi]] if st_ else [], mark=sp_)
        ttok = P.tok("T")
        for bi, l0, nb in evacs:
            pB[bi].wrote(ttok)
        for _k, (bi, l0, nb) in enumerate(evacs):
            sap = psrc(bi, 128, nb, 128)
            dap = ddst(dst_t, 1024, l0, nb, 128, 128)
            SE_(lambda e, dap=dap, sap=sap: e.copy(dap, sap),
               reads=[pB[bi]], writes=[dstbuf] if _k == 0 else [])
        dstbuf.wrote(P.tok("S"))
        for bi, l0, nb in evacs:
            pB[bi].read(P.tok("S"))

    def scan_chunk(c):
        tc = c % CPG
        first = (c == 0)
        if not first:
            vt = VE_(lambda e: e.tensor_tensor(
                S0bd_t.rearrange("p (r c) -> p r c", c=128),
                Sp_t.rearrange("p (r c) -> p r c", c=128),
                mrep(bdm, 8), OP.mult),
                reads=[B["S"], B["masks"]], writes=[B["S0bd"]])
            B["S"].read(vt)
            vt = VE_(lambda e: e.tensor_tensor(
                C0bd_t.rearrange("p (r c) -> p r c", c=128),
                Cp_t.rearrange("p (r c) -> p r c", c=128),
                mrep(bdm, 8), OP.mult),
                reads=[B["C"], B["masks"]], writes=[B["C0bd"]])
            B["C"].read(vt)
        def emit_s0(w):
            if sgate & 1024:
                src3q = bass.AP(QTg_t, (4 * w) * GB + tc * 128,
                                [[8 * GB, 128], [GB, 4], [1, 128]])
                src3k = bass.AP(KTg_t, (4 * w) * GB + tc * 128,
                                [[8 * GB, 128], [GB, 4], [1, 128]])
                for qq in range(2):
                    pm3 = bass.AP(pm, qq * 128, [[256, 128], [0, 4], [1, 128]])
                    qdst = bass.AP(QTz_t, qq * 128, [[1024, 128], [256, 4], [1, 128]])
                    kdst = bass.AP(KTz_t, qq * 128, [[1024, 128], [256, 4], [1, 128]])
                    V_(lambda e, qdst=qdst, src3q=src3q, pm3=pm3: e.tensor_tensor(
                        qdst, src3q, pm3, OP.mult),
                       reads=[B["QTg"], B["masks"]], writes=[B["QTz"]])
                    V_(lambda e, kdst=kdst, src3k=src3k, pm3=pm3: e.tensor_tensor(
                        kdst, src3k, pm3, OP.mult),
                       reads=[B["KTg"], B["masks"]], writes=[B["KTz"]])

        emit_s0(0)
        for w in range(2):

            cells, evacs = sect("B", 4)
            for u in (range(4) if sgate & 128 else []):
                p = 4 * w + u
                bi, col, st_, sp_ = cells[u]
                cell = bank(bi)[:, col:col + 128]
                kns = kn_t[:, p * 128:(p + 1) * 128]
                T_(lambda e, cell=cell, kns=kns, st_=st_, sp_=sp_: e.matmul(
                    cell, kns, kns, start=st_, stop=sp_,
                    skip_group_check=True),
                   reads=([B["kn"]] if u == 0 else []),
                   writes=[pB[bi]] if st_ else [], mark=sp_)
            for bi, l0, nb in evacs:
                pB[bi].wrote(P.tok("T"))
            for _k, (bi, l0, nb) in enumerate(
                    evacs if (sgate & 128 and sgate & 131072) else []):
                VE_(lambda e, bi=bi, l0=l0, nb=nb: e.tensor_tensor(
                    slot3(KKbd_t, 512, l0, nb), pbank2(bi, nb * 128),
                    mrep(bdm, nb), OP.mult),
                   reads=[pB[bi]] + ([B["masks"]] if _k == 0 else []),
                   writes=[B["KKbd"]] if _k == 0 else [])
                sdst = bass.AP(Sp_t, (4 * w + l0) * 128,
                               [[1024, 128], [3 * 128, nb], [1, 128]])
                VE_(lambda e, bi=bi, nb=nb, sdst=sdst: e.tensor_tensor(
                    sdst, sdst, pbank2(bi, nb * 128), OP.add),
                   reads=[pB[bi]], writes=[B["S"]] if _k == 0 else [])
            B["KKbd"].wrote(P.tok("V"))
            B["S"].wrote(P.tok("V"))
            for bi, l0, nb in evacs:
                pB[bi].read(P.tok("V"))

            cells, evacs = sect("A", 4)
            for u in (range(4) if sgate & 256 else []):
                p = 4 * w + u
                bi, col, st_, sp_ = cells[u]
                cell = bank(bi)[:, col:col + 128]
                qns = qn_t[:, p * 128:(p + 1) * 128]
                vns = vn_t[:, p * 128:(p + 1) * 128]
                T_(lambda e, cell=cell, qns=qns, vns=vns, st_=st_, sp_=sp_:
                   e.matmul(cell, qns, vns, start=st_, stop=sp_,
                            skip_group_check=True),
                   reads=([B["qn"]] if u == 0 else []),
                   writes=[pB[bi]] if st_ else [], mark=sp_)
            for bi, l0, nb in evacs:
                pB[bi].wrote(P.tok("T"))
            for _k, (bi, l0, nb) in enumerate(
                    evacs if (sgate & 256 and sgate & 262144) else []):
                cdst = bass.AP(Cp_t, (4 * w + l0) * 128,
                               [[1024, 128], [3 * 128, nb], [1, 128]])
                VE_(lambda e, bi=bi, nb=nb, cdst=cdst: e.tensor_tensor(
                    cdst, cdst, pbank2(bi, nb * 128), OP.add),
                   reads=[pB[bi]], writes=[B["C"]] if _k == 0 else [])
            B["C"].wrote(P.tok("V"))
            for bi, l0, nb in evacs:
                pB[bi].read(P.tok("V"))

            cells, evacs = sect("A", 8)
            for l in (range(8) if sgate & 1 else []):
                h = 8 * w + l
                p = h // 2
                bi, col, st_, sp_ = cells[l]
                cell = bank(bi)[:, col:col + 128]
                QTc = QTg_t[:, p * GB + tc * 128: p * GB + tc * 128 + 128]
                KTzs = KTz_t[:, ts(l)]
                T_(lambda e, cell=cell, KTzs=KTzs, QTc=QTc, st_=st_, sp_=sp_:
                   e.matmul(cell, KTzs, QTc, start=st_, stop=sp_,
                            skip_group_check=True),
                   reads=([B["QTg"], B["KTz"]] if l == 0 else []),
                   writes=[pB[bi]] if st_ else [], mark=sp_)
            at_tok = P.tok("T")
            for bi, l0, nb in evacs:
                pB[bi].wrote(at_tok)
            for _k, (bi, l0, nb) in enumerate(evacs):
                sap = psrc(bi, 128, nb, 128)
                dap = ddst(Atb_t, 1024, l0, nb, 128, 128)
                SE_(lambda e, dap=dap, sap=sap: e.copy(dap, sap),
                   reads=[pB[bi]], writes=[B["Atb"]] if _k == 0 else [])
            B["Atb"].wrote(P.tok("S"))
            for bi, l0, nb in evacs:
                pB[bi].read(P.tok("S"))
            for _k, (bi, l0, nb) in enumerate(
                    evacs if (sgate & 1 and sgate & 8192) else []):
                VE_(lambda e, bi=bi, l0=l0, nb=nb: e.tensor_tensor(
                    slot3(Atr_t, 1024, l0, nb), pbank2(bi, nb * 128),
                    mrep(triu, nb), OP.mult),
                   reads=[pB[bi]] + ([B["masks"], B["Atb"]] if _k == 0
                                     else []),
                   writes=[B["Atr"]] if _k == 0 else [])
            B["Atr"].wrote(P.tok("V"))
            for bi, l0, nb in evacs:
                pB[bi].read(P.tok("V"))

            cells, evacs = sect("B", 8)
            for l in (range(8) if sgate & 2 else []):
                h = 8 * w + l
                p = h // 2
                bi, col, st_, sp_ = cells[l]
                cell = bank(bi)[:, col:col + 128]
                KTc = KTg_t[:, p * GB + tc * 128: p * GB + tc * 128 + 128]
                QTzs = QTz_t[:, ts(l)]
                T_(lambda e, cell=cell, QTzs=QTzs, KTc=KTc, st_=st_, sp_=sp_:
                   e.matmul(cell, QTzs, KTc, start=st_, stop=sp_,
                            skip_group_check=True),
                   reads=([B["QTz"], B["KTg"]] if l == 0 else []),
                   writes=[pB[bi]] if st_ else [], mark=sp_)
            for bi, l0, nb in evacs:
                pB[bi].wrote(P.tok("T"))
            for _k, (bi, l0, nb) in enumerate(
                    evacs if (sgate & 2 and sgate & 16384) else []):
                VE_(lambda e, bi=bi, l0=l0, nb=nb: e.tensor_tensor(
                    slot3(Aqs_t, 1024, l0, nb), pbank2(bi, nb * 128),
                    mrep(striu, nb), OP.mult),
                   reads=[pB[bi]] + ([B["masks"]] if _k == 0 else []),
                   writes=[B["Aqs"]] if _k == 0 else [])
            B["Aqs"].wrote(P.tok("V"))
            for bi, l0, nb in evacs:
                pB[bi].read(P.tok("V"))

            if (not first) and (sgate & 4):
                cells, evacs = sect("A", 4)
                for u in range(4):
                    p = 4 * w + u
                    bi, col, st_, sp_ = cells[u]
                    cell = bank(bi)[:, col:col + 128]
                    S0s = S0bd_t[:, p * 128:(p + 1) * 128]
                    QTc = QTg_t[:, p * GB + tc * 128: p * GB + tc * 128 + 128]
                    T_(lambda e, cell=cell, S0s=S0s, QTc=QTc, st_=st_,
                       sp_=sp_: e.matmul(cell, S0s, QTc, start=st_, stop=sp_,
                                         skip_group_check=True),
                       reads=([B["S0bd"]] if u == 0 else []),
                       writes=[pB[bi]] if st_ else [], mark=sp_)
                for bi, l0, nb in evacs:
                    pB[bi].wrote(P.tok("T"))
                for _k, (bi, l0, nb) in enumerate(evacs):
                    sap = psrc(bi, 128, nb, 128)
                    dap = ddst(Ybf_t, 512, l0, nb, 128, 128)
                    SE_(lambda e, dap=dap, sap=sap: e.copy(dap, sap),
                       reads=[pB[bi]], writes=[B["Ybf"]] if _k == 0 else [])
                B["Ybf"].wrote(P.tok("S"))
                for bi, l0, nb in evacs:
                    pB[bi].read(P.tok("S"))
                cells, evacs = sect("B", 4)
                for u in range(4):
                    p = 4 * w + u
                    bi, col, st_, sp_ = cells[u]
                    cell = bank(bi)[:, col:col + 128]
                    S0s = S0bd_t[:, p * 128:(p + 1) * 128]
                    C0s = C0bd_t[:, p * 128:(p + 1) * 128]
                    T_(lambda e, cell=cell, S0s=S0s, C0s=C0s, st_=st_,
                       sp_=sp_: e.matmul(cell, S0s, C0s, start=st_, stop=sp_,
                                         skip_group_check=True),
                       reads=([B["C0bd"]] if u == 0 else []),
                       writes=[pB[bi]] if st_ else [], mark=sp_)
                for bi, l0, nb in evacs:
                    pB[bi].wrote(P.tok("T"))
                for _k, (bi, l0, nb) in enumerate(
                        evacs if sgate & 32768 else []):
                    gsrc = bass.AP(Gp_t, (4 * w + l0) * 128,
                                   [[1024, 128], [3 * 128, nb], [1, 128]])
                    VE_(lambda e, bi=bi, l0=l0, nb=nb, gsrc=gsrc:
                        e.tensor_tensor(
                            slot3(W0b_t, 512, l0, nb), pbank2(bi, nb * 128),
                            gsrc, OP.subtract),
                       reads=[pB[bi]] + ([B["G"]] if _k == 0 else []),
                       writes=[B["W0b"]] if _k == 0 else [])
                B["W0b"].wrote(P.tok("V"))
                B["G"].read(P.tok("V"))
                for bi, l0, nb in evacs:
                    pB[bi].read(P.tok("V"))

            cellsM, evacsM = sect("A", 8)
            for l in (range(8) if sgate & 8 else []):
                bi, col, st_, sp_ = cellsM[l]
                cell = bank(bi)[:, col:col + 128]
                Atrs = Atr_t[:, ts(l)]
                Atbs = Atb_t[:, ts(l)]
                T_(lambda e, cell=cell, Atrs=Atrs, Atbs=Atbs, st_=st_,
                   sp_=sp_, first=first: e.matmul(
                       cell, Atrs, Atbs, start=st_,
                       stop=(first and sp_), skip_group_check=True),
                   reads=([B["Atr"], B["Atb"]] if l == 0 else []),
                   writes=[pB[bi]] if st_ else [], mark=(first and l == 7))
            cellsZ, evacsZ = sect("B", 8)
            for l in (range(8) if sgate & 16 else []):
                h = 8 * w + l
                Aqss = Aqs_t[:, ts(l)]
                vns = vn_t[:, h * 64: h * 64 + 64]
                bi, col, st_, sp_ = cellsZ[l]
                cell = bank(bi)[:, col:col + 64]
                T_(lambda e, cell=cell, Aqss=Aqss, vns=vns, st_=st_, sp_=sp_:
                   e.matmul(cell, Aqss, vns, start=st_, stop=sp_,
                            skip_group_check=True),
                   reads=([B["Aqs"], B["vn"]] if l == 0 else []),
                   writes=[pB[bi]] if st_ else [], mark=sp_)
            for bi, l0, nb in evacsZ:
                pB[bi].wrote(P.tok("T"))
            for _k, (bi, l0, nb) in enumerate(evacsZ):
                sap = psrc(bi, 128, nb, 64)
                dap = ddst(Zb_t, 512, l0, nb, 64, 64)
                SE_(lambda e, dap=dap, sap=sap: e.copy(dap, sap),
                   reads=[pB[bi]], writes=[B["Zb"]] if _k == 0 else [])
            B["Zb"].wrote(P.tok("S"))
            for bi, l0, nb in evacsZ:
                pB[bi].read(P.tok("S"))
            if (not first) and (sgate & 32):
                for l in range(8):
                    u = l // 2
                    bi, col, st_, sp_ = cellsM[l]
                    cell = bank(bi)[:, col:col + 128]
                    QTzs = QTz_t[:, ts(l)]
                    Ys = Ybf_t[:, u * 128: u * 128 + 128]
                    T_(lambda e, cell=cell, QTzs=QTzs, Ys=Ys, sp_=sp_:
                       e.matmul(cell, QTzs, Ys, start=False, stop=sp_,
                                skip_group_check=True),
                       reads=([B["Ybf"], B["QTz"]] if l == 0 else []),
                       writes=[], mark=(l == 7))
            mt_tok = P.tok("T")
            for bi, l0, nb in evacsM:
                pB[bi].wrote(mt_tok)
            for _k, (bi, l0, nb) in enumerate(
                    evacsM if (sgate & 8 and sgate & 65536) else []):
                VE_(lambda e, bi=bi, l0=l0, nb=nb: e.tensor_tensor(
                    slot3(MtS_t, 1024, l0, nb), pbank2(bi, nb * 128),
                    mrep(triu, nb), OP.mult),
                   reads=[pB[bi]] + ([B["masks"]] if _k == 0 else []),
                   writes=[B["MtS"]] if _k == 0 else [])
            B["MtS"].wrote(P.tok("V"))
            for bi, l0, nb in evacsM:
                pB[bi].read(P.tok("V"))

            cellsO, evacsO = sect("A", 4)
            if (not first) and (sgate & 64):
                for u in range(4):
                    p = 4 * w + u
                    bi, col, st_, sp_ = cellsO[u]
                    cell = bank(bi)[:, col:col + 128]
                    W0s = W0b_t[:, u * 128:(u + 1) * 128]
                    QTc = QTg_t[:, p * GB + tc * 128: p * GB + tc * 128 + 128]
                    T_(lambda e, cell=cell, W0s=W0s, QTc=QTc, st_=st_:
                       e.matmul(cell, W0s, QTc, start=st_, stop=False,
                                skip_group_check=True),
                       reads=([B["W0b"]] if u == 0 else []),
                       writes=[pB[bi]] if st_ else [], mark=False)
            if w == 0:
                B["QTz"].read(P.tok("T"))
                B["KTz"].read(P.tok("T"))
                emit_s0(1)
            cells, evacs = sect("B", 4)
            if (not first) and (sgate & 512):
                for u in range(4):
                    p = 4 * w + u
                    bi, col, st_, sp_ = cells[u]
                    cell = bank(bi)[:, col:col + 128]
                    KKs = KKbd_t[:, u * 128:(u + 1) * 128]
                    C0s = C0bd_t[:, p * 128:(p + 1) * 128]
                    T_(lambda e, cell=cell, KKs=KKs, C0s=C0s, st_=st_:
                       e.matmul(cell, KKs, C0s, start=st_, stop=False,
                                skip_group_check=True),
                       reads=([B["KKbd"]] if u == 0 else []),
                       writes=[pB[bi]] if st_ else [], mark=False)
            for l in ([0, 2, 4, 6, 1, 3, 5, 7] if sgate & 64 else []):
                h = 8 * w + l
                q = l % 2
                u = l // 2
                psl = slice(64 * q, 64 * q + 64)
                bi, col, st_, sp_ = cellsO[u]
                cell = bank(bi)[psl, col:col + 128]
                vns = vn_t[:, h * 64: h * 64 + 64]
                MtSs = MtS_t[:, ts(l)]
                T_(lambda e, cell=cell, vns=vns, MtSs=MtSs, q=q, st_=st_,
                   sp_=sp_, first=first: e.matmul(
                       cell, vns, MtSs, start=(st_ if first else False),
                       stop=(sp_ and q == 1),
                       tile_position=(0, 64 * q), skip_group_check=True),
                   reads=([B["MtS"], B["vn"]] if l == 0 else []),
                   writes=[pB[bi]] if (first and st_ and q == 0) else [],
                   mark=(l == 7))
            ot_tok = P.tok("T")
            for bi, l0, nb in evacsO:
                pB[bi].wrote(ot_tok)
            for bi, l0, nb in evacsO:
                sap = psrc(bi, 128, nb, 128)
                dap = bass.AP(OTg_t, (4 * w + l0) * 512 + (tc % 4) * 128,
                              [[4096, 128], [3 * 512, nb], [1, 128]])
                SE_(lambda e, dap=dap, sap=sap: e.copy(dap, sap),
                   reads=[pB[bi]], writes=[B["OTg"]])
            for bi, l0, nb in evacsO:
                pB[bi].read(P.tok("S"))

            for u in (range(4) if sgate & 512 else []):
                p = 4 * w + u
                bi, col, st_, sp_ = cells[u]
                cell = bank(bi)[:, col:col + 128]
                kns = kn_t[:, p * 128:(p + 1) * 128]
                Zs = Zb_t[:, u * 128:(u + 1) * 128]
                T_(lambda e, cell=cell, kns=kns, Zs=Zs, st_=st_, sp_=sp_,
                   first=first: e.matmul(
                       cell, kns, Zs, start=(first and st_), stop=sp_,
                       skip_group_check=True),
                   reads=([B["Zb"], B["kn"]] if u == 0 else []),
                   writes=[pB[bi]] if (first and st_) else [],
                   mark=sp_)
            g_tok = P.tok("T")
            for nm in ("KKbd", "C0bd", "S0bd", "W0b", "Ybf", "MtS", "vn",
                       "Atr", "Atb", "Aqs", "Zb", "kn", "qn", "QTg", "KTg",
                       "QTz", "KTz"):
                B[nm].read(g_tok)
            for bi, l0, nb in evacs:
                pB[bi].wrote(g_tok)
            for _k, (bi, l0, nb) in enumerate(
                    evacs if (sgate & 512 and sgate & 524288) else []):
                VE_(lambda e, bi=bi, l0=l0, nb=nb: e.tensor_tensor(
                    slot3(updt, 512, l0, nb), pbank2(bi, nb * 128),
                    mrep(bdm, nb), OP.mult),
                   reads=[pB[bi]] + ([B["masks"]] if _k == 0 else []),
                   writes=[B["updt"]] if _k == 0 else [])
            B["updt"].wrote(P.tok("V"))
            for bi, l0, nb in evacs:
                pB[bi].read(P.tok("V"))
            vt = VE_(lambda e, w=w: e.tensor_tensor(
                Gp_t[:, 4 * w * 128: 4 * w * 128 + 512],
                Gp_t[:, 4 * w * 128: 4 * w * 128 + 512],
                updt[:], OP.add),
                reads=[B["updt"]], writes=[B["G"]])
            B["updt"].read(vt)

    def tap(nm, t_, buf):
        if debug and nm in dbg_d:
            dt = DMA("G", lambda e, nm=nm, t_=t_: e.dma_start(
                dbg_d[nm].ap(), t_[:]), reads=[buf])
            buf.read(dt)

    for g in range(NG):
        if g == 0:
            layernorm_group(0, xn_war_tok=P.tok("T"))
            tap("txn", xn, B["xn"])
        proj_group_T(g, B["wqs"], wqs_t, B["QTg"], QTg_t)
        proj_group_T(g, B["wks"], wks_t, B["KTg"], KTg_t)
        if g == 0:
            tap("tQT", QTg_t, B["QTg"])
            tap("tKT", KTg_t, B["KTg"])
        for ci in range(CPG):
            c = g * CPG + ci
            proj_t2n(c, QTg_t, B["QTg"], B["qn"], qn_t, "A")
            proj_t2n(c, KTg_t, B["KTg"], B["kn"], kn_t, "B")
            proj_chunk_nat(c, B["wvs"], wvs_t, B["vn"], vn_t)
            if ci == CPG - 1 and g + 1 < NG:
                ln_stats(g + 1)
            if c == 0:
                tap("tqn", qn_t, B["qn"])
                tap("tkn", kn_t, B["kn"])
                tap("tvn", vn_t, B["vn"])
            if c < nsc:
                scan_chunk(c)
            if ci == CPG - 1 and wo_on:
                for w2 in range(2):
                    for ji in range(4):
                        jd = 4 * w2 + ji
                        bi = ji
                        pt = bank(bi)
                        pb = pB[bi]
                        for jh in range(NJ):
                            lhs = wos_t[:, jh * D + jd * 128:
                                        jh * D + jd * 128 + 128]
                            rhs = OTg_t[:, jh * 512: jh * 512 + 512]
                            T_(lambda e, pt=pt, lhs=lhs, rhs=rhs, jh=jh:
                               e.matmul(pt, lhs, rhs,
                                        start=(jh == 0),
                                        stop=(jh == NJ - 1)),
                               reads=([B["wos"], B["OTg"]] if jh == 0
                                      else []),
                               writes=([pb] if jh == 0 else []),
                               mark=(jh == NJ - 1))
                        pb.wrote(P.tok("T"))
                        xs = xres[:, jd * T + g * GB:
                                  jd * T + g * GB + 512]
                        vt = V_(lambda e, xs=xs, pt=pt: e.tensor_tensor(
                            xs, xs, pt[:], OP.add),
                            reads=[pb], writes=[B["xres"]])
                        pb.read(vt)
                    if w2 == 0 and g + 1 < NG:
                        ln_apply(g + 1, xn_war_tok=P.tok("T"))
                B["OTg"].read(P.tok("T"))
            if c == 0:
                tap("tS1", Sp_t, B["S"])
                tap("tC1", Cp_t, B["C"])
                tap("tG1", Gp_t, B["G"])
            if c == 1:
                tap("tS2", Sp_t, B["S"])
                tap("tC2", Cp_t, B["C"])
                tap("tG2", Gp_t, B["G"])
        if g + 1 < NG and not wo_on:
            ln_apply(g + 1, xn_war_tok=P.tok("T"))

    if debug:
        dt = DMA("G", lambda e: e.dma_start(dbg_d["dbg2"].ap(), xres[:]),
                 reads=[B["xres"]])
        B["xres"].read(dt)

    P.barrier()
    ffn_sublayer(first_call=False)

    for j in range(NJ):
        q = ("Y", "G")[j % 2]
        dt = DMA(q, lambda e, j=j: e.dma_start(
            out_d.ap()[j * 128:(j + 1) * 128, :], xres[:, j * T:(j + 1) * T]),
            reads=[B["xres"]])
        B["xres"].read(dt)

    P.wait("Y", P.qdone("Y"))
    P.wait("Y", P.qdone("G"))
    P.emit("Y", lambda e: e.nop(), mark=False)

    with nc.Block() as blk:
        blk.tensor(lambda e: P.run("T", e))
        blk.vector(lambda e: P.run("V", e))
        blk.scalar(lambda e: P.run("S", e))
        blk.gpsimd(lambda e: P.run("G", e))
        blk.sync(lambda e: P.run("Y", e))

    stack.close()
    return nc


_NC_CACHE = {}


def _get_nc(debug=False):
    key = bool(debug)
    if key not in _NC_CACHE:
        _NC_CACHE[key] = build(debug=debug)
    return _NC_CACHE[key]


def _host_constants():
    triu1 = np.triu(np.ones((128, 128), np.float32)).astype(ml_dtypes.bfloat16)
    striu1 = np.triu(np.ones((128, 128), np.float32), 1).astype(
        ml_dtypes.bfloat16)
    q = np.arange(128) // 64
    bdm1 = (q[:, None] == q[None, :]).astype(np.float32).astype(
        ml_dtypes.bfloat16)
    pm1 = np.zeros((128, 256), np.float32)
    pm1[0:64, 0:128] = 1.0
    pm1[64:128, 128:256] = 1.0
    pm1 = pm1.astype(ml_dtypes.bfloat16)
    ident = np.eye(128, dtype=np.float32)
    return triu1, striu1, bdm1, pm1, ident


def _in_maps(L0_attn_x, L0_attn_W_q, L0_attn_W_k, L0_attn_W_v, L0_attn_W_o,
             L0_ffn_w1, L0_ffn_b1, L0_ffn_w2, L0_ffn_b2):
    triu1, striu1, bdm1, pm1, ident = _host_constants()
    common = {
        "wq": np.ascontiguousarray(L0_attn_W_q, np.float32),
        "wk": np.ascontiguousarray(L0_attn_W_k, np.float32),
        "wv": np.ascontiguousarray(L0_attn_W_v, np.float32),
        "wo": np.ascontiguousarray(L0_attn_W_o, np.float32),
        "w1": np.ascontiguousarray(L0_ffn_w1, np.float32),
        "b1": np.ascontiguousarray(L0_ffn_b1, np.float32),
        "w2": np.ascontiguousarray(L0_ffn_w2, np.float32),
        "b2": np.ascontiguousarray(L0_ffn_b2, np.float32),
        "triu1": triu1, "striu1": striu1, "bdm1": bdm1, "pm1": pm1,
        "ident": ident,
    }
    x = np.ascontiguousarray(L0_attn_x, np.float32)
    return [dict(common, x=np.ascontiguousarray(x[i].T)) for i in range(8)]


def kernel(L0_attn_x, L0_attn_W_q, L0_attn_W_k, L0_attn_W_v, L0_attn_W_o,
           L0_ffn_w1, L0_ffn_b1, L0_ffn_w2, L0_ffn_b2):
    nc = _get_nc(debug=False)
    in_maps = _in_maps(L0_attn_x, L0_attn_W_q, L0_attn_W_k, L0_attn_W_v,
                       L0_attn_W_o, L0_ffn_w1, L0_ffn_b1, L0_ffn_w2,
                       L0_ffn_b2)
    res = run_bass_kernel_spmd(nc, in_maps, core_ids=list(range(8)))
    out = np.stack([np.asarray(r["out"]).T for r in res.results], axis=0)
    return np.ascontiguousarray(out, np.float32)
